# version 06
# baseline (speedup 1.0000x reference)
import copy
import numpy as np
import ml_dtypes
import concourse.bass as bass
import concourse.mybir as mybir
from concourse.bass_utils import run_bass_kernel_spmd

F32 = mybir.dt.float32
BF16 = mybir.dt.bfloat16
AF = mybir.ActivationFunctionType
ALU = mybir.AluOpType
AX = mybir.AxisListType

T = 2048
DM = 2048
INTER = 4096
NH, HDIM, NG, SSZ = 64, 64, 8, 128
CONV_DIM = 6144
PROJ = 10304
DE = 8192
CHK = 128
NCHUNK = 16
QT = 512
NQ = 4
EPS_LN = 1e-5
EPS_RMS = 1e-6
EPS_Q = 1e-8
KS = 16
NTT = 16
NSLAB_HBC = 48
NSLAB_HS = 32


class P:

    ENGS = ("tensor", "vector", "scalar", "gpsimd", "sync")

    def __init__(self, nc, alloc_sem):
        self.nc = nc
        self.q = {k: [] for k in self.ENGS}
        self.cnt = {}
        self.sems = {}
        self.alloc_sem = alloc_sem
        self.slots = []

    def inc(self, sem, n=1):
        self.cnt[id(sem)] = self.cnt.get(id(sem), 0) + n
        return self.cnt[id(sem)]

    def val(self, sem):
        return self.cnt.get(id(sem), 0)

    def emit(self, eng, fn):
        self.q[eng].append(fn)

    def slot(self, name):
        s = DMASlot(self, self.alloc_sem(name))
        self.slots.append(s)
        return s

    def op(self, eng, fn):
        sem = self.sems["ms_" + eng]
        tgt = self.inc(sem, 1)
        self.emit(eng, lambda e, f=fn, s=sem: f(e).then_inc(s, 1))
        return (sem, tgt)

    def raw(self, eng, fn):
        self.emit(eng, fn)

    def wait(self, eng, tok):
        if tok is None:
            return
        sem, tgt = tok
        self.emit(eng, lambda e, s=sem, t=tgt: e.wait_ge(s, t))

    def fence(self, engs=None):
        snapshot = [(s, self.val(s)) for s in self.sems.values() if self.val(s) > 0]
        snapshot += [(sl.sem, sl.total) for sl in self.slots if sl.total > 0]
        for eng in engs or self.ENGS:
            for s, v in snapshot:
                self.emit(eng, lambda e, s=s, v=v: e.wait_ge(s, v))


class DMASlot:
    def __init__(self, p, sem):
        self.p = p
        self.sem = sem
        self.total = 0

    def load(self, eng, out, in_, **kw):
        p = self.p
        if self.total > 0:
            prev = self.total
            p.emit(eng, lambda e, s=self.sem, v=prev: e.wait_ge(s, v))
        self.total += 16
        tgt = self.total
        p.emit(eng, lambda e, o=out, i=in_, k=kw, s=self.sem: e.dma_start(
            out=o, in_=i, **k).then_inc(s, 16))
        return (self.sem, tgt)

    def tok(self):
        return (self.sem, self.total) if self.total else None


def dedup_ldweights(nc):
    for f in nc.m.functions:
        for bb in f.blocks:
            il = bb.instructions
            prev = None
            out_l = []
            changed = False
            for i in il:
                if i.__class__.__name__ == "InstMatmult":
                    key = str(i.ins[1])
                    if ("float32" not in key and prev == key and i.ldweights is None
                            and i.is_transpose is not True):
                        i = copy.replace(i, ldweights=False)
                        changed = True
                    prev = key
                else:
                    if i.__class__.__name__ in ("InstMatmult", "InstLdweights"):
                        prev = None
                out_l.append(i)
            if changed:
                bb.instructions = out_l


def make_consts():
    c = {}
    c["ident_b"] = np.eye(128, dtype=ml_dtypes.bfloat16)
    c["ident_f"] = np.eye(128, dtype=np.float32)
    c["cums_pos"] = np.triu(np.ones((128, 128), np.float32))
    c["rev_incl"] = np.tril(np.ones((128, 128), np.float32))
    c["rev_strict"] = np.tril(np.ones((128, 128), np.float32), -1)
    c["trilmaskT"] = np.triu(np.ones((128, 128), np.float32)).astype(ml_dtypes.bfloat16)
    c["onesrow"] = np.ones((1, 128), np.float32)
    c["onescol"] = np.ones((128, 1), np.float32)
    return c


class KB:

    def __init__(self, debug=(), p2_lvl=9, p2_aprep=5):
        self.p2_lvl = p2_lvl
        self.p2_aprep = p2_aprep
        self.debug = set(debug)
        self.nc = bass.Bass("TRN2", target_bir_lowering=False)
        self.dbg = {}
        self._ctx = []

    def _enter(self, cm):
        v = cm.__enter__()
        self._ctx.append(cm)
        return v

    def sb(self, name, shape, dt):
        return self._enter(self.nc.sbuf_tensor(name, list(shape), dt))

    def ps(self, name, shape, dt=F32):
        return self._enter(self.nc.psum_tensor(name, list(shape), dt))

    def semaphore(self, name):
        return self._enter(self.nc.semaphore(name))

    def dbg_out(self, name, shape, dt=F32):
        if name not in self.dbg:
            self.dbg[name] = self.nc.dram_tensor(name, list(shape), dt, kind="ExternalOutput").ap()
        return self.dbg[name]

    def build(self, phases=("p0", "p1", "p2", "p3")):
        nc = self.nc
        inp = lambda n, s, d=F32: nc.dram_tensor(n, list(s), d, kind="ExternalInput").ap()
        self.x_in = inp("x", (T, DM))
        self.wp = inp("wp", (DM, PROJ))
        self.convw = inp("convw", (128, NSLAB_HBC * 4))
        self.convb = inp("convb", (128, NSLAB_HBC))
        self.dtb128 = inp("dtb128", (128, NH))
        self.alog128 = inp("alog128", (128, NH))
        self.d128 = inp("d128", (128, NH))
        self.normw128 = inp("normw128", (128, INTER))
        self.wo = inp("wo", (INTER, DM))
        self.wu_in = inp("wu", (DE, DM))
        self.wg_in = inp("wg", (DE, DM))
        self.wd_in = inp("wd", (DM, DE))
        for n in ("ident_b", "trilmaskT"):
            setattr(self, n, inp(n, (128, 128), BF16))
        for n in ("ident_f", "cums_pos", "rev_incl", "rev_strict"):
            setattr(self, n, inp(n, (128, 128)))
        self.onesrow = inp("onesrow", (1, 128))
        self.onescol = inp("onescol", (128, 1))
        self.eps128 = inp("eps128", (128, 4))

        self.out = nc.dram_tensor("out", [T, DM], F32, kind="ExternalOutput").ap()

        dram = lambda n, s, d: nc.dram_tensor(n, list(s), d, kind="Internal").ap()
        self.q_upT = dram("q_upT", (DM, DE), BF16)
        self.q_gT = dram("q_gT", (DM, DE), BF16)
        self.q_dT = dram("q_dT", (DE, DM), BF16)
        self.wo_b = dram("wo_b", (INTER, DM), BF16)
        self.hbc_s = dram("hbc_s", (NSLAB_HBC, 128, T), BF16)
        self.sgate_s = dram("sgate_s", (NTT, 128, INTER), BF16)
        self.dt_s = dram("dt_s", (NTT, 128, NH), F32)
        self.ynT_s = dram("ynT_s", (NCHUNK, NSLAB_HS, 128, 128), BF16)
        self.aflat_s = dram("aflat_s", (NCHUNK, 1, NH * CHK), F32)
        self.x1_s = dram("x1_s", (T, DM), F32)

        sb = self.sb
        self.c_identb = sb("c_identb", (128, 128), BF16)
        self.c_identf = sb("c_identf", (128, 128), F32)
        self.c_cums = sb("c_cums", (128, 128), F32)
        self.c_revi = sb("c_revi", (128, 128), F32)
        self.c_revs = sb("c_revs", (128, 128), F32)
        self.c_maskT = sb("c_maskT", (128, 128), BF16)
        self.c_ones = sb("c_ones", (1, 128), F32)
        self.c_onesc = sb("c_onesc", (128, 1), F32)
        self.c_convw = sb("c_convw", (128, NSLAB_HBC * 4), F32)
        self.c_convb = sb("c_convb", (128, NSLAB_HBC), F32)
        self.c_dtb = sb("c_dtb", (128, NH), F32)
        self.c_alog = sb("c_alog", (128, NH), F32)
        self.c_expa = sb("c_expa", (128, NH), F32)
        self.c_d128 = sb("c_d128", (128, NH), F32)
        self.c_normw = sb("c_normw", (128, INTER), F32)
        self.c_scal = sb("c_scal", (128, 16), F32)
        self.c_eps = sb("c_eps", (128, 4), F32)
        self.bufA = sb("bufA", (128, 32768), BF16)
        self.bufB = sb("bufB", (128, 5120), F32)
        self.bufC = sb("bufC", (128, 16384), BF16)
        self.bufD = sb("bufD", (128, 7168), F32)
        self.bufE = sb("bufE", (128, 8192), BF16)
        self.bufS = sb("bufS", (128, 1024), F32)
        self.aflat = sb("aflat", (1, 4096), F32)
        self.pb = [self.ps(f"pb{i}", (128, 512)) for i in range(8)]
        self.pt = [self.pb[6][:, :].bitcast(BF16), self.pb[7][:, :].bitcast(BF16)]

        p = P(nc, self.semaphore)
        self.p = p
        for eng in P.ENGS:
            p.sems["ms_" + eng] = self.semaphore("sm_" + eng)

        with self.nc.Block() as blk:
            self._load_consts()
            if "p0" in phases:
                self.p0_quant()
            p.fence()
            if "p1" in phases:
                self.p1_ln1_inproj()
            p.fence()
            if "p2" in phases:
                self.p2_mamba()
            p.fence()
            if "p3" in phases:
                self.p3_ffn()
            p.fence()
            self._final_wait()

            for eng in P.ENGS:
                getattr(blk, eng)(self._runner(eng))

        for cm in reversed(self._ctx):
            cm.__exit__(None, None, None)
        dedup_ldweights(nc)
        return nc

    def _runner(self, eng):
        ops = self.p.q[eng]
        def run(e):
            for f in ops:
                f(e)
        return run

    def _final_wait(self):
        p = self.p
        for s in p.sems.values():
            v = p.val(s)
            if v > 0:
                p.raw("sync", lambda e, s=s, v=v: e.wait_ge(s, v))
        for sl in p.slots:
            if sl.total > 0:
                p.raw("sync", lambda e, s=sl.sem, v=sl.total: e.wait_ge(s, v))

    def _load_consts(self):
        p = self.p
        sl = p.slot("sl_const")
        pairs = [(self.ident_b, self.c_identb), (self.ident_f, self.c_identf),
                 (self.cums_pos, self.c_cums), (self.rev_incl, self.c_revi),
                 (self.rev_strict, self.c_revs), (self.trilmaskT, self.c_maskT),
                 (self.onesrow, self.c_ones), (self.onescol, self.c_onesc),
                 (self.convw, self.c_convw), (self.convb, self.c_convb),
                 (self.dtb128, self.c_dtb), (self.alog128, self.c_alog),
                 (self.d128, self.c_d128), (self.normw128, self.c_normw),
                 (self.eps128, self.c_eps)]
        tok = None
        for srcT, dst in pairs:
            tok = sl.load("sync", dst[:, :], srcT)
        p.wait("scalar", tok)
        self.t_expa = p.op("scalar", lambda e: e.activation(self.c_expa[:, :], self.c_alog[:, :], AF.Exp))

    def p0_quant(self):
        p = self.p
        stage = self.bufD
        red = self.bufS
        mats = [(self.wu_in, DE, DM, 0, self.q_upT), (self.wg_in, DE, DM, 1, self.q_gT),
                (self.wd_in, DM, DE, 2, self.q_dT)]
        st_sl = [p.slot(f"p0st{i}") for i in range(3)]

        toks = []
        nld = 0
        for wsrc, R, C, mi, _ in mats:
            npc = C // 2048
            for j in range(R // 128):
                for pc in range(npc):
                    idx = j * npc + pc
                    slot = nld % 3
                    ss = stage[:, slot * 2048: slot * 2048 + 2048]
                    if nld >= 3:
                        p.wait("sync", toks[-3])
                    tl = st_sl[slot].load("sync", ss, wsrc[j * 128:(j + 1) * 128,
                                                           pc * 2048:(pc + 1) * 2048])
                    p.wait("vector", tl)
                    col = red[:, mi * 300 + idx: mi * 300 + idx + 1]
                    tv = p.op("vector", lambda e, c=col, s=ss: e.tensor_reduce(
                        c, s, axis=AX.X, op=ALU.add, apply_absolute_value=True))
                    toks.append(tv)
                    nld += 1
        tf = None
        p.wait("vector", toks[-1])
        for _, R, C, mi, _ in mats:
            tf = p.op("vector", lambda e, mi=mi: e.tensor_reduce(
                red[:, 900 + mi: 900 + mi + 1], red[:, mi * 300: mi * 300 + 64],
                axis=AX.X, op=ALU.add))
        p.wait("tensor", tf)
        tm = None
        for mi in range(3):
            tm = p.op("tensor", lambda e, mi=mi: e.matmul(
                self.pb[0][0:1, mi:mi + 1], red[:, 900 + mi: 900 + mi + 1],
                self.c_onesc[:, :], start=True, stop=True))
        svec = self.bufS[0:1, 960:976]
        t0 = p.op("vector", lambda e: e.memset(svec[0:1, 0:16], 0.0))
        p.wait("scalar", t0)
        p.wait("scalar", tm)
        t1 = p.op("scalar", lambda e: e.activation(svec[0:1, 0:3], self.pb[0][0:1, 0:3],
                                                   AF.Copy, scale=1.0 / (DE * DM)))
        p.wait("scalar", t1)
        t2 = p.op("scalar", lambda e: e.activation(svec[0:1, 0:3], svec[0:1, 0:3],
                                                   AF.Identity, bias=self.c_eps[0:1, 0:1]))
        p.wait("scalar", t2)
        t3 = p.op("scalar", lambda e: e.activation(svec[0:1, 3:6], svec[0:1, 0:3],
                                                   AF.Copy, scale=0.5))
        p.wait("scalar", t3)
        t3b = p.op("scalar", lambda e: e.activation(svec[0:1, 10:13], svec[0:1, 0:3],
                                                    AF.Copy, scale=-0.5))
        p.wait("vector", t3b)
        t4 = p.op("vector", lambda e: e.tensor_mul(svec[0:1, 6:7], svec[0:1, 4:5], svec[0:1, 5:6]))
        p.wait("tensor", t4)
        p.wait("tensor", t3b)
        tm = p.op("tensor", lambda e: e.matmul(self.pb[1][:, 0:16], self.c_ones[:, :],
                                               svec[0:1, 0:16], start=True, stop=True))
        p.wait("scalar", tm)
        self.t_scal = p.op("scalar", lambda e: e.activation(self.c_scal[:, 0:16],
                                                            self.pb[1][:, 0:16], AF.Copy))

        p.fence()
        sgn = self.bufC[:, 0:2048]
        m01 = self.bufE[:, 0:2048]
        tern = self.bufA
        trn = self.bufC[:, 8192:16384]
        wr_sl = [p.slot(f"p0wr{i}") for i in range(4)]
        ev_toks = [None, None]
        wr_toks = {}
        last_tr = [None]
        prev_grp = [None]
        for wsrc, R, C, mi, qdst in mats:
            npc = C // 2048
            for g in range(R // 512):
                for pc in range(npc):
                    tt_toks = []
                    sg_mk = []
                    if prev_grp[0] is not None:
                        for tk in prev_grp[0]:
                            p.wait("sync", tk)
                    for jj in range(4):
                        j = g * 4 + jj
                        slot = jj % 3
                        ss = stage[:, slot * 2048: slot * 2048 + 2048]
                        if jj >= 3:
                            p.wait("sync", tt_toks[jj - 3])
                            p.wait("sync", sg_mk[jj - 3][0])
                            p.wait("sync", sg_mk[jj - 3][1])
                        tl = st_sl[slot].load("sync", ss, wsrc[j * 128:(j + 1) * 128,
                                                               pc * 2048:(pc + 1) * 2048])
                        p.wait("scalar", tl)
                        p.wait("scalar", self.t_scal)
                        if tt_toks:
                            p.wait("scalar", tt_toks[-1])
                        tsg = p.op("scalar", lambda e, s=ss, mi=mi: e.sign(
                            sgn[:, :], s, bias=self.c_scal[:, 10 + mi:11 + mi]))
                        p.wait("scalar", tsg)
                        tmk = p.op("scalar", lambda e, s=ss, mi=mi: e.sign(
                            m01[:, :], s, bias=self.c_scal[:, 3 + mi:4 + mi]))
                        p.wait("gpsimd", tsg)
                        p.wait("gpsimd", tmk)
                        p.wait("gpsimd", last_tr[0])
                        ttn = p.op("gpsimd", lambda e, d=tern[:, jj * 2048:(jj + 1) * 2048]:
                                   e.tensor_add(d, sgn[:, :], m01[:, :]))
                        tt_toks.append(ttn)
                        sg_mk.append((tsg, tmk))
                    prev_grp[0] = [tt_toks[-1], sg_mk[-1][0], sg_mk[-1][1]]
                    p.wait("tensor", tt_toks[-1])
                    for c in range(16):
                        bank = c % 2
                        if ev_toks[bank] is not None:
                            p.wait("tensor", ev_toks[bank])
                        ttr = None
                        for jj in range(4):
                            ttr = p.op("tensor", lambda e, o=self.pt[bank][:, jj * 128:(jj + 1) * 128],
                                       s=tern[:, jj * 2048 + c * 128: jj * 2048 + c * 128 + 128]:
                                       e.transpose(o, s, self.c_identb[:, :]))
                        last_tr[0] = ttr
                        eng = "vector" if bank == 0 else "scalar"
                        p.wait(eng, ttr)
                        if (mi, c) in wr_toks:
                            p.wait(eng, wr_toks.pop((mi, c)))
                        if eng == "vector":
                            tev = p.op("vector", lambda e, o=trn[:, c * 512:(c + 1) * 512], b=bank:
                                       e.tensor_copy(o, self.pt[b][:, 0:512]))
                        else:
                            tev = p.op("scalar", lambda e, o=trn[:, c * 512:(c + 1) * 512], b=bank:
                                       e.activation(o, self.pt[b][:, 0:512], AF.Copy))
                        ev_toks[bank] = tev
                        p.wait("scalar", tev)
                        wr_toks[(mi, c)] = wr_sl[c % 4].load(
                            "scalar", qdst[pc * 2048 + c * 128: pc * 2048 + (c + 1) * 128,
                                           g * 512:(g + 1) * 512],
                            trn[:, c * 512:(c + 1) * 512])

        p.fence()
        wo_c = p.slot("p0wo_c")
        wo_w = p.slot("p0wo_w")
        for cblk in range(4):
            dst = self.bufC[:, 0:8 * 2048].rearrange("p (e d) -> p e d", e=8)
            srcb = self.wo[cblk * 1024:(cblk + 1) * 1024, :].rearrange("(e p) d -> p e d", p=128)
            tl = wo_c.load("gpsimd", dst, srcb)
            p.wait("sync", tl)
            tw = wo_w.load("sync", self.wo_b[cblk * 1024:(cblk + 1) * 1024, :]
                           .rearrange("(e p) d -> p e d", p=128), dst)
            p.wait("gpsimd", tw)

    def p1_ln1_inproj(self):
        p = self.p
        xlnT = lambda k: self.bufA[:, k * 2048:(k + 1) * 2048]
        xst = [self.bufB[:, 0:2048], self.bufB[:, 2048:4096]]
        sq = self.bufD[:, 0:2048]
        st = self.bufS
        xln = self.bufE[:, 0:2048]
        x_sl = [p.slot("p1x0"), p.slot("p1x1")]

        xload = {}
        xln_done = {}
        def loadx(tt):
            if tt >= NTT:
                return
            if tt - 2 >= 0:
                p.wait("sync", xln_done[tt - 2])
            xload[tt] = x_sl[tt % 2].load("sync", xst[tt % 2],
                                          self.x_in[tt * 128:(tt + 1) * 128, :])
        loadx(0); loadx(1)
        ev_tok = [None, None]
        for tt in range(NTT):
            xs = xst[tt % 2]
            p.wait("vector", xload[tt])
            t_s1 = p.op("vector", lambda e, x=xs: e.reduce_sum(st[:, 0:1], x, axis=AX.X))
            p.wait("scalar", xload[tt])
            t_s2 = p.op("scalar", lambda e, x=xs: e.activation(sq, x, AF.Square,
                                                               accum_out=st[:, 1:2]))
            p.wait("scalar", t_s1)
            t_m = p.op("scalar", lambda e: e.activation(st[:, 2:3], st[:, 0:1],
                                                        AF.Copy, scale=1.0 / 2048.0))
            p.wait("scalar", t_m)
            t_m2 = p.op("scalar", lambda e: e.activation(st[:, 3:4], st[:, 2:3], AF.Square))
            p.wait("vector", t_s2)
            p.wait("vector", t_m2)
            t_var = p.op("vector", lambda e: e.tensor_scalar(
                st[:, 4:5], st[:, 1:2], 1.0 / 2048.0, None, ALU.mult))
            p.wait("vector", t_var)
            t_var = p.op("vector", lambda e: e.tensor_sub(st[:, 4:5], st[:, 4:5], st[:, 3:4]))
            p.wait("scalar", t_var)
            t_ln = p.op("scalar", lambda e: e.activation(st[:, 5:6], st[:, 4:5], AF.Ln,
                                                         bias=self.c_eps[:, 1:2]))
            p.wait("scalar", t_ln)
            t_rs = p.op("scalar", lambda e: e.activation(st[:, 6:7], st[:, 5:6], AF.Exp,
                                                         scale=-0.5))
            p.wait("vector", t_rs)
            t_xln = p.op("vector", lambda e, x=xs: e.tensor_scalar(
                xln, x, st[:, 2:3], st[:, 6:7], ALU.subtract, ALU.mult))
            for kb in range(4):
                bank = kb % 2
                if ev_tok[bank] is not None:
                    p.wait("tensor", ev_tok[bank])
                p.wait("tensor", t_xln)
                ttr = None
                for kk in range(4):
                    k = kb * 4 + kk
                    ttr = p.op("tensor", lambda e, o=self.pt[bank][:, kk * 128:(kk + 1) * 128],
                               s=xln[:, k * 128:(k + 1) * 128]:
                               e.transpose(o, s, self.c_identb[:, :]))
                eng = "vector" if bank == 0 else "scalar"
                p.wait(eng, ttr)
                dst = self.bufA[:, :].rearrange("p (k c) -> p k c", c=2048)[
                    :, kb * 4:(kb + 1) * 4, tt * 128:(tt + 1) * 128]
                if eng == "vector":
                    tev = p.op("vector", lambda e, d=dst, b=bank: e.tensor_copy(
                        d, self.pt[b][:, 0:512].rearrange("p (k c) -> p k c", c=128)))
                else:
                    tev = p.op("scalar", lambda e, d=dst, b=bank: e.activation(
                        d, self.pt[b][:, 0:512].rearrange("p (k c) -> p k c", c=128), AF.Copy))
                ev_tok[bank] = tev
            xln_done[tt] = t_xln
            loadx(tt + 2)
        p.fence()

        blk_slot = [self.bufC[:, 0:8192], self.bufC[:, 8192:16384]]
        blk_sl = [p.slot("p1w0"), p.slot("p1w1")]
        sg_stage = [self.bufE[:, 2048 + i * 512: 2048 + (i + 1) * 512] for i in range(4)]
        sg_sl = [p.slot(f"p1sg{i}") for i in range(4)]
        sg_wr = [None] * 4
        ps_ev = {}
        slot_busy = [None, None]

        def wblock(gp, cols, width=512):
            slot = blk_slot[gp % 2]
            if slot_busy[gp % 2] is not None:
                p.wait("gpsimd", slot_busy[gp % 2])
            dst = slot.rearrange("p (k c) -> p k c", k=16)[:, :, 0:width]
            srcb = self.wp[:, cols:cols + width].rearrange("(k p) c -> p k c", p=128)
            return blk_sl[gp % 2].load("gpsimd", dst, srcb), slot

        for gp in range(8):
            tok, slot = wblock(gp, gp * 512)
            for tt in range(NTT):
                bank = (gp * NTT + tt) % 4
                if ps_ev.get(bank) is not None:
                    p.wait("tensor", ps_ev[bank])
                p.wait("tensor", tok)
                tm = None
                for k in range(KS):
                    tm = p.op("tensor", lambda e, b=bank, k=k, tt=tt, slot=slot:
                              e.matmul(self.pb[b][:, :],
                                       xlnT(k)[:, tt * 128:(tt + 1) * 128],
                                       slot.rearrange("p (k c) -> p k c", k=16)[:, k, :],
                                       start=(k == 0), stop=(k == KS - 1)))
                sgs = sg_stage[bank]
                p.wait("scalar", tm)
                if sg_wr[bank] is not None:
                    p.wait("scalar", sg_wr[bank])
                tev = p.op("scalar", lambda e, d=sgs, b=bank:
                           e.activation(d, self.pb[b][:, :], AF.Silu))
                ps_ev[bank] = tev
                p.wait("scalar", tev)
                sg_wr[bank] = sg_sl[bank].load(
                    "scalar", self.sgate_s[tt][:, gp * 512:(gp + 1) * 512], sgs)
                slot_busy[gp % 2] = tm
        dtok, dslot = wblock(0, PROJ - 64, width=64)
        dt_stage = self.bufD[:, 2048:2112]
        dt_sl = p.slot("p1dt")
        last_wr = None
        for tt in range(NTT):
            p.wait("tensor", dtok)
            if ps_ev.get(4) is not None:
                p.wait("tensor", ps_ev[4])
            tm = None
            for k in range(KS):
                tm = p.op("tensor", lambda e, k=k, tt=tt:
                          e.matmul(self.pb[4][:, 0:64],
                                   xlnT(k)[:, tt * 128:(tt + 1) * 128],
                                   dslot.rearrange("p (k c) -> p k c", k=16)[:, k, 0:64],
                                   start=(k == 0), stop=(k == KS - 1)))
            p.wait("scalar", tm)
            if last_wr is not None:
                p.wait("scalar", last_wr)
            tev = p.op("scalar", lambda e: e.activation(dt_stage, self.pb[4][:, 0:64], AF.Copy))
            ps_ev[4] = tev
            p.wait("scalar", tev)
            last_wr = dt_sl.load("scalar", self.dt_s[tt], dt_stage)
            slot_busy[0] = tm

        p.fence(("tensor",))
        hb_stage = [self.bufE[:, 4096 + i * 512: 4096 + (i + 1) * 512] for i in range(4)]
        hb_sl = [p.slot(f"p1hb{i}") for i in range(4)]
        hb_wr = [None] * 4
        ps_ev = {}
        for cb in range(12):
            tok, slot = wblock(cb, INTER + cb * 512)
            for ct in range(4):
                for tp in range(4):
                    bank = (ct * 4 + tp) % 4
                    if ps_ev.get(bank) is not None:
                        p.wait("tensor", ps_ev[bank])
                    p.wait("tensor", tok)
                    tm = None
                    for k in range(KS):
                        tm = p.op("tensor", lambda e, b=bank, k=k, ct=ct, tp=tp, slot=slot:
                                  e.matmul(self.pb[b][:, :],
                                           slot.rearrange("p (k c) -> p k c", k=16)[:, k, ct * 128:(ct + 1) * 128],
                                           xlnT(k)[:, tp * 512:(tp + 1) * 512],
                                           start=(k == 0), stop=(k == KS - 1)))
                    eng = "vector" if bank % 2 == 0 else "scalar"
                    hbs = hb_stage[bank]
                    p.wait(eng, tm)
                    if hb_wr[bank] is not None:
                        p.wait(eng, hb_wr[bank])
                    if eng == "vector":
                        tev = p.op("vector", lambda e, d=hbs, b=bank: e.tensor_copy(d, self.pb[b][:, :]))
                    else:
                        tev = p.op("scalar", lambda e, d=hbs, b=bank: e.activation(d, self.pb[b][:, :], AF.Copy))
                    ps_ev[bank] = tev
                    wr_eng = "scalar" if eng == "scalar" else "sync"
                    p.wait(wr_eng, tev)
                    hb_wr[bank] = hb_sl[bank].load(
                        wr_eng, self.hbc_s[cb * 4 + ct][:, tp * 512:(tp + 1) * 512], hbs)
                    slot_busy[cb % 2] = tm

    def p2_mamba(self):
        p = self.p
        A = self.bufA
        C_ = self.bufC
        D_ = self.bufD
        E_ = self.bufE
        pb = self.pb
        hs_fm = lambda s: A[:, s * 512:(s + 1) * 512]
        hs_tm = A[:, 16384:20480]
        hs_dt = A[:, 20480:24576]
        mT = A[:, 24576:32768]
        yn_b = A[:, 24576:28672]
        conv_in = [C_[:, 0:520], C_[:, 520:1040]]
        B_fm = lambda g: C_[:, 2048 + g * 512: 2048 + (g + 1) * 512]
        Cc_fm = lambda g: C_[:, 6144 + g * 512: 6144 + (g + 1) * 512]
        sgate_c = C_[:, 10240:14336]
        ynT_st = C_[:, 14336:16384]
        B_tm = E_[:, 0:1024]
        Sst_b = E_[:, 1024:5120]
        SmT = E_[:, 5120:6144]
        Ereg = E_[:, 6144:7168]
        y = self.bufB[:, 0:4096]
        Sst = D_[:, 0:4096]
        dtv = {"raw": D_[:, 4096:4160], "u": D_[:, 4160:4224], "au": D_[:, 4224:4288],
               "e1": D_[:, 4288:4352], "l1": D_[:, 4352:4416], "mx": D_[:, 4416:4480],
               "sp": D_[:, 4480:4544], "abt": D_[:, 4544:4608], "acn": D_[:, 4608:4672],
               "eac": D_[:, 4672:4736], "dst": D_[:, 4736:4800], "eal": D_[:, 4800:4864],
               "dtD": D_[:, 4864:4928], "nacn": D_[:, 5056:5120]}
        acumT = D_[0:64, 4928:5056]
        yo_tmp = D_[:, 5120:5632]
        su_tmp = D_[:, 5632:6144]
        cva = D_[:, 6144:6656]
        cvb = D_[:, 6656:7168]
        ealr = self.bufS[0:1, 100:164]

        sl_hbc = [p.slot("p2hb0"), p.slot("p2hb1")]
        sl_dt = p.slot("p2dt")
        sl_sg = p.slot("p2sg")
        sl_af_w = p.slot("p2afw")
        sl_af_r = [p.slot(f"p2afr{i}") for i in range(4)]
        sl_ynw = [p.slot("p2yw0"), p.slot("p2yw1")]

        t0 = p.op("vector", lambda e: e.memset(Sst, 0.0))
        t0b = p.op("vector", lambda e: e.memset(Sst_b, 0.0))
        tok_state_b = t0b
        tok_state_mm = None
        conv_busy = {}
        ev_tok = [None, None]
        yn_wr = [None, None]
        prev_tYN = None
        prev_tYF = None
        prev_yd = None
        prev_chunk_tensor = None

        for q in range(NQ):
            q0 = q * QT
            for s in range(48):
                ci_sl = conv_in[s % 2]
                if conv_busy.get(("tap", s - 2)) is not None:
                    p.wait("sync", conv_busy[("tap", s - 2)])
                lo = max(q0 - 1, 0)
                hi = min(q0 + QT + 2, T)
                ncols = hi - lo
                off = 1 if q0 == 0 else 0
                tmemz = None
                if off > 0 or ncols + off < 515:
                    if conv_busy.get(("tap", s - 2)) is not None:
                        p.wait("gpsimd", conv_busy[("tap", s - 2)])
                    tmemz = p.op("gpsimd", lambda e, c=ci_sl: e.memset(c[:, 0:515], 0.0))
                    p.wait("sync", tmemz)
                tl = sl_hbc[s % 2].load("sync", ci_sl[:, off:off + ncols],
                                        self.hbc_s[s][:, lo:hi])
                p.wait("vector", tl)
                if tmemz is not None:
                    p.wait("vector", tmemz)
                if conv_busy.get(("scr", s - 1)) is not None:
                    p.wait("vector", conv_busy[("scr", s - 1)])
                if prev_chunk_tensor is not None and s < 2:
                    p.wait("vector", prev_chunk_tensor)
                    p.wait("gpsimd", prev_chunk_tensor)
                    p.wait("scalar", prev_chunk_tensor)
                tc0 = p.op("vector", lambda e, c=ci_sl, s=s: e.tensor_scalar(
                    cva, c[:, 0:512], self.c_convw[:, s * 4: s * 4 + 1], None, ALU.mult))
                p.wait("vector", tc0)
                tc1 = p.op("vector", lambda e, c=ci_sl, s=s: e.scalar_tensor_tensor(
                    cvb, c[:, 1:513], self.c_convw[:, s * 4 + 1: s * 4 + 2], cva,
                    ALU.mult, ALU.add))
                p.wait("vector", tc1)
                tc2 = p.op("vector", lambda e, c=ci_sl, s=s: e.scalar_tensor_tensor(
                    cva, c[:, 2:514], self.c_convw[:, s * 4 + 2: s * 4 + 3], cvb,
                    ALU.mult, ALU.add))
                p.wait("vector", tc2)
                tc3 = p.op("vector", lambda e, c=ci_sl, s=s: e.scalar_tensor_tensor(
                    cvb, c[:, 3:515], self.c_convw[:, s * 4 + 3: s * 4 + 4], cva,
                    ALU.mult, ALU.add))
                conv_busy[("tap", s)] = tc3
                if s < 32:
                    dst = hs_fm(s)
                elif s < 40:
                    dst = B_fm(s - 32)
                else:
                    dst = Cc_fm(s - 40)
                p.wait("scalar", tc3)
                tsl = p.op("scalar", lambda e, d=dst, s=s: e.activation(
                    d, cvb, AF.Silu, bias=self.c_convb[:, s:s + 1]))
                conv_busy[("scr", s)] = tsl
                conv_busy[("silu", s)] = tsl
            t_conv_all = conv_busy[("silu", 47)]
            p.wait("tensor", t_conv_all)
            if "p2b" in self.debug:
                dB = self.dbg_out("d_B", (NQ, 128, 4096), BF16)
                dC = self.dbg_out("d_C", (NQ, 128, 4096), BF16)
                self._dbgsl5 = getattr(self, "_dbgsl5", None) or p.slot("dbgb")
                self._dbgsl6 = getattr(self, "_dbgsl6", None) or p.slot("dbgc")
                p.wait("sync", t_conv_all)
                tb1 = self._dbgsl5.load("sync", dB[q], C_[:, 2048:6144])
                tb2 = self._dbgsl6.load("sync", dC[q], C_[:, 6144:10240])
                p.wait("vector", tb1)
                p.wait("vector", tb2)
                p.wait("scalar", tb1)
                p.wait("scalar", tb2)

            for cc in range(4):
                if self.p2_lvl < 2:
                    p.fence()
                    continue
                ci = q * 4 + cc
                csl = slice(cc * 128, (cc + 1) * 128)

                if prev_yd is not None:
                    p.wait("sync", prev_yd)
                tok_dt_l = sl_dt.load("sync", dtv["raw"], self.dt_s[ci])
                p.wait("vector", tok_dt_l)
                tU = p.op("vector", lambda e: e.tensor_add(dtv["u"], dtv["raw"], self.c_dtb[:, :]))
                p.wait("scalar", tU)
                tAU = p.op("scalar", lambda e: e.activation(dtv["au"], dtv["u"], AF.Abs))
                p.wait("scalar", tAU)
                tE1 = p.op("scalar", lambda e: e.activation(dtv["e1"], dtv["au"], AF.Exp, scale=-1.0))
                p.wait("scalar", tE1)
                tL1 = p.op("scalar", lambda e: e.activation(dtv["l1"], dtv["e1"], AF.Ln,
                                                            bias=self.c_eps[:, 3:4]))
                tMX = p.op("scalar", lambda e: e.activation(dtv["mx"], dtv["u"], AF.Relu))
                p.wait("vector", tL1)
                p.wait("vector", tMX)
                tSP = p.op("vector", lambda e: e.tensor_add(dtv["sp"], dtv["mx"], dtv["l1"]))
                p.wait("vector", tSP)
                p.wait("vector", self.t_expa)
                tABT = p.op("vector", lambda e: e.tensor_mul(dtv["abt"], dtv["sp"], self.c_expa[:, :]))
                if self.p2_aprep < 1:
                    p.fence()
                    prev_yd = prev_chunk_tensor = tABT
                    continue
                p.wait("tensor", tABT)
                tM1 = p.op("tensor", lambda e: e.matmul(pb[5][:, 0:64], self.c_cums[:, :],
                                                        dtv["abt"], start=True, stop=True))
                tM2 = p.op("tensor", lambda e: e.matmul(pb[3][:, 0:64], self.c_revs[:, :],
                                                        dtv["abt"], start=True, stop=True))
                tM3 = p.op("tensor", lambda e: e.matmul(pb[2][:, 0:64], self.c_revi[:, :],
                                                        dtv["abt"], start=True, stop=True))
                p.wait("scalar", tM1)
                tACN = p.op("scalar", lambda e: e.activation(dtv["acn"], pb[5][:, 0:64], AF.Copy))
                tNAC = p.op("scalar", lambda e: e.activation(dtv["nacn"], pb[5][:, 0:64],
                                                             AF.Copy, scale=-1.0))
                tEAC = p.op("scalar", lambda e: e.activation(dtv["eac"], pb[5][:, 0:64],
                                                             AF.Exp, scale=-1.0))
                p.wait("scalar", tM2)
                tDST = p.op("scalar", lambda e: e.activation(dtv["dst"], pb[3][:, 0:64],
                                                             AF.Exp, scale=-1.0))
                if self.p2_aprep < 2:
                    p.fence()
                    prev_yd = prev_chunk_tensor = tDST
                    continue
                p.wait("scalar", tM3)
                tEALR = p.op("scalar", lambda e: e.activation(ealr, pb[2][0:1, 0:64],
                                                              AF.Exp, scale=-1.0))
                p.wait("tensor", tEALR)
                p.wait("tensor", tDST)
                tM4 = p.op("tensor", lambda e: e.matmul(pb[3][:, 0:64], self.c_ones[:, :],
                                                        ealr, start=True, stop=True))
                p.wait("scalar", tM4)
                tEAL = p.op("scalar", lambda e: e.activation(dtv["eal"], pb[3][:, 0:64], AF.Copy))
                if self.p2_aprep < 3:
                    p.fence()
                    prev_yd = prev_chunk_tensor = tEAL
                    continue
                p.wait("tensor", tACN)
                tTRA = p.op("tensor", lambda e: e.transpose(pb[4][0:64, 0:128], dtv["acn"],
                                                            self.c_identf[:, :]))
                p.wait("scalar", tTRA)
                p.wait("scalar", sl_af_w.tok())
                tACT2 = p.op("scalar", lambda e: e.activation(acumT, pb[4][0:64, 0:128], AF.Copy))
                if self.p2_aprep < 4:
                    p.fence()
                    prev_yd = prev_chunk_tensor = tACT2
                    continue
                p.wait("scalar", tACT2)
                tAFW = sl_af_w.load("scalar", self.aflat_s[ci][0, :]
                                    .rearrange("(h t) -> h t", h=64), acumT)

                if self.p2_lvl < 3:
                    p.fence()
                    prev_yd = prev_chunk_tensor = tACT2
                    continue
                if prev_tYN is not None:
                    p.wait("tensor", prev_tYN)
                for grp in range(8):
                    bank = grp % 2
                    if ev_tok[bank] is not None:
                        p.wait("tensor", ev_tok[bank])
                    ttr = None
                    for kk in range(4):
                        ttr = p.op("tensor", lambda e, o=self.pt[bank][:, kk * 128:(kk + 1) * 128],
                                   src=hs_fm(grp * 4 + kk)[:, csl]:
                                   e.transpose(o, src, self.c_identb[:, :]))
                    eng = "vector" if bank == 0 else "scalar"
                    p.wait(eng, ttr)
                    if eng == "vector":
                        tev = p.op("vector", lambda e, g=grp, b=bank: e.tensor_copy(
                            hs_tm[:, g * 512:(g + 1) * 512], self.pt[b][:, 0:512]))
                    else:
                        tev = p.op("scalar", lambda e, g=grp, b=bank: e.activation(
                            hs_tm[:, g * 512:(g + 1) * 512], self.pt[b][:, 0:512], AF.Copy))
                    ev_tok[bank] = tev
                tok_hstm = ev_tok[:]
                for gb in range(2):
                    bank = gb
                    if ev_tok[bank] is not None:
                        p.wait("tensor", ev_tok[bank])
                    ttr = None
                    for kk in range(4):
                        ttr = p.op("tensor", lambda e, o=self.pt[bank][:, kk * 128:(kk + 1) * 128],
                                   src=B_fm(gb * 4 + kk)[:, csl]:
                                   e.transpose(o, src, self.c_identb[:, :]))
                    eng = "vector" if bank == 0 else "scalar"
                    p.wait(eng, ttr)
                    if eng == "vector":
                        tev = p.op("vector", lambda e, g=gb, b=bank: e.tensor_copy(
                            B_tm[:, g * 512:(g + 1) * 512], self.pt[b][:, 0:512]))
                    else:
                        tev = p.op("scalar", lambda e, g=gb, b=bank: e.activation(
                            B_tm[:, g * 512:(g + 1) * 512], self.pt[b][:, 0:512], AF.Copy))
                    ev_tok[bank] = tev
                tok_btm = ev_tok[:]

                if self.p2_lvl < 4:
                    p.fence()
                    prev_yd = prev_chunk_tensor = ev_tok[1]
                    continue
                tok_smt = []
                for half in range(2):
                    bank = half
                    if getattr(self, "_prev_yo", None) is not None:
                        p.wait("tensor", self._prev_yo[6 + half])
                    tm = None
                    for gg in range(4):
                        g = half * 4 + gg
                        tm = p.op("tensor", lambda e, b=bank, gg=gg, g=g, csl=csl: e.matmul(
                            pb[b][:, gg * 128:(gg + 1) * 128],
                            B_fm(g)[:, csl], Cc_fm(g)[:, csl], start=True, stop=True))
                    p.wait("vector", tm)
                    if getattr(self, "_dbg_s", None) is not None:
                        p.wait("vector", self._dbg_s)
                    tmk = p.op("vector", lambda e, b=bank, h=half: e.tensor_tensor(
                        SmT[:, h * 512:(h + 1) * 512].rearrange("p (g l) -> p g l", g=4),
                        pb[b][:, :].rearrange("p (g l) -> p g l", g=4),
                        self.c_maskT[:, :].unsqueeze(1).broadcast_to([128, 4, 128])
                            .transpose([0, 1, 2]),
                        ALU.mult))
                    tok_smt.append((tmk, tm))

                if self.p2_lvl < 5:
                    p.fence()
                    prev_yd = prev_chunk_tensor = tok_smt[1][0]
                    continue
                tok_mT = {}
                prev_ex = None
                p.wait("sync", tAFW)
                for r in range(16):
                    hbase = r * 4
                    afslot = r % 4
                    afr = self.aflat[0:1, afslot * 1024: afslot * 1024 + 512]
                    if r >= 4:
                        p.wait("sync", tok_mT[r - 4])
                    elif getattr(self, "_prev_mT", None) is not None:
                        p.wait("sync", self._prev_mT[12 + r])
                    tAF = sl_af_r[afslot].load(
                        "sync", afr, self.aflat_s[ci][0:1, hbase * 128:(hbase + 4) * 128])
                    bank = 2 + (r % 2)
                    p.wait("tensor", tAF)
                    if r >= 2:
                        p.wait("tensor", tok_mT[r - 2])
                    else:
                        p.wait("tensor", tDST)
                        p.wait("tensor", tEALR)
                        p.wait("tensor", tEAL)
                        if prev_yd is not None:
                            p.wait("tensor", prev_yd)
                    tBC = p.op("tensor", lambda e, b=bank, afr=afr: e.matmul(
                        pb[b][:, 0:512], self.c_ones[:, :], afr,
                        start=True, stop=True))
                    tEx = None
                    p.wait("scalar", tBC)
                    p.wait("scalar", tNAC)
                    if prev_ex is not None:
                        p.wait("scalar", prev_ex)
                        p.wait("scalar", tok_mT[r - 1])
                    if r < 2 and getattr(self, "_dbg_tok", None) is not None:
                        p.wait("vector", self._dbg_tok)
                    for hh in range(4):
                        h = hbase + hh
                        tEx = p.op("scalar", lambda e, b=bank, hh=hh, h=h: e.activation(
                            Ereg[:, hh * 128:(hh + 1) * 128],
                            pb[b][:, hh * 128:(hh + 1) * 128], AF.Relu, scale=1.0,
                            bias=dtv["nacn"][:, h:h + 1]))
                    p.wait("scalar", tEx)
                    tEx = p.op("scalar", lambda e: e.activation(
                        Ereg[:, 512:1024], Ereg[:, 0:512], AF.Exp, scale=-1.0))
                    prev_ex = tEx
                    p.wait("vector", tEx)
                    p.wait("vector", tok_smt[hbase // 32][0])
                    if prev_yd is not None and r < 2:
                        p.wait("vector", prev_yd)
                    g = hbase // 8
                    tMT = p.op("vector", lambda e, r=r, g=g: e.tensor_tensor(
                        mT[:, r * 512:(r + 1) * 512].rearrange("p (h l) -> p h l", h=4),
                        Ereg[:, 512:1024].rearrange("p (h l) -> p h l", h=4),
                        SmT[:, g * 128:(g + 1) * 128].unsqueeze(1).broadcast_to([128, 4, 128]),
                        ALU.mult))
                    tok_mT[r] = tMT
                if True:
                    self._prev_mT = tok_mT

                if self.p2_lvl < 6:
                    p.fence()
                    prev_yd = prev_chunk_tensor = tok_mT[15]
                    continue
                if "p2m" in self.debug:
                    dmT = self.dbg_out("d_mT", (NCHUNK, 128, 8192), BF16)
                    dsm = self.dbg_out("d_sm", (NCHUNK, 128, 1024), BF16)
                    self._dbgsl3 = getattr(self, "_dbgsl3", None) or p.slot("dbgm")
                    self._dbgsl4 = getattr(self, "_dbgsl4", None) or p.slot("dbgs")
                    p.wait("sync", tok_mT[15])
                    self._dbg_m = self._dbgsl3.load("sync", dmT[ci], mT)
                    self._dbg_s = self._dbgsl4.load("sync", dsm[ci], SmT)
                for t4 in tok_hstm:
                    p.wait("vector", t4)
                p.wait("vector", tSP)
                tHD = p.op("vector", lambda e: e.tensor_tensor(
                    hs_dt[:, :].rearrange("p (h d) -> p h d", h=64),
                    hs_tm[:, :].rearrange("p (h d) -> p h d", h=64),
                    dtv["sp"][:, :].unsqueeze(2).broadcast_to([128, 64, 64]),
                    ALU.mult))

                tok_yd = {}
                tok_yev = {}
                for g8 in range(8):
                    bank = 4 + (g8 % 2)
                    if g8 >= 2:
                        p.wait("tensor", tok_yev[g8 - 2])
                    else:
                        p.wait("tensor", tEAL if bank == 5 else tACT2)
                    p.wait("tensor", tHD)
                    tm = None
                    for hh in range(8):
                        h = g8 * 8 + hh
                        if hh % 4 == 0:
                            p.wait("tensor", tok_mT[h // 4])
                        tm = p.op("tensor", lambda e, b=bank, hh=hh, h=h: e.matmul(
                            pb[b][:, hh * 64:(hh + 1) * 64],
                            mT[:, h * 128:(h + 1) * 128],
                            hs_dt[:, h * 64:(h + 1) * 64],
                            start=True, stop=True))
                    tok_yd[g8] = tm
                    p.wait("scalar", tm)
                    if g8 < 2 and prev_tYN is not None:
                        p.wait("scalar", prev_tYN)
                    if g8 < 2 and getattr(self, "_dbg_tok_y", None) is not None:
                        p.wait("scalar", self._dbg_tok_y)
                    tev = p.op("scalar", lambda e, b=bank, g8=g8: e.activation(
                        y[:, g8 * 512:(g8 + 1) * 512], pb[b][:, :], AF.Copy))
                    tok_yev[g8] = tev

                if self.p2_lvl < 7:
                    p.fence()
                    prev_yd = prev_chunk_tensor = tok_yd[7]
                    prev_tYN = tok_yev[7]
                    continue
                tok_yo = {}
                last_yo_mm = None
                for g8 in range(8):
                    bank = g8 % 2
                    if g8 < 2:
                        p.wait("tensor", tok_smt[0][0])
                        p.wait("tensor", tok_smt[1][0])
                        p.wait("tensor", tok_state_b)
                    if g8 >= 2:
                        p.wait("tensor", tok_yo[g8 - 2])
                    tm = None
                    for hh in range(8):
                        h = g8 * 8 + hh
                        g = h // 8
                        tm = p.op("tensor", lambda e, b=bank, hh=hh, h=h, g=g, csl=csl: e.matmul(
                            pb[b][:, hh * 64:(hh + 1) * 64],
                            Cc_fm(g)[:, csl],
                            Sst_b[:, h * 64:(h + 1) * 64],
                            start=True, stop=True))
                    last_yo_mm = tm
                    p.wait("vector", tm)
                    p.wait("vector", tEAC)
                    p.wait("vector", tok_yev[g8])
                    if g8 >= 1:
                        p.wait("vector", tok_yo[g8 - 1])
                    tt1 = p.op("vector", lambda e, b=bank, g8=g8: e.tensor_tensor(
                        yo_tmp[:, :].rearrange("p (h d) -> p h d", h=8),
                        pb[b][:, :].rearrange("p (h d) -> p h d", h=8),
                        dtv["eac"][:, g8 * 8:(g8 + 1) * 8].unsqueeze(2).broadcast_to([128, 8, 64]),
                        ALU.mult))
                    p.wait("vector", tt1)
                    tt2 = p.op("vector", lambda e, g8=g8: e.tensor_add(
                        y[:, g8 * 512:(g8 + 1) * 512], y[:, g8 * 512:(g8 + 1) * 512],
                        yo_tmp[:, :]))
                    tok_yo[g8] = tt2
                    self._prev_yo = tok_yo

                if self.p2_lvl < 8:
                    p.fence()
                    prev_yd = prev_chunk_tensor = tok_yo[7]
                    prev_tYN = tok_yev[7]
                    continue
                p.wait("gpsimd", tok_yd[7])
                p.wait("gpsimd", tHD)
                for t4 in tok_hstm:
                    p.wait("gpsimd", t4)
                tDR = p.op("gpsimd", lambda e: e.tensor_tensor(
                    hs_dt[:, :].rearrange("p (h d) -> p h d", h=64),
                    hs_tm[:, :].rearrange("p (h d) -> p h d", h=64),
                    self.c_d128[:, :].unsqueeze(2).broadcast_to([128, 64, 64]),
                    ALU.mult))
                if prev_tYF is not None:
                    p.wait("sync", prev_tYF)
                tok_sg = sl_sg.load("sync", sgate_c, self.sgate_s[ci])
                p.wait("vector", tok_yo[7])
                p.wait("vector", tDR)
                tY2 = p.op("vector", lambda e: e.tensor_add(y, y, hs_dt))
                p.wait("vector", tok_sg)
                p.wait("vector", tY2)
                tYF = p.op("vector", lambda e: e.tensor_mul(y, y, sgate_c))
                prev_tYF = tYF
                if "p2y" in self.debug:
                    dy = self.dbg_out("d_y", (NCHUNK, 128, INTER))
                    p.wait("sync", tYF)
                    self._dbgsl = getattr(self, "_dbgsl", None) or p.slot("dbgy")
                    self._dbg_tok_y = self._dbgsl.load("sync", dy[ci], y)
                p.wait("scalar", tYF)
                p.wait("scalar", tok_yd[7])
                if getattr(self, "_dbg_m", None) is not None:
                    p.wait("scalar", self._dbg_m)
                tSQ = p.op("scalar", lambda e: e.activation(yn_b, y, AF.Square,
                                                            accum_out=self.bufS[:, 10:11]))
                p.wait("scalar", tSQ)
                tMS = p.op("scalar", lambda e: e.activation(self.bufS[:, 11:12],
                                                            self.bufS[:, 10:11],
                                                            AF.Copy, scale=1.0 / INTER))
                p.wait("scalar", tMS)
                tLN = p.op("scalar", lambda e: e.activation(self.bufS[:, 12:13],
                                                            self.bufS[:, 11:12],
                                                            AF.Ln, bias=self.c_eps[:, 2:3]))
                p.wait("scalar", tLN)
                tRS = p.op("scalar", lambda e: e.activation(self.bufS[:, 13:14],
                                                            self.bufS[:, 12:13],
                                                            AF.Exp, scale=-0.5))
                p.wait("vector", tRS)
                p.wait("vector", tSQ)
                tYN = p.op("vector", lambda e: e.scalar_tensor_tensor(
                    yn_b, y, self.bufS[:, 13:14], self.c_normw[:, :], ALU.mult, ALU.mult))
                prev_tYN = tYN
                if "p2y" in self.debug:
                    dyn = self.dbg_out("d_yn", (NCHUNK, 128, INTER), BF16)
                    p.wait("sync", tYN)
                    self._dbgsl2 = getattr(self, "_dbgsl2", None) or p.slot("dbgyn")
                    self._dbg_tok = self._dbgsl2.load("sync", dyn[ci], yn_b)

                for grp4 in range(4):
                    bank = grp4 % 2
                    if ev_tok[bank] is not None:
                        p.wait("tensor", ev_tok[bank])
                    p.wait("tensor", tYN)
                    ttr = None
                    for kk in range(8):
                        s = grp4 * 8 + kk
                        ttr = p.op("tensor", lambda e, o=self.pt[bank][:, kk * 128:(kk + 1) * 128],
                                   src=yn_b[:, s * 128:(s + 1) * 128]:
                                   e.transpose(o, src, self.c_identb[:, :]))
                    eng = "vector" if bank == 0 else "scalar"
                    p.wait(eng, ttr)
                    if yn_wr[grp4 % 2] is not None:
                        p.wait(eng, yn_wr[grp4 % 2])
                    if eng == "vector":
                        tev = p.op("vector", lambda e, g=grp4, b=bank: e.tensor_copy(
                            ynT_st[:, (g % 2) * 1024:((g % 2) + 1) * 1024], self.pt[b][:, 0:1024]))
                    else:
                        tev = p.op("scalar", lambda e, g=grp4, b=bank: e.activation(
                            ynT_st[:, (g % 2) * 1024:((g % 2) + 1) * 1024], self.pt[b][:, 0:1024],
                            AF.Copy))
                    ev_tok[bank] = tev
                    p.wait("sync", tev)
                    yn_wr[grp4 % 2] = sl_ynw[grp4 % 2].load(
                        "sync",
                        self.ynT_s[ci][grp4 * 8:(grp4 + 1) * 8].transpose([1, 0, 2]),
                        ynT_st[:, (grp4 % 2) * 1024:((grp4 % 2) + 1) * 1024]
                        .rearrange("p (s b) -> p s b", s=8))

                if self.p2_lvl < 9:
                    p.fence()
                    prev_yd = prev_chunk_tensor = tYN
                    continue
                p.wait("vector", tY2)
                p.wait("vector", tDST)
                tdD = p.op("vector", lambda e: e.tensor_mul(dtv["dtD"], dtv["sp"], dtv["dst"]))
                p.wait("vector", tdD)
                p.wait("vector", tok_yd[7])
                tHS2 = p.op("vector", lambda e: e.tensor_tensor(
                    hs_dt[:, :].rearrange("p (h d) -> p h d", h=64),
                    hs_tm[:, :].rearrange("p (h d) -> p h d", h=64),
                    dtv["dtD"][:, :].unsqueeze(2).broadcast_to([128, 64, 64]),
                    ALU.mult))
                tok_sup = {}
                for g8 in range(8):
                    bank = 2 + (g8 % 2)
                    if g8 >= 2:
                        p.wait("tensor", tok_sup[g8 - 2])
                    else:
                        p.wait("tensor", tok_mT[14 + g8])
                    p.wait("tensor", tHS2)
                    for t4 in tok_btm:
                        p.wait("tensor", t4)
                    tm = None
                    for hh in range(8):
                        h = g8 * 8 + hh
                        tm = p.op("tensor", lambda e, b=bank, hh=hh, h=h, g=g8: e.matmul(
                            pb[b][:, hh * 64:(hh + 1) * 64],
                            B_tm[:, g * 128:(g + 1) * 128],
                            hs_dt[:, h * 64:(h + 1) * 64],
                            start=True, stop=True))
                    p.wait("gpsimd", tEAL)
                    p.wait("gpsimd", t0)
                    if tok_sup.get(g8 - 1) is not None:
                        p.wait("gpsimd", tok_sup[g8 - 1])
                    tP = p.op("gpsimd", lambda e, g8=g8: e.tensor_tensor(
                        su_tmp[:, :].rearrange("p (h d) -> p h d", h=8),
                        Sst[:, g8 * 512:(g8 + 1) * 512].rearrange("p (h d) -> p h d", h=8),
                        dtv["eal"][:, g8 * 8:(g8 + 1) * 8].unsqueeze(2).broadcast_to([128, 8, 64]),
                        ALU.mult))
                    p.wait("vector", tP)
                    p.wait("vector", tm)
                    tS = p.op("vector", lambda e, b=bank, g8=g8: e.tensor_add(
                        Sst[:, g8 * 512:(g8 + 1) * 512], su_tmp[:, :], pb[b][:, :]))
                    tok_sup[g8] = tS
                p.wait("scalar", tok_sup[6])
                p.wait("scalar", tok_sup[7])
                p.wait("scalar", last_yo_mm)
                tok_state_b = p.op("scalar", lambda e: e.activation(Sst_b, Sst, AF.Copy))
                prev_yd = tok_sup[7]
                prev_chunk_tensor = tok_sup[7]

    def p3_ffn(self):
        p = self.p
        A = self.bufA
        C_ = self.bufC
        D_ = self.bufD
        E_ = self.bufE
        pb = self.pb
        h_q = lambda es: A[:, es * 512:(es + 1) * 512]
        wo_sl = lambda i: A[:, 4096 + i * 2048: 4096 + (i + 1) * 2048]
        ynT_q = lambda t4: C_[:, t4 * 4096:(t4 + 1) * 4096]
        wup_sl = lambda i: C_[:, i * 2048:(i + 1) * 2048]
        wg_sl = lambda i: C_[:, 4096 + i * 2048: 4096 + (i + 1) * 2048]
        wd_sl = lambda i: C_[:, 8192 + i * 2048: 8192 + (i + 1) * 2048]
        xln2T = lambda k: E_[:, k * 512:(k + 1) * 512]
        x1t = D_[:, 0:2048]
        x1r = [D_[:, 0:2048], D_[:, 2048:4096]]
        sq = D_[:, 4096:6144]
        st = self.bufS
        xl2 = self.bufB[:, 0:1024].bitcast(BF16)
        sup = self.bufB[:, 1024:1280].bitcast(BF16)
        outst = [D_[:, 6144 + i * 512: 6144 + (i + 1) * 512] for i in range(2)]

        sl_yn = [p.slot(f"p3yn{i}") for i in range(4)]
        sl_wo = [p.slot("p3wo0"), p.slot("p3wo1")]
        sl_x = p.slot("p3x")
        sl_x1w = p.slot("p3x1w")
        sl_x1r = [p.slot("p3xr0"), p.slot("p3xr1")]
        sl_wu = [p.slot("p3wu0"), p.slot("p3wu1")]
        sl_wg = [p.slot("p3wg0"), p.slot("p3wg1")]
        sl_wd = [p.slot("p3wd0"), p.slot("p3wd1")]
        sl_out = [p.slot("p3o0"), p.slot("p3o1")]
        self._wd_prev_mm = None
        self._dn_prev_evac = None

        for q in range(NQ):
            p.fence()
            tok_yn = {}
            for t4 in range(4):
                tok_yn[t4] = sl_yn[t4].load(
                    "sync", ynT_q(t4).rearrange("p (s b) -> p s b", s=32),
                    self.ynT_s[q * 4 + t4].transpose([1, 0, 2]))
            ev_tok = [None, None]
            prev_evac = None
            for pr in range(2):
                wo_tok = [None, None]
                last_mm = None
                if prev_evac is not None:
                    p.wait("tensor", prev_evac)
                for b2 in range(2):
                    if ev_tok[b2] is not None:
                        p.wait("tensor", ev_tok[b2])
                for isl in range(32):
                    slot = isl % 2
                    if isl >= 2:
                        p.wait("scalar", last_mm)
                    wo_tok[slot] = sl_wo[slot].load(
                        "scalar", wo_sl(slot), self.wo_b[isl * 128:(isl + 1) * 128, :])
                    p.wait("tensor", wo_tok[slot])
                    for ttl in range(2):
                        t4 = pr * 2 + ttl
                        p.wait("tensor", tok_yn[t4])
                        for dp in range(4):
                            bank = ttl * 4 + dp
                            last_mm = p.op("tensor", lambda e, b=bank, isl=isl, t4=t4,
                                           slot=slot, dp=dp: e.matmul(
                                pb[b][:, :],
                                ynT_q(t4)[:, isl * 128:(isl + 1) * 128],
                                wo_sl(slot)[:, dp * 512:(dp + 1) * 512],
                                start=(isl == 0), stop=(isl == 31)))
                tadds = [None, None]
                for ttl in range(2):
                    t4 = pr * 2 + ttl
                    tt_glob = q * 4 + t4
                    tx = sl_x1r[ttl].load("sync", x1r[ttl],
                                          self.x_in[tt_glob * 128:(tt_glob + 1) * 128, :])
                    p.wait("vector", tx)
                    p.wait("vector", last_mm)
                    tadd = None
                    for dp in range(4):
                        bank = ttl * 4 + dp
                        tadd = p.op("vector", lambda e, b=bank, dp=dp, ttl=ttl: e.tensor_add(
                            x1r[ttl][:, dp * 512:(dp + 1) * 512],
                            x1r[ttl][:, dp * 512:(dp + 1) * 512], pb[b][:, :]))
                    tadds[ttl] = tadd
                prev_evac = tadds[1]
                for ttl in range(2):
                    t4 = pr * 2 + ttl
                    tt_glob = q * 4 + t4
                    x1v = x1r[ttl]
                    p.wait("vector", tadds[ttl])
                    t_s1 = p.op("vector", lambda e, x1v=x1v: e.reduce_sum(st[:, 0:1], x1v, axis=AX.X))
                    p.wait("scalar", tadds[ttl])
                    t_s2 = p.op("scalar", lambda e, x1v=x1v: e.activation(sq, x1v, AF.Square,
                                                                 accum_out=st[:, 1:2]))
                    p.wait("scalar", t_s1)
                    t_m = p.op("scalar", lambda e: e.activation(st[:, 2:3], st[:, 0:1],
                                                                AF.Copy, scale=1.0 / 2048.0))
                    p.wait("scalar", t_m)
                    t_m2 = p.op("scalar", lambda e: e.activation(st[:, 3:4], st[:, 2:3], AF.Square))
                    p.wait("vector", t_s2)
                    p.wait("vector", t_m2)
                    t_v = p.op("vector", lambda e: e.tensor_scalar(
                        st[:, 4:5], st[:, 1:2], 1.0 / 2048.0, None, ALU.mult))
                    p.wait("vector", t_v)
                    t_v = p.op("vector", lambda e: e.tensor_sub(st[:, 4:5], st[:, 4:5], st[:, 3:4]))
                    p.wait("scalar", t_v)
                    t_ln = p.op("scalar", lambda e: e.activation(st[:, 5:6], st[:, 4:5], AF.Ln,
                                                                 bias=self.c_eps[:, 1:2]))
                    p.wait("scalar", t_ln)
                    t_rs = p.op("scalar", lambda e: e.activation(st[:, 6:7], st[:, 5:6], AF.Exp,
                                                                 scale=-0.5))
                    p.wait("vector", t_rs)
                    t_x2 = p.op("vector", lambda e, x1v=x1v: e.tensor_scalar(
                        xl2[:, 0:2048], x1v, st[:, 2:3], st[:, 6:7], ALU.subtract, ALU.mult))
                    p.wait("sync", t_x2)
                    p.wait("sync", t_s2)
                    tw = sl_x1w.load("sync", self.x1_s[tt_glob * 128:(tt_glob + 1) * 128, :], x1v)
                    for kb in range(4):
                        bank = kb % 2
                        if ev_tok[bank] is not None:
                            p.wait("tensor", ev_tok[bank])
                        p.wait("tensor", t_x2)
                        ttr = None
                        for kk in range(4):
                            k = kb * 4 + kk
                            ttr = p.op("tensor", lambda e, b=bank, kk=kk, k=k: e.transpose(
                                self.pt[b][:, kk * 128:(kk + 1) * 128],
                                xl2[:, k * 128:(k + 1) * 128], self.c_identb[:, :]))
                        eng = "vector" if bank == 0 else "scalar"
                        p.wait(eng, ttr)
                        dst = E_[:, :].rearrange("p (k c) -> p k c", c=512)[
                            :, kb * 4:(kb + 1) * 4, t4 * 128:(t4 + 1) * 128]
                        if eng == "vector":
                            tev = p.op("vector", lambda e, d=dst, b=bank: e.tensor_copy(
                                d, self.pt[b][:, 0:512].rearrange("p (k c) -> p k c", c=128)))
                        else:
                            tev = p.op("scalar", lambda e, d=dst, b=bank: e.activation(
                                d, self.pt[b][:, 0:512].rearrange("p (k c) -> p k c", c=128),
                                AF.Copy))
                        ev_tok[bank] = tev
                    p.wait("sync", tw)
            p.fence()
            h_tok = {}
            wu_tok = [None, None]
            wg_tok = [None, None]
            for es in range(64):
                slot = es % 2
                if es >= 2:
                    p.wait("scalar", h_tok[es - 2])
                    p.wait("scalar", h_tok[es - 1])
                wu_tok[slot] = sl_wu[slot].load(
                    "scalar", wup_sl(slot).rearrange("p (k c) -> p k c", k=16),
                    self.q_upT[:, es * 128:(es + 1) * 128].rearrange("(k p) c -> p k c", p=128))
                wg_tok[slot] = sl_wg[slot].load(
                    "scalar", wg_sl(slot).rearrange("p (k c) -> p k c", k=16),
                    self.q_gT[:, es * 128:(es + 1) * 128].rearrange("(k p) c -> p k c", p=128))
                bku, bkg = (0, 1) if es % 2 == 0 else (2, 3)
                p.wait("tensor", wu_tok[slot])
                p.wait("tensor", wg_tok[slot])
                if es >= 2:
                    p.wait("tensor", h_tok[es - 2])
                tmu = tmg = None
                for k in range(KS):
                    tmu = p.op("tensor", lambda e, b=bku, k=k, slot=slot: e.matmul(
                        pb[b][:, :],
                        wup_sl(slot).rearrange("p (k c) -> p k c", k=16)[:, k, :],
                        xln2T(k), start=(k == 0), stop=(k == KS - 1)))
                for k in range(KS):
                    tmg = p.op("tensor", lambda e, b=bkg, k=k, slot=slot: e.matmul(
                        pb[b][:, :],
                        wg_sl(slot).rearrange("p (k c) -> p k c", k=16)[:, k, :],
                        xln2T(k), start=(k == 0), stop=(k == KS - 1)))
                p.wait("scalar", tmu)
                p.wait("scalar", self.t_scal)
                if es >= 1:
                    p.wait("scalar", h_tok[es - 1])
                tsu = p.op("scalar", lambda e, b=bku: e.activation(
                    sup[:, 0:512], pb[b][:, :], AF.Silu, scale=self.c_scal[:, 3:4]))
                p.wait("vector", tsu)
                p.wait("vector", tmg)
                th = p.op("vector", lambda e, b=bkg, es=es: e.tensor_mul(
                    h_q(es), sup[:, 0:512], pb[b][:, :]))
                h_tok[es] = th
            p.fence(("tensor", "gpsimd", "vector", "scalar", "sync"))
            for pr in range(2):
                if self._dn_prev_evac is not None:
                    p.wait("tensor", self._dn_prev_evac)
                x1_tok = [None, None]
                for ttl in range(2):
                    t4 = pr * 2 + ttl
                    tt_glob = q * 4 + t4
                    x1_tok[ttl] = sl_x1r[ttl].load(
                        "sync", x1r[ttl], self.x1_s[tt_glob * 128:(tt_glob + 1) * 128, :])
                wd_tok = [None, None]
                last_mm = self._wd_prev_mm
                for es in range(64):
                    slot = es % 2
                    if last_mm is not None:
                        p.wait("scalar", last_mm)
                    wd_tok[slot] = sl_wd[slot].load(
                        "scalar", wd_sl(slot), self.q_dT[es * 128:(es + 1) * 128, :])
                    p.wait("tensor", wd_tok[slot])
                    for ttl in range(2):
                        t4 = pr * 2 + ttl
                        for dp in range(4):
                            bank = ttl * 4 + dp
                            last_mm = p.op("tensor", lambda e, b=bank, es=es, t4=t4,
                                           slot=slot, dp=dp: e.matmul(
                                pb[b][:, :],
                                h_q(es)[:, t4 * 128:(t4 + 1) * 128],
                                wd_sl(slot)[:, dp * 512:(dp + 1) * 512],
                                start=(es == 0), stop=(es == 63)))
                self._wd_prev_mm = last_mm
                for ttl in range(2):
                    t4 = pr * 2 + ttl
                    tt_glob = q * 4 + t4
                    for dp in range(4):
                        bank = ttl * 4 + dp
                        sslot = dp % 2
                        p.wait("vector", last_mm)
                        p.wait("vector", x1_tok[ttl])
                        if sl_out[sslot].tok() is not None:
                            p.wait("vector", sl_out[sslot].tok())
                        tev = p.op("vector", lambda e, b=bank, ttl=ttl, dp=dp, ss=sslot:
                                   e.scalar_tensor_tensor(
                            outst[ss], pb[b][:, :], self.c_scal[:, 6:7],
                            x1r[ttl][:, dp * 512:(dp + 1) * 512], ALU.mult, ALU.add))
                        p.wait("sync", tev)
                        sl_out[sslot].load(
                            "sync", self.out[tt_glob * 128:(tt_glob + 1) * 128,
                                             dp * 512:(dp + 1) * 512], outst[sslot])
                        self._dn_prev_evac = tev


def prep_base(L0_attn_in_proj_w, L0_attn_conv_w, L0_attn_conv_b, L0_attn_dt_bias,
              L0_attn_A_log, L0_attn_D, L0_attn_norm_w, L0_attn_out_proj_w,
              L0_ffn_W_up, L0_ffn_W_gate, L0_ffn_W_down):
    rep = lambda v: np.ascontiguousarray(
        np.repeat(np.asarray(v, np.float32)[None, :], 128, 0))
    convw = np.ascontiguousarray(
        np.asarray(L0_attn_conv_w, np.float32)[:, 0, :].reshape(4, NSLAB_HBC, 128)
        .transpose(2, 1, 0).reshape(128, NSLAB_HBC * 4))
    convb = np.ascontiguousarray(
        np.asarray(L0_attn_conv_b, np.float32).reshape(NSLAB_HBC, 128).T)
    base = {
        "wp": np.asarray(L0_attn_in_proj_w, np.float32),
        "convw": convw, "convb": convb,
        "dtb128": rep(L0_attn_dt_bias), "alog128": rep(L0_attn_A_log),
        "d128": rep(L0_attn_D), "normw128": rep(L0_attn_norm_w),
        "wo": np.asarray(L0_attn_out_proj_w, np.float32),
        "wu": np.asarray(L0_ffn_W_up, np.float32),
        "wg": np.asarray(L0_ffn_W_gate, np.float32),
        "wd": np.asarray(L0_ffn_W_down, np.float32),
    }
    base["eps128"] = np.ascontiguousarray(np.repeat(
        np.array([[EPS_Q, EPS_LN, EPS_RMS, 1.0]], np.float32), 128, 0))
    base.update(make_consts())
    return base


def kernel(L0_attn_input_states, L0_attn_in_proj_w, L0_attn_conv_w,
           L0_attn_conv_b, L0_attn_dt_bias, L0_attn_A_log, L0_attn_D,
           L0_attn_norm_w, L0_attn_out_proj_w, L0_ffn_W_up, L0_ffn_W_gate,
           L0_ffn_W_down):
    kb = KB()
    nc = kb.build()
    base = prep_base(L0_attn_in_proj_w, L0_attn_conv_w, L0_attn_conv_b,
                     L0_attn_dt_bias, L0_attn_A_log, L0_attn_D, L0_attn_norm_w,
                     L0_attn_out_proj_w, L0_ffn_W_up, L0_ffn_W_gate, L0_ffn_W_down)
    x = np.asarray(L0_attn_input_states, np.float32)
    in_maps = [dict(base, x=np.ascontiguousarray(x[c])) for c in range(8)]
    res = run_bass_kernel_spmd(nc, in_maps, core_ids=list(range(8)))
    return np.stack([r["out"] for r in res.results]).astype(np.float32)


# version 07 (final)
# speedup vs baseline: 1.3363x; 1.3363x over previous
import copy
import numpy as np
import ml_dtypes
import concourse.bass as bass
import concourse.mybir as mybir
from concourse.bass_utils import run_bass_kernel_spmd

F32 = mybir.dt.float32
BF16 = mybir.dt.bfloat16
AF = mybir.ActivationFunctionType
ALU = mybir.AluOpType
AX = mybir.AxisListType

T = 2048
DM = 2048
INTER = 4096
NH, HDIM, NG, SSZ = 64, 64, 8, 128
CONV_DIM = 6144
PROJ = 10304
DE = 8192
CHK = 128
NCHUNK = 16
QT = 512
NQ = 4
EPS_LN = 1e-5
EPS_RMS = 1e-6
EPS_Q = 1e-8
KS = 16
NTT = 16
NSLAB_HBC = 48
NSLAB_HS = 32


class P:

    ENGS = ("tensor", "vector", "scalar", "gpsimd", "sync")

    def __init__(self, nc, alloc_sem):
        self.nc = nc
        self.q = {k: [] for k in self.ENGS}
        self.cnt = {}
        self.sems = {}
        self.alloc_sem = alloc_sem
        self.slots = []

    def inc(self, sem, n=1):
        self.cnt[id(sem)] = self.cnt.get(id(sem), 0) + n
        return self.cnt[id(sem)]

    def val(self, sem):
        return self.cnt.get(id(sem), 0)

    def emit(self, eng, fn):
        self.q[eng].append(fn)

    def slot(self, name):
        s = DMASlot(self, self.alloc_sem(name))
        self.slots.append(s)
        return s

    def op(self, eng, fn):
        sem = self.sems["ms_" + eng]
        tgt = self.inc(sem, 1)
        self.emit(eng, lambda e, f=fn, s=sem: f(e).then_inc(s, 1))
        return (sem, tgt)

    def raw(self, eng, fn):
        self.emit(eng, fn)

    def wait(self, eng, tok):
        if tok is None:
            return
        sem, tgt = tok
        self.emit(eng, lambda e, s=sem, t=tgt: e.wait_ge(s, t))

    def fence(self, engs=None):
        snapshot = [(s, self.val(s)) for s in self.sems.values() if self.val(s) > 0]
        snapshot += [(sl.sem, sl.total) for sl in self.slots if sl.total > 0]
        for eng in engs or self.ENGS:
            for s, v in snapshot:
                self.emit(eng, lambda e, s=s, v=v: e.wait_ge(s, v))


class DMASlot:
    def __init__(self, p, sem):
        self.p = p
        self.sem = sem
        self.total = 0

    def load(self, eng, out, in_, **kw):
        p = self.p
        if self.total > 0:
            prev = self.total
            p.emit(eng, lambda e, s=self.sem, v=prev: e.wait_ge(s, v))
        self.total += 16
        tgt = self.total
        p.emit(eng, lambda e, o=out, i=in_, k=kw, s=self.sem: e.dma_start(
            out=o, in_=i, **k).then_inc(s, 16))
        return (self.sem, tgt)

    def tok(self):
        return (self.sem, self.total) if self.total else None


def dedup_ldweights(nc):
    for f in nc.m.functions:
        for bb in f.blocks:
            il = bb.instructions
            prev = None
            out_l = []
            changed = False
            for i in il:
                if i.__class__.__name__ == "InstMatmult":
                    key = str(i.ins[1])
                    if ("float32" not in key and prev == key and i.ldweights is None
                            and i.is_transpose is not True):
                        i = copy.replace(i, ldweights=False)
                        changed = True
                    prev = key
                else:
                    if i.__class__.__name__ in ("InstMatmult", "InstLdweights"):
                        prev = None
                out_l.append(i)
            if changed:
                bb.instructions = out_l


def make_consts():
    c = {}
    c["ident_b"] = np.eye(128, dtype=ml_dtypes.bfloat16)
    c["ident_f"] = np.eye(128, dtype=np.float32)
    c["cums_pos"] = np.triu(np.ones((128, 128), np.float32))
    c["rev_incl"] = np.tril(np.ones((128, 128), np.float32))
    c["rev_strict"] = np.tril(np.ones((128, 128), np.float32), -1)
    c["trilmaskT"] = np.triu(np.ones((128, 128), np.float32)).astype(ml_dtypes.bfloat16)
    c["onesrow"] = np.ones((1, 128), np.float32)
    c["onescol"] = np.ones((128, 1), np.float32)
    return c


class KB:

    def __init__(self, debug=(), p2_lvl=9, p2_aprep=5):
        self.p2_lvl = p2_lvl
        self.p2_aprep = p2_aprep
        self.debug = set(debug)
        self.nc = bass.Bass("TRN2", target_bir_lowering=False)
        self.dbg = {}
        self._ctx = []

    def _enter(self, cm):
        v = cm.__enter__()
        self._ctx.append(cm)
        return v

    def sb(self, name, shape, dt):
        return self._enter(self.nc.sbuf_tensor(name, list(shape), dt))

    def ps(self, name, shape, dt=F32):
        return self._enter(self.nc.psum_tensor(name, list(shape), dt))

    def semaphore(self, name):
        return self._enter(self.nc.semaphore(name))

    def dbg_out(self, name, shape, dt=F32):
        if name not in self.dbg:
            self.dbg[name] = self.nc.dram_tensor(name, list(shape), dt, kind="ExternalOutput").ap()
        return self.dbg[name]

    def build(self, phases=("p0", "p1", "p2", "p3")):
        nc = self.nc
        inp = lambda n, s, d=F32: nc.dram_tensor(n, list(s), d, kind="ExternalInput").ap()
        self.x_in = inp("x", (T, DM))
        self.wp = inp("wp", (DM, PROJ))
        self.convw = inp("convw", (128, NSLAB_HBC * 4))
        self.convb = inp("convb", (128, NSLAB_HBC))
        self.dtb128 = inp("dtb128", (128, NH))
        self.alog128 = inp("alog128", (128, NH))
        self.d128 = inp("d128", (128, NH))
        self.normw128 = inp("normw128", (128, INTER))
        self.wo = inp("wo", (INTER, DM))
        self.wu_in = inp("wu", (DE, DM))
        self.wg_in = inp("wg", (DE, DM))
        self.wd_in = inp("wd", (DM, DE))
        for n in ("ident_b", "trilmaskT"):
            setattr(self, n, inp(n, (128, 128), BF16))
        for n in ("ident_f", "cums_pos", "rev_incl", "rev_strict"):
            setattr(self, n, inp(n, (128, 128)))
        self.onesrow = inp("onesrow", (1, 128))
        self.onescol = inp("onescol", (128, 1))
        self.eps128 = inp("eps128", (128, 4))

        self.out = nc.dram_tensor("out", [T, DM], F32, kind="ExternalOutput").ap()

        dram = lambda n, s, d: nc.dram_tensor(n, list(s), d, kind="Internal").ap()
        self.q_upT = dram("q_upT", (DM, DE), BF16)
        self.q_gT = dram("q_gT", (DM, DE), BF16)
        self.q_dT = dram("q_dT", (DE, DM), BF16)
        self.wo_b = dram("wo_b", (INTER, DM), BF16)
        self.hbc_s = dram("hbc_s", (NSLAB_HBC, 128, T), BF16)
        self.sgate_s = dram("sgate_s", (NTT, 128, INTER), BF16)
        self.dt_s = dram("dt_s", (NTT, 128, NH), F32)
        self.ynT_s = dram("ynT_s", (NCHUNK, NSLAB_HS, 128, 128), BF16)
        self.aflat_s = dram("aflat_s", (NCHUNK, 1, NH * CHK), F32)
        self.x1_s = dram("x1_s", (T, DM), F32)

        sb = self.sb
        self.c_identb = sb("c_identb", (128, 128), BF16)
        self.c_identf = sb("c_identf", (128, 128), F32)
        self.c_cums = sb("c_cums", (128, 128), F32)
        self.c_revi = sb("c_revi", (128, 128), F32)
        self.c_revs = sb("c_revs", (128, 128), F32)
        self.c_maskT = sb("c_maskT", (128, 128), BF16)
        self.c_ones = sb("c_ones", (1, 128), F32)
        self.c_onesc = sb("c_onesc", (128, 1), F32)
        self.c_convw = sb("c_convw", (128, NSLAB_HBC * 4), F32)
        self.c_convb = sb("c_convb", (128, NSLAB_HBC), F32)
        self.c_dtb = sb("c_dtb", (128, NH), F32)
        self.c_alog = sb("c_alog", (128, NH), F32)
        self.c_expa = sb("c_expa", (128, NH), F32)
        self.c_d128 = sb("c_d128", (128, NH), F32)
        self.c_normw = sb("c_normw", (128, INTER), F32)
        self.c_scal = sb("c_scal", (128, 16), F32)
        self.c_eps = sb("c_eps", (128, 4), F32)
        self.bufA = sb("bufA", (128, 32768), BF16)
        self.bufB = sb("bufB", (128, 5120), F32)
        self.bufC = sb("bufC", (128, 16384), BF16)
        self.bufD = sb("bufD", (128, 7168), F32)
        self.bufE = sb("bufE", (128, 8192), BF16)
        self.bufS = sb("bufS", (128, 1024), F32)
        self.aflat = sb("aflat", (1, 4096), F32)
        self.pb = [self.ps(f"pb{i}", (128, 512)) for i in range(8)]
        self.pt = [self.pb[6][:, :].bitcast(BF16), self.pb[7][:, :].bitcast(BF16)]

        p = P(nc, self.semaphore)
        self.p = p
        for eng in P.ENGS:
            p.sems["ms_" + eng] = self.semaphore("sm_" + eng)

        with self.nc.Block() as blk:
            self._load_consts()
            if "p0" in phases:
                self.p0_quant()
            p.fence()
            if "p1" in phases:
                self.p1_ln1_inproj()
            p.fence()
            if "p2" in phases:
                self.p2_mamba()
            p.fence()
            if "p3" in phases:
                self.p3_ffn()
            p.fence()
            self._final_wait()

            for eng in P.ENGS:
                getattr(blk, eng)(self._runner(eng))

        for cm in reversed(self._ctx):
            cm.__exit__(None, None, None)
        dedup_ldweights(nc)
        return nc

    def _runner(self, eng):
        ops = self.p.q[eng]
        def run(e):
            for f in ops:
                f(e)
        return run

    def _final_wait(self):
        p = self.p
        for s in p.sems.values():
            v = p.val(s)
            if v > 0:
                p.raw("sync", lambda e, s=s, v=v: e.wait_ge(s, v))
        for sl in p.slots:
            if sl.total > 0:
                p.raw("sync", lambda e, s=sl.sem, v=sl.total: e.wait_ge(s, v))

    def _load_consts(self):
        p = self.p
        sl = p.slot("sl_const")
        pairs = [(self.ident_b, self.c_identb), (self.ident_f, self.c_identf),
                 (self.cums_pos, self.c_cums), (self.rev_incl, self.c_revi),
                 (self.rev_strict, self.c_revs), (self.trilmaskT, self.c_maskT),
                 (self.onesrow, self.c_ones), (self.onescol, self.c_onesc),
                 (self.convw, self.c_convw), (self.convb, self.c_convb),
                 (self.dtb128, self.c_dtb), (self.alog128, self.c_alog),
                 (self.d128, self.c_d128), (self.normw128, self.c_normw),
                 (self.eps128, self.c_eps)]
        tok = None
        for srcT, dst in pairs:
            tok = sl.load("sync", dst[:, :], srcT)
        p.wait("scalar", tok)
        self.t_expa = p.op("scalar", lambda e: e.activation(self.c_expa[:, :], self.c_alog[:, :], AF.Exp))

    def p0_quant(self):
        p = self.p
        stage = self.bufD
        red = self.bufS
        mats = [(self.wu_in, DE, DM, 0, self.q_upT), (self.wg_in, DE, DM, 1, self.q_gT),
                (self.wd_in, DM, DE, 2, self.q_dT)]
        st_sl = [p.slot(f"p0st{i}") for i in range(3)]

        toks = []
        nld = 0
        for wsrc, R, C, mi, _ in mats:
            npc = C // 2048
            for j in range(R // 128):
                for pc in range(npc):
                    idx = j * npc + pc
                    slot = nld % 3
                    ss = stage[:, slot * 2048: slot * 2048 + 2048]
                    if nld >= 3:
                        p.wait("sync", toks[-3])
                    tl = st_sl[slot].load("sync", ss, wsrc[j * 128:(j + 1) * 128,
                                                           pc * 2048:(pc + 1) * 2048])
                    p.wait("vector", tl)
                    col = red[:, mi * 300 + idx: mi * 300 + idx + 1]
                    tv = p.op("vector", lambda e, c=col, s=ss: e.tensor_reduce(
                        c, s, axis=AX.X, op=ALU.add, apply_absolute_value=True))
                    toks.append(tv)
                    nld += 1
        tf = None
        p.wait("vector", toks[-1])
        for _, R, C, mi, _ in mats:
            tf = p.op("vector", lambda e, mi=mi: e.tensor_reduce(
                red[:, 900 + mi: 900 + mi + 1], red[:, mi * 300: mi * 300 + 64],
                axis=AX.X, op=ALU.add))
        p.wait("tensor", tf)
        tm = None
        for mi in range(3):
            tm = p.op("tensor", lambda e, mi=mi: e.matmul(
                self.pb[0][0:1, mi:mi + 1], red[:, 900 + mi: 900 + mi + 1],
                self.c_onesc[:, :], start=True, stop=True))
        svec = self.bufS[0:1, 960:976]
        t0 = p.op("vector", lambda e: e.memset(svec[0:1, 0:16], 0.0))
        p.wait("scalar", t0)
        p.wait("scalar", tm)
        t1 = p.op("scalar", lambda e: e.activation(svec[0:1, 0:3], self.pb[0][0:1, 0:3],
                                                   AF.Copy, scale=1.0 / (DE * DM)))
        p.wait("scalar", t1)
        t2 = p.op("scalar", lambda e: e.activation(svec[0:1, 0:3], svec[0:1, 0:3],
                                                   AF.Identity, bias=self.c_eps[0:1, 0:1]))
        p.wait("scalar", t2)
        t3 = p.op("scalar", lambda e: e.activation(svec[0:1, 3:6], svec[0:1, 0:3],
                                                   AF.Copy, scale=0.5))
        p.wait("scalar", t3)
        t3b = p.op("scalar", lambda e: e.activation(svec[0:1, 10:13], svec[0:1, 0:3],
                                                    AF.Copy, scale=-0.5))
        p.wait("vector", t3b)
        t4 = p.op("vector", lambda e: e.tensor_mul(svec[0:1, 6:7], svec[0:1, 4:5], svec[0:1, 5:6]))
        p.wait("tensor", t4)
        p.wait("tensor", t3b)
        tm = p.op("tensor", lambda e: e.matmul(self.pb[1][:, 0:16], self.c_ones[:, :],
                                               svec[0:1, 0:16], start=True, stop=True))
        p.wait("scalar", tm)
        self.t_scal = p.op("scalar", lambda e: e.activation(self.c_scal[:, 0:16],
                                                            self.pb[1][:, 0:16], AF.Copy))

        p.fence()
        sgn = self.bufC[:, 0:2048]
        m01 = self.bufE[:, 0:2048]
        tern = self.bufA
        trn = self.bufC[:, 8192:16384]
        wr_sl = [p.slot(f"p0wr{i}") for i in range(4)]
        ev_toks = [None, None]
        wr_toks = {}
        last_tr = [None]
        prev_grp = [None]
        for wsrc, R, C, mi, qdst in mats:
            npc = C // 2048
            for g in range(R // 512):
                for pc in range(npc):
                    tt_toks = []
                    sg_mk = []
                    if prev_grp[0] is not None:
                        for tk in prev_grp[0]:
                            p.wait("sync", tk)
                    for jj in range(4):
                        j = g * 4 + jj
                        slot = jj % 3
                        ss = stage[:, slot * 2048: slot * 2048 + 2048]
                        if jj >= 3:
                            p.wait("sync", tt_toks[jj - 3])
                            p.wait("sync", sg_mk[jj - 3][0])
                            p.wait("sync", sg_mk[jj - 3][1])
                        tl = st_sl[slot].load("sync", ss, wsrc[j * 128:(j + 1) * 128,
                                                               pc * 2048:(pc + 1) * 2048])
                        p.wait("scalar", tl)
                        p.wait("scalar", self.t_scal)
                        if tt_toks:
                            p.wait("scalar", tt_toks[-1])
                        tsg = p.op("scalar", lambda e, s=ss, mi=mi: e.sign(
                            sgn[:, :], s, bias=self.c_scal[:, 10 + mi:11 + mi]))
                        p.wait("scalar", tsg)
                        tmk = p.op("scalar", lambda e, s=ss, mi=mi: e.sign(
                            m01[:, :], s, bias=self.c_scal[:, 3 + mi:4 + mi]))
                        p.wait("gpsimd", tsg)
                        p.wait("gpsimd", tmk)
                        p.wait("gpsimd", last_tr[0])
                        ttn = p.op("gpsimd", lambda e, d=tern[:, jj * 2048:(jj + 1) * 2048]:
                                   e.tensor_add(d, sgn[:, :], m01[:, :]))
                        tt_toks.append(ttn)
                        sg_mk.append((tsg, tmk))
                    prev_grp[0] = [tt_toks[-1], sg_mk[-1][0], sg_mk[-1][1]]
                    p.wait("tensor", tt_toks[-1])
                    for c in range(16):
                        bank = c % 2
                        if ev_toks[bank] is not None:
                            p.wait("tensor", ev_toks[bank])
                        ttr = None
                        for jj in range(4):
                            ttr = p.op("tensor", lambda e, o=self.pt[bank][:, jj * 128:(jj + 1) * 128],
                                       s=tern[:, jj * 2048 + c * 128: jj * 2048 + c * 128 + 128]:
                                       e.transpose(o, s, self.c_identb[:, :]))
                        last_tr[0] = ttr
                        eng = "vector" if bank == 0 else "scalar"
                        p.wait(eng, ttr)
                        if (mi, c) in wr_toks:
                            p.wait(eng, wr_toks.pop((mi, c)))
                        if eng == "vector":
                            tev = p.op("vector", lambda e, o=trn[:, c * 512:(c + 1) * 512], b=bank:
                                       e.tensor_copy(o, self.pt[b][:, 0:512]))
                        else:
                            tev = p.op("scalar", lambda e, o=trn[:, c * 512:(c + 1) * 512], b=bank:
                                       e.activation(o, self.pt[b][:, 0:512], AF.Copy))
                        ev_toks[bank] = tev
                        p.wait("scalar", tev)
                        wr_toks[(mi, c)] = wr_sl[c % 4].load(
                            "scalar", qdst[pc * 2048 + c * 128: pc * 2048 + (c + 1) * 128,
                                           g * 512:(g + 1) * 512],
                            trn[:, c * 512:(c + 1) * 512])

        p.fence()
        wo_c = p.slot("p0wo_c")
        wo_w = p.slot("p0wo_w")
        for cblk in range(4):
            dst = self.bufC[:, 0:8 * 2048].rearrange("p (e d) -> p e d", e=8)
            srcb = self.wo[cblk * 1024:(cblk + 1) * 1024, :].rearrange("(e p) d -> p e d", p=128)
            tl = wo_c.load("gpsimd", dst, srcb)
            p.wait("sync", tl)
            tw = wo_w.load("sync", self.wo_b[cblk * 1024:(cblk + 1) * 1024, :]
                           .rearrange("(e p) d -> p e d", p=128), dst)
            p.wait("gpsimd", tw)

    def p1_ln1_inproj(self):
        p = self.p
        xlnT = lambda k: self.bufA[:, k * 2048:(k + 1) * 2048]
        xst = [self.bufB[:, 0:2048], self.bufB[:, 2048:4096]]
        sq = self.bufD[:, 0:2048]
        st = self.bufS
        xln = self.bufE[:, 0:2048]
        x_sl = [p.slot("p1x0"), p.slot("p1x1")]

        xload = {}
        xln_done = {}
        def loadx(tt):
            if tt >= NTT:
                return
            if tt - 2 >= 0:
                p.wait("sync", xln_done[tt - 2])
            xload[tt] = x_sl[tt % 2].load("sync", xst[tt % 2],
                                          self.x_in[tt * 128:(tt + 1) * 128, :])
        loadx(0); loadx(1)
        ev_tok = [None, None]
        for tt in range(NTT):
            xs = xst[tt % 2]
            p.wait("vector", xload[tt])
            t_s1 = p.op("vector", lambda e, x=xs: e.reduce_sum(st[:, 0:1], x, axis=AX.X))
            p.wait("scalar", xload[tt])
            t_s2 = p.op("scalar", lambda e, x=xs: e.activation(sq, x, AF.Square,
                                                               accum_out=st[:, 1:2]))
            p.wait("scalar", t_s1)
            t_m = p.op("scalar", lambda e: e.activation(st[:, 2:3], st[:, 0:1],
                                                        AF.Copy, scale=1.0 / 2048.0))
            p.wait("scalar", t_m)
            t_m2 = p.op("scalar", lambda e: e.activation(st[:, 3:4], st[:, 2:3], AF.Square))
            p.wait("vector", t_s2)
            p.wait("vector", t_m2)
            t_var = p.op("vector", lambda e: e.tensor_scalar(
                st[:, 4:5], st[:, 1:2], 1.0 / 2048.0, None, ALU.mult))
            p.wait("vector", t_var)
            t_var = p.op("vector", lambda e: e.tensor_sub(st[:, 4:5], st[:, 4:5], st[:, 3:4]))
            p.wait("scalar", t_var)
            t_ln = p.op("scalar", lambda e: e.activation(st[:, 5:6], st[:, 4:5], AF.Ln,
                                                         bias=self.c_eps[:, 1:2]))
            p.wait("scalar", t_ln)
            t_rs = p.op("scalar", lambda e: e.activation(st[:, 6:7], st[:, 5:6], AF.Exp,
                                                         scale=-0.5))
            p.wait("vector", t_rs)
            t_xln = p.op("vector", lambda e, x=xs: e.tensor_scalar(
                xln, x, st[:, 2:3], st[:, 6:7], ALU.subtract, ALU.mult))
            for kb in range(4):
                bank = kb % 2
                if ev_tok[bank] is not None:
                    p.wait("tensor", ev_tok[bank])
                p.wait("tensor", t_xln)
                ttr = None
                for kk in range(4):
                    k = kb * 4 + kk
                    ttr = p.op("tensor", lambda e, o=self.pt[bank][:, kk * 128:(kk + 1) * 128],
                               s=xln[:, k * 128:(k + 1) * 128]:
                               e.transpose(o, s, self.c_identb[:, :]))
                eng = "vector" if bank == 0 else "scalar"
                p.wait(eng, ttr)
                dst = self.bufA[:, :].rearrange("p (k c) -> p k c", c=2048)[
                    :, kb * 4:(kb + 1) * 4, tt * 128:(tt + 1) * 128]
                if eng == "vector":
                    tev = p.op("vector", lambda e, d=dst, b=bank: e.tensor_copy(
                        d, self.pt[b][:, 0:512].rearrange("p (k c) -> p k c", c=128)))
                else:
                    tev = p.op("scalar", lambda e, d=dst, b=bank: e.activation(
                        d, self.pt[b][:, 0:512].rearrange("p (k c) -> p k c", c=128), AF.Copy))
                ev_tok[bank] = tev
            xln_done[tt] = t_xln
            loadx(tt + 2)
        p.fence()

        blk_slot = [self.bufC[:, 0:8192], self.bufC[:, 8192:16384]]
        blk_sl = [p.slot("p1w0"), p.slot("p1w1")]
        sg_stage = [self.bufE[:, 2048 + i * 512: 2048 + (i + 1) * 512] for i in range(4)]
        sg_sl = [p.slot(f"p1sg{i}") for i in range(4)]
        sg_wr = [None] * 4
        ps_ev = {}
        slot_busy = [None, None]

        def wblock(gp, cols, width=512):
            slot = blk_slot[gp % 2]
            if slot_busy[gp % 2] is not None:
                p.wait("gpsimd", slot_busy[gp % 2])
            dst = slot.rearrange("p (k c) -> p k c", k=16)[:, :, 0:width]
            srcb = self.wp[:, cols:cols + width].rearrange("(k p) c -> p k c", p=128)
            return blk_sl[gp % 2].load("gpsimd", dst, srcb), slot

        for gp in range(8):
            tok, slot = wblock(gp, gp * 512)
            for tt in range(NTT):
                bank = (gp * NTT + tt) % 4
                if ps_ev.get(bank) is not None:
                    p.wait("tensor", ps_ev[bank])
                p.wait("tensor", tok)
                tm = None
                for k in range(KS):
                    tm = p.op("tensor", lambda e, b=bank, k=k, tt=tt, slot=slot:
                              e.matmul(self.pb[b][:, :],
                                       xlnT(k)[:, tt * 128:(tt + 1) * 128],
                                       slot.rearrange("p (k c) -> p k c", k=16)[:, k, :],
                                       start=(k == 0), stop=(k == KS - 1)))
                sgs = sg_stage[bank]
                p.wait("scalar", tm)
                if sg_wr[bank] is not None:
                    p.wait("scalar", sg_wr[bank])
                tev = p.op("scalar", lambda e, d=sgs, b=bank:
                           e.activation(d, self.pb[b][:, :], AF.Silu))
                ps_ev[bank] = tev
                p.wait("scalar", tev)
                sg_wr[bank] = sg_sl[bank].load(
                    "scalar", self.sgate_s[tt][:, gp * 512:(gp + 1) * 512], sgs)
                slot_busy[gp % 2] = tm
        dtok, dslot = wblock(0, PROJ - 64, width=64)
        dt_stage = self.bufD[:, 2048:2112]
        dt_sl = p.slot("p1dt")
        last_wr = None
        for tt in range(NTT):
            p.wait("tensor", dtok)
            if ps_ev.get(4) is not None:
                p.wait("tensor", ps_ev[4])
            tm = None
            for k in range(KS):
                tm = p.op("tensor", lambda e, k=k, tt=tt:
                          e.matmul(self.pb[4][:, 0:64],
                                   xlnT(k)[:, tt * 128:(tt + 1) * 128],
                                   dslot.rearrange("p (k c) -> p k c", k=16)[:, k, 0:64],
                                   start=(k == 0), stop=(k == KS - 1)))
            p.wait("scalar", tm)
            if last_wr is not None:
                p.wait("scalar", last_wr)
            tev = p.op("scalar", lambda e: e.activation(dt_stage, self.pb[4][:, 0:64], AF.Copy))
            ps_ev[4] = tev
            p.wait("scalar", tev)
            last_wr = dt_sl.load("scalar", self.dt_s[tt], dt_stage)
            slot_busy[0] = tm

        p.fence(("tensor",))
        hb_stage = [self.bufE[:, 4096 + i * 512: 4096 + (i + 1) * 512] for i in range(4)]
        hb_sl = [p.slot(f"p1hb{i}") for i in range(4)]
        hb_wr = [None] * 4
        ps_ev = {}
        for cb in range(12):
            tok, slot = wblock(cb, INTER + cb * 512)
            for ct in range(4):
                for tp in range(4):
                    bank = (ct * 4 + tp) % 4
                    if ps_ev.get(bank) is not None:
                        p.wait("tensor", ps_ev[bank])
                    p.wait("tensor", tok)
                    tm = None
                    for k in range(KS):
                        tm = p.op("tensor", lambda e, b=bank, k=k, ct=ct, tp=tp, slot=slot:
                                  e.matmul(self.pb[b][:, :],
                                           slot.rearrange("p (k c) -> p k c", k=16)[:, k, ct * 128:(ct + 1) * 128],
                                           xlnT(k)[:, tp * 512:(tp + 1) * 512],
                                           start=(k == 0), stop=(k == KS - 1)))
                    eng = "vector" if bank % 2 == 0 else "scalar"
                    hbs = hb_stage[bank]
                    p.wait(eng, tm)
                    if hb_wr[bank] is not None:
                        p.wait(eng, hb_wr[bank])
                    if eng == "vector":
                        tev = p.op("vector", lambda e, d=hbs, b=bank: e.tensor_copy(d, self.pb[b][:, :]))
                    else:
                        tev = p.op("scalar", lambda e, d=hbs, b=bank: e.activation(d, self.pb[b][:, :], AF.Copy))
                    ps_ev[bank] = tev
                    wr_eng = "scalar" if eng == "scalar" else "sync"
                    p.wait(wr_eng, tev)
                    hb_wr[bank] = hb_sl[bank].load(
                        wr_eng, self.hbc_s[cb * 4 + ct][:, tp * 512:(tp + 1) * 512], hbs)
                    slot_busy[cb % 2] = tm

    def p2_mamba(self):
        p = self.p
        A = self.bufA
        C_ = self.bufC
        D_ = self.bufD
        E_ = self.bufE
        pb = self.pb
        hs_fm = lambda s: A[:, s * 512:(s + 1) * 512]
        hs_tm = A[:, 16384:20480]
        hs_dt = A[:, 20480:24576]
        mT = A[:, 24576:32768]
        yn_b = A[:, 24576:28672]
        conv_in = [C_[:, 0:520], C_[:, 520:1040]]
        B_fm = lambda g: C_[:, 2048 + g * 512: 2048 + (g + 1) * 512]
        Cc_fm = lambda g: C_[:, 6144 + g * 512: 6144 + (g + 1) * 512]
        sgate_c = C_[:, 10240:14336]
        ynT_st = C_[:, 14336:16384]
        B_tm = E_[:, 0:1024]
        Sst_b = E_[:, 1024:5120]
        SmT = E_[:, 5120:6144]
        Ereg = E_[:, 6144:7168]
        y = self.bufB[:, 0:4096]
        Sst = D_[:, 0:4096]
        dtv = {"raw": D_[:, 4096:4160], "u": D_[:, 4160:4224], "au": D_[:, 4224:4288],
               "e1": D_[:, 4288:4352], "l1": D_[:, 4352:4416], "mx": D_[:, 4416:4480],
               "sp": D_[:, 4480:4544], "abt": D_[:, 4544:4608], "acn": D_[:, 4608:4672],
               "eac": D_[:, 4672:4736], "dst": D_[:, 4736:4800], "eal": D_[:, 4800:4864],
               "dtD": D_[:, 4864:4928], "nacn": D_[:, 5056:5120]}
        acumT = D_[0:64, 4928:5056]
        yo_tmp = D_[:, 5120:5632]
        su_tmp = D_[:, 5632:6144]
        cva = D_[:, 6144:6656]
        cvb = D_[:, 6656:7168]
        ealr = self.bufS[0:1, 100:164]

        sl_hbc = [p.slot("p2hb0"), p.slot("p2hb1")]
        sl_dt = p.slot("p2dt")
        sl_sg = p.slot("p2sg")
        sl_af_w = p.slot("p2afw")
        sl_af_r = [p.slot(f"p2afr{i}") for i in range(4)]
        sl_ynw = [p.slot("p2yw0"), p.slot("p2yw1")]

        t0 = p.op("vector", lambda e: e.memset(Sst, 0.0))
        t0b = p.op("vector", lambda e: e.memset(Sst_b, 0.0))
        tok_state_b = t0b
        tok_state_mm = None
        conv_busy = {}
        ev_tok = [None, None]
        yn_wr = [None, None]
        prev_tYN = None
        prev_tYF = None
        prev_yd = None
        prev_chunk_tensor = None

        for q in range(NQ):
            q0 = q * QT
            for s in range(48):
                ci_sl = conv_in[s % 2]
                if conv_busy.get(("tap", s - 2)) is not None:
                    p.wait("sync", conv_busy[("tap", s - 2)])
                lo = max(q0 - 1, 0)
                hi = min(q0 + QT + 2, T)
                ncols = hi - lo
                off = 1 if q0 == 0 else 0
                tmemz = None
                if off > 0 or ncols + off < 515:
                    if conv_busy.get(("tap", s - 2)) is not None:
                        p.wait("gpsimd", conv_busy[("tap", s - 2)])
                    tmemz = p.op("gpsimd", lambda e, c=ci_sl: e.memset(c[:, 0:515], 0.0))
                    p.wait("sync", tmemz)
                tl = sl_hbc[s % 2].load("sync", ci_sl[:, off:off + ncols],
                                        self.hbc_s[s][:, lo:hi])
                p.wait("vector", tl)
                if tmemz is not None:
                    p.wait("vector", tmemz)
                if conv_busy.get(("scr", s - 1)) is not None:
                    p.wait("vector", conv_busy[("scr", s - 1)])
                if prev_chunk_tensor is not None and s < 2:
                    p.wait("vector", prev_chunk_tensor)
                    p.wait("gpsimd", prev_chunk_tensor)
                    p.wait("scalar", prev_chunk_tensor)
                tc0 = p.op("vector", lambda e, c=ci_sl, s=s: e.tensor_scalar(
                    cva, c[:, 0:512], self.c_convw[:, s * 4: s * 4 + 1], None, ALU.mult))
                p.wait("vector", tc0)
                tc1 = p.op("vector", lambda e, c=ci_sl, s=s: e.scalar_tensor_tensor(
                    cvb, c[:, 1:513], self.c_convw[:, s * 4 + 1: s * 4 + 2], cva,
                    ALU.mult, ALU.add))
                p.wait("vector", tc1)
                tc2 = p.op("vector", lambda e, c=ci_sl, s=s: e.scalar_tensor_tensor(
                    cva, c[:, 2:514], self.c_convw[:, s * 4 + 2: s * 4 + 3], cvb,
                    ALU.mult, ALU.add))
                p.wait("vector", tc2)
                tc3 = p.op("vector", lambda e, c=ci_sl, s=s: e.scalar_tensor_tensor(
                    cvb, c[:, 3:515], self.c_convw[:, s * 4 + 3: s * 4 + 4], cva,
                    ALU.mult, ALU.add))
                conv_busy[("tap", s)] = tc3
                if s < 32:
                    dst = hs_fm(s)
                elif s < 40:
                    dst = B_fm(s - 32)
                else:
                    dst = Cc_fm(s - 40)
                p.wait("scalar", tc3)
                tsl = p.op("scalar", lambda e, d=dst, s=s: e.activation(
                    d, cvb, AF.Silu, bias=self.c_convb[:, s:s + 1]))
                conv_busy[("scr", s)] = tsl
                conv_busy[("silu", s)] = tsl
            t_conv_all = conv_busy[("silu", 47)]
            p.wait("tensor", t_conv_all)
            if "p2b" in self.debug:
                dB = self.dbg_out("d_B", (NQ, 128, 4096), BF16)
                dC = self.dbg_out("d_C", (NQ, 128, 4096), BF16)
                self._dbgsl5 = getattr(self, "_dbgsl5", None) or p.slot("dbgb")
                self._dbgsl6 = getattr(self, "_dbgsl6", None) or p.slot("dbgc")
                p.wait("sync", t_conv_all)
                tb1 = self._dbgsl5.load("sync", dB[q], C_[:, 2048:6144])
                tb2 = self._dbgsl6.load("sync", dC[q], C_[:, 6144:10240])
                p.wait("vector", tb1)
                p.wait("vector", tb2)
                p.wait("scalar", tb1)
                p.wait("scalar", tb2)

            for cc in range(4):
                if self.p2_lvl < 2:
                    p.fence()
                    continue
                ci = q * 4 + cc
                csl = slice(cc * 128, (cc + 1) * 128)

                if prev_yd is not None:
                    p.wait("sync", prev_yd)
                tok_dt_l = sl_dt.load("sync", dtv["raw"], self.dt_s[ci])
                p.wait("vector", tok_dt_l)
                tU = p.op("vector", lambda e: e.tensor_add(dtv["u"], dtv["raw"], self.c_dtb[:, :]))
                p.wait("scalar", tU)
                tAU = p.op("scalar", lambda e: e.activation(dtv["au"], dtv["u"], AF.Abs))
                p.wait("scalar", tAU)
                tE1 = p.op("scalar", lambda e: e.activation(dtv["e1"], dtv["au"], AF.Exp, scale=-1.0))
                p.wait("scalar", tE1)
                tL1 = p.op("scalar", lambda e: e.activation(dtv["l1"], dtv["e1"], AF.Ln,
                                                            bias=self.c_eps[:, 3:4]))
                tMX = p.op("scalar", lambda e: e.activation(dtv["mx"], dtv["u"], AF.Relu))
                p.wait("vector", tL1)
                p.wait("vector", tMX)
                tSP = p.op("vector", lambda e: e.tensor_add(dtv["sp"], dtv["mx"], dtv["l1"]))
                p.wait("vector", tSP)
                p.wait("vector", self.t_expa)
                tABT = p.op("vector", lambda e: e.tensor_mul(dtv["abt"], dtv["sp"], self.c_expa[:, :]))
                if self.p2_aprep < 1:
                    p.fence()
                    prev_yd = prev_chunk_tensor = tABT
                    continue
                p.wait("tensor", tABT)
                tM1 = p.op("tensor", lambda e: e.matmul(pb[5][:, 0:64], self.c_cums[:, :],
                                                        dtv["abt"], start=True, stop=True))
                tM2 = p.op("tensor", lambda e: e.matmul(pb[3][:, 0:64], self.c_revs[:, :],
                                                        dtv["abt"], start=True, stop=True))
                tM3 = p.op("tensor", lambda e: e.matmul(pb[2][:, 0:64], self.c_revi[:, :],
                                                        dtv["abt"], start=True, stop=True))
                p.wait("scalar", tM1)
                tACN = p.op("scalar", lambda e: e.activation(dtv["acn"], pb[5][:, 0:64], AF.Copy))
                tNAC = p.op("scalar", lambda e: e.activation(dtv["nacn"], pb[5][:, 0:64],
                                                             AF.Copy, scale=-1.0))
                tEAC = p.op("scalar", lambda e: e.activation(dtv["eac"], pb[5][:, 0:64],
                                                             AF.Exp, scale=-1.0))
                p.wait("scalar", tM2)
                tDST = p.op("scalar", lambda e: e.activation(dtv["dst"], pb[3][:, 0:64],
                                                             AF.Exp, scale=-1.0))
                if self.p2_aprep < 2:
                    p.fence()
                    prev_yd = prev_chunk_tensor = tDST
                    continue
                p.wait("scalar", tM3)
                tEALR = p.op("scalar", lambda e: e.activation(ealr, pb[2][0:1, 0:64],
                                                              AF.Exp, scale=-1.0))
                p.wait("tensor", tEALR)
                p.wait("tensor", tDST)
                tM4 = p.op("tensor", lambda e: e.matmul(pb[3][:, 0:64], self.c_ones[:, :],
                                                        ealr, start=True, stop=True))
                p.wait("scalar", tM4)
                tEAL = p.op("scalar", lambda e: e.activation(dtv["eal"], pb[3][:, 0:64], AF.Copy))
                if self.p2_aprep < 3:
                    p.fence()
                    prev_yd = prev_chunk_tensor = tEAL
                    continue
                p.wait("tensor", tACN)
                tTRA = p.op("tensor", lambda e: e.transpose(pb[4][0:64, 0:128], dtv["acn"],
                                                            self.c_identf[:, :]))
                p.wait("scalar", tTRA)
                p.wait("scalar", sl_af_w.tok())
                tACT2 = p.op("scalar", lambda e: e.activation(acumT, pb[4][0:64, 0:128], AF.Copy))
                if self.p2_aprep < 4:
                    p.fence()
                    prev_yd = prev_chunk_tensor = tACT2
                    continue
                p.wait("scalar", tACT2)
                tAFW = sl_af_w.load("scalar", self.aflat_s[ci][0, :]
                                    .rearrange("(h t) -> h t", h=64), acumT)

                if self.p2_lvl < 3:
                    p.fence()
                    prev_yd = prev_chunk_tensor = tACT2
                    continue
                if prev_tYN is not None:
                    p.wait("tensor", prev_tYN)
                for grp in range(8):
                    bank = grp % 2
                    if ev_tok[bank] is not None:
                        p.wait("tensor", ev_tok[bank])
                    ttr = None
                    for kk in range(4):
                        ttr = p.op("tensor", lambda e, o=self.pt[bank][:, kk * 128:(kk + 1) * 128],
                                   src=hs_fm(grp * 4 + kk)[:, csl]:
                                   e.transpose(o, src, self.c_identb[:, :]))
                    eng = "vector" if bank == 0 else "scalar"
                    p.wait(eng, ttr)
                    if eng == "vector":
                        tev = p.op("vector", lambda e, g=grp, b=bank: e.tensor_copy(
                            hs_tm[:, g * 512:(g + 1) * 512], self.pt[b][:, 0:512]))
                    else:
                        tev = p.op("scalar", lambda e, g=grp, b=bank: e.activation(
                            hs_tm[:, g * 512:(g + 1) * 512], self.pt[b][:, 0:512], AF.Copy))
                    ev_tok[bank] = tev
                tok_hstm = ev_tok[:]
                for gb in range(2):
                    bank = gb
                    if ev_tok[bank] is not None:
                        p.wait("tensor", ev_tok[bank])
                    ttr = None
                    for kk in range(4):
                        ttr = p.op("tensor", lambda e, o=self.pt[bank][:, kk * 128:(kk + 1) * 128],
                                   src=B_fm(gb * 4 + kk)[:, csl]:
                                   e.transpose(o, src, self.c_identb[:, :]))
                    eng = "vector" if bank == 0 else "scalar"
                    p.wait(eng, ttr)
                    if eng == "vector":
                        tev = p.op("vector", lambda e, g=gb, b=bank: e.tensor_copy(
                            B_tm[:, g * 512:(g + 1) * 512], self.pt[b][:, 0:512]))
                    else:
                        tev = p.op("scalar", lambda e, g=gb, b=bank: e.activation(
                            B_tm[:, g * 512:(g + 1) * 512], self.pt[b][:, 0:512], AF.Copy))
                    ev_tok[bank] = tev
                tok_btm = ev_tok[:]

                if self.p2_lvl < 4:
                    p.fence()
                    prev_yd = prev_chunk_tensor = ev_tok[1]
                    continue
                tok_smt = []
                for half in range(2):
                    bank = half
                    if getattr(self, "_prev_yo", None) is not None:
                        p.wait("tensor", self._prev_yo[6 + half])
                    tm = None
                    for gg in range(4):
                        g = half * 4 + gg
                        tm = p.op("tensor", lambda e, b=bank, gg=gg, g=g, csl=csl: e.matmul(
                            pb[b][:, gg * 128:(gg + 1) * 128],
                            B_fm(g)[:, csl], Cc_fm(g)[:, csl], start=True, stop=True))
                    p.wait("vector", tm)
                    if getattr(self, "_dbg_s", None) is not None:
                        p.wait("vector", self._dbg_s)
                    tmk = p.op("vector", lambda e, b=bank, h=half: e.tensor_tensor(
                        SmT[:, h * 512:(h + 1) * 512].rearrange("p (g l) -> p g l", g=4),
                        pb[b][:, :].rearrange("p (g l) -> p g l", g=4),
                        self.c_maskT[:, :].unsqueeze(1).broadcast_to([128, 4, 128])
                            .transpose([0, 1, 2]),
                        ALU.mult))
                    tok_smt.append((tmk, tm))

                if self.p2_lvl < 5:
                    p.fence()
                    prev_yd = prev_chunk_tensor = tok_smt[1][0]
                    continue
                tok_mT = {}
                prev_ex = None
                p.wait("sync", tAFW)
                for r in range(16):
                    hbase = r * 4
                    afslot = r % 4
                    afr = self.aflat[0:1, afslot * 1024: afslot * 1024 + 512]
                    if r >= 4:
                        p.wait("sync", tok_mT[r - 4])
                    elif getattr(self, "_prev_mT", None) is not None:
                        p.wait("sync", self._prev_mT[12 + r])
                    tAF = sl_af_r[afslot].load(
                        "sync", afr, self.aflat_s[ci][0:1, hbase * 128:(hbase + 4) * 128])
                    bank = 2 + (r % 2)
                    p.wait("tensor", tAF)
                    if r >= 2:
                        p.wait("tensor", tok_mT[r - 2])
                    else:
                        p.wait("tensor", tDST)
                        p.wait("tensor", tEALR)
                        p.wait("tensor", tEAL)
                        if prev_yd is not None:
                            p.wait("tensor", prev_yd)
                    tBC = p.op("tensor", lambda e, b=bank, afr=afr: e.matmul(
                        pb[b][:, 0:512], self.c_ones[:, :], afr,
                        start=True, stop=True))
                    tEx = None
                    p.wait("scalar", tBC)
                    p.wait("scalar", tNAC)
                    if prev_ex is not None:
                        p.wait("scalar", prev_ex)
                        p.wait("scalar", tok_mT[r - 1])
                    if r < 2 and getattr(self, "_dbg_tok", None) is not None:
                        p.wait("vector", self._dbg_tok)
                    for hh in range(4):
                        h = hbase + hh
                        tEx = p.op("scalar", lambda e, b=bank, hh=hh, h=h: e.activation(
                            Ereg[:, hh * 128:(hh + 1) * 128],
                            pb[b][:, hh * 128:(hh + 1) * 128], AF.Relu, scale=1.0,
                            bias=dtv["nacn"][:, h:h + 1]))
                    p.wait("scalar", tEx)
                    tEx = p.op("scalar", lambda e: e.activation(
                        Ereg[:, 512:1024], Ereg[:, 0:512], AF.Exp, scale=-1.0))
                    prev_ex = tEx
                    p.wait("vector", tEx)
                    p.wait("vector", tok_smt[hbase // 32][0])
                    if prev_yd is not None and r < 2:
                        p.wait("vector", prev_yd)
                    g = hbase // 8
                    tMT = p.op("vector", lambda e, r=r, g=g: e.tensor_tensor(
                        mT[:, r * 512:(r + 1) * 512].rearrange("p (h l) -> p h l", h=4),
                        Ereg[:, 512:1024].rearrange("p (h l) -> p h l", h=4),
                        SmT[:, g * 128:(g + 1) * 128].unsqueeze(1).broadcast_to([128, 4, 128]),
                        ALU.mult))
                    tok_mT[r] = tMT
                if True:
                    self._prev_mT = tok_mT

                if self.p2_lvl < 6:
                    p.fence()
                    prev_yd = prev_chunk_tensor = tok_mT[15]
                    continue
                if "p2m" in self.debug:
                    dmT = self.dbg_out("d_mT", (NCHUNK, 128, 8192), BF16)
                    dsm = self.dbg_out("d_sm", (NCHUNK, 128, 1024), BF16)
                    self._dbgsl3 = getattr(self, "_dbgsl3", None) or p.slot("dbgm")
                    self._dbgsl4 = getattr(self, "_dbgsl4", None) or p.slot("dbgs")
                    p.wait("sync", tok_mT[15])
                    self._dbg_m = self._dbgsl3.load("sync", dmT[ci], mT)
                    self._dbg_s = self._dbgsl4.load("sync", dsm[ci], SmT)
                for t4 in tok_hstm:
                    p.wait("vector", t4)
                p.wait("vector", tSP)
                tHD = p.op("vector", lambda e: e.tensor_tensor(
                    hs_dt[:, :].rearrange("p (h d) -> p h d", h=64),
                    hs_tm[:, :].rearrange("p (h d) -> p h d", h=64),
                    dtv["sp"][:, :].unsqueeze(2).broadcast_to([128, 64, 64]),
                    ALU.mult))

                tok_yd = {}
                tok_yev = {}
                for g8 in range(8):
                    bank = 4 + (g8 % 2)
                    if g8 >= 2:
                        p.wait("tensor", tok_yev[g8 - 2])
                    else:
                        p.wait("tensor", tEAL if bank == 5 else tACT2)
                    p.wait("tensor", tHD)
                    tm = None
                    for hh in range(8):
                        h = g8 * 8 + hh
                        if hh % 4 == 0:
                            p.wait("tensor", tok_mT[h // 4])
                        tm = p.op("tensor", lambda e, b=bank, hh=hh, h=h: e.matmul(
                            pb[b][:, hh * 64:(hh + 1) * 64],
                            mT[:, h * 128:(h + 1) * 128],
                            hs_dt[:, h * 64:(h + 1) * 64],
                            start=True, stop=True))
                    tok_yd[g8] = tm
                    p.wait("scalar", tm)
                    if g8 < 2 and prev_tYN is not None:
                        p.wait("scalar", prev_tYN)
                    if g8 < 2 and getattr(self, "_dbg_tok_y", None) is not None:
                        p.wait("scalar", self._dbg_tok_y)
                    tev = p.op("scalar", lambda e, b=bank, g8=g8: e.activation(
                        y[:, g8 * 512:(g8 + 1) * 512], pb[b][:, :], AF.Copy))
                    tok_yev[g8] = tev

                if self.p2_lvl < 7:
                    p.fence()
                    prev_yd = prev_chunk_tensor = tok_yd[7]
                    prev_tYN = tok_yev[7]
                    continue
                tok_yo = {}
                last_yo_mm = None
                for g8 in range(8):
                    bank = g8 % 2
                    if g8 < 2:
                        p.wait("tensor", tok_smt[0][0])
                        p.wait("tensor", tok_smt[1][0])
                        p.wait("tensor", tok_state_b)
                    if g8 >= 2:
                        p.wait("tensor", tok_yo[g8 - 2])
                    tm = None
                    for hh in range(8):
                        h = g8 * 8 + hh
                        g = h // 8
                        tm = p.op("tensor", lambda e, b=bank, hh=hh, h=h, g=g, csl=csl: e.matmul(
                            pb[b][:, hh * 64:(hh + 1) * 64],
                            Cc_fm(g)[:, csl],
                            Sst_b[:, h * 64:(h + 1) * 64],
                            start=True, stop=True))
                    last_yo_mm = tm
                    p.wait("vector", tm)
                    p.wait("vector", tEAC)
                    p.wait("vector", tok_yev[g8])
                    if g8 >= 1:
                        p.wait("vector", tok_yo[g8 - 1])
                    tt1 = p.op("vector", lambda e, b=bank, g8=g8: e.tensor_tensor(
                        yo_tmp[:, :].rearrange("p (h d) -> p h d", h=8),
                        pb[b][:, :].rearrange("p (h d) -> p h d", h=8),
                        dtv["eac"][:, g8 * 8:(g8 + 1) * 8].unsqueeze(2).broadcast_to([128, 8, 64]),
                        ALU.mult))
                    p.wait("vector", tt1)
                    tt2 = p.op("vector", lambda e, g8=g8: e.tensor_add(
                        y[:, g8 * 512:(g8 + 1) * 512], y[:, g8 * 512:(g8 + 1) * 512],
                        yo_tmp[:, :]))
                    tok_yo[g8] = tt2
                    self._prev_yo = tok_yo

                if self.p2_lvl < 8:
                    p.fence()
                    prev_yd = prev_chunk_tensor = tok_yo[7]
                    prev_tYN = tok_yev[7]
                    continue
                p.wait("gpsimd", tok_yd[7])
                p.wait("gpsimd", tHD)
                for t4 in tok_hstm:
                    p.wait("gpsimd", t4)
                tDR = p.op("gpsimd", lambda e: e.tensor_tensor(
                    hs_dt[:, :].rearrange("p (h d) -> p h d", h=64),
                    hs_tm[:, :].rearrange("p (h d) -> p h d", h=64),
                    self.c_d128[:, :].unsqueeze(2).broadcast_to([128, 64, 64]),
                    ALU.mult))
                if prev_tYF is not None:
                    p.wait("sync", prev_tYF)
                tok_sg = sl_sg.load("sync", sgate_c, self.sgate_s[ci])
                p.wait("vector", tok_yo[7])
                p.wait("vector", tDR)
                tY2 = p.op("vector", lambda e: e.tensor_add(y, y, hs_dt))
                p.wait("vector", tok_sg)
                p.wait("vector", tY2)
                tYF = p.op("vector", lambda e: e.tensor_mul(y, y, sgate_c))
                prev_tYF = tYF
                if "p2y" in self.debug:
                    dy = self.dbg_out("d_y", (NCHUNK, 128, INTER))
                    p.wait("sync", tYF)
                    self._dbgsl = getattr(self, "_dbgsl", None) or p.slot("dbgy")
                    self._dbg_tok_y = self._dbgsl.load("sync", dy[ci], y)
                p.wait("scalar", tYF)
                p.wait("scalar", tok_yd[7])
                if getattr(self, "_dbg_m", None) is not None:
                    p.wait("scalar", self._dbg_m)
                tSQ = p.op("scalar", lambda e: e.activation(yn_b, y, AF.Square,
                                                            accum_out=self.bufS[:, 10:11]))
                p.wait("scalar", tSQ)
                tMS = p.op("scalar", lambda e: e.activation(self.bufS[:, 11:12],
                                                            self.bufS[:, 10:11],
                                                            AF.Copy, scale=1.0 / INTER))
                p.wait("scalar", tMS)
                tLN = p.op("scalar", lambda e: e.activation(self.bufS[:, 12:13],
                                                            self.bufS[:, 11:12],
                                                            AF.Ln, bias=self.c_eps[:, 2:3]))
                p.wait("scalar", tLN)
                tRS = p.op("scalar", lambda e: e.activation(self.bufS[:, 13:14],
                                                            self.bufS[:, 12:13],
                                                            AF.Exp, scale=-0.5))
                p.wait("vector", tRS)
                p.wait("vector", tSQ)
                tYN = p.op("vector", lambda e: e.scalar_tensor_tensor(
                    yn_b, y, self.bufS[:, 13:14], self.c_normw[:, :], ALU.mult, ALU.mult))
                prev_tYN = tYN
                if "p2y" in self.debug:
                    dyn = self.dbg_out("d_yn", (NCHUNK, 128, INTER), BF16)
                    p.wait("sync", tYN)
                    self._dbgsl2 = getattr(self, "_dbgsl2", None) or p.slot("dbgyn")
                    self._dbg_tok = self._dbgsl2.load("sync", dyn[ci], yn_b)

                for grp4 in range(4):
                    bank = grp4 % 2
                    if ev_tok[bank] is not None:
                        p.wait("tensor", ev_tok[bank])
                    p.wait("tensor", tYN)
                    ttr = None
                    for kk in range(8):
                        s = grp4 * 8 + kk
                        ttr = p.op("tensor", lambda e, o=self.pt[bank][:, kk * 128:(kk + 1) * 128],
                                   src=yn_b[:, s * 128:(s + 1) * 128]:
                                   e.transpose(o, src, self.c_identb[:, :]))
                    eng = "vector" if bank == 0 else "scalar"
                    p.wait(eng, ttr)
                    if yn_wr[grp4 % 2] is not None:
                        p.wait(eng, yn_wr[grp4 % 2])
                    if eng == "vector":
                        tev = p.op("vector", lambda e, g=grp4, b=bank: e.tensor_copy(
                            ynT_st[:, (g % 2) * 1024:((g % 2) + 1) * 1024], self.pt[b][:, 0:1024]))
                    else:
                        tev = p.op("scalar", lambda e, g=grp4, b=bank: e.activation(
                            ynT_st[:, (g % 2) * 1024:((g % 2) + 1) * 1024], self.pt[b][:, 0:1024],
                            AF.Copy))
                    ev_tok[bank] = tev
                    p.wait("sync", tev)
                    yn_wr[grp4 % 2] = sl_ynw[grp4 % 2].load(
                        "sync",
                        self.ynT_s[ci][grp4 * 8:(grp4 + 1) * 8].transpose([1, 0, 2]),
                        ynT_st[:, (grp4 % 2) * 1024:((grp4 % 2) + 1) * 1024]
                        .rearrange("p (s b) -> p s b", s=8))

                if self.p2_lvl < 9:
                    p.fence()
                    prev_yd = prev_chunk_tensor = tYN
                    continue
                p.wait("vector", tY2)
                p.wait("vector", tDST)
                tdD = p.op("vector", lambda e: e.tensor_mul(dtv["dtD"], dtv["sp"], dtv["dst"]))
                p.wait("vector", tdD)
                p.wait("vector", tok_yd[7])
                tHS2 = p.op("vector", lambda e: e.tensor_tensor(
                    hs_dt[:, :].rearrange("p (h d) -> p h d", h=64),
                    hs_tm[:, :].rearrange("p (h d) -> p h d", h=64),
                    dtv["dtD"][:, :].unsqueeze(2).broadcast_to([128, 64, 64]),
                    ALU.mult))
                tok_sup = {}
                for g8 in range(8):
                    bank = 2 + (g8 % 2)
                    if g8 >= 2:
                        p.wait("tensor", tok_sup[g8 - 2])
                    else:
                        p.wait("tensor", tok_mT[14 + g8])
                    p.wait("tensor", tHS2)
                    for t4 in tok_btm:
                        p.wait("tensor", t4)
                    tm = None
                    for hh in range(8):
                        h = g8 * 8 + hh
                        tm = p.op("tensor", lambda e, b=bank, hh=hh, h=h, g=g8: e.matmul(
                            pb[b][:, hh * 64:(hh + 1) * 64],
                            B_tm[:, g * 128:(g + 1) * 128],
                            hs_dt[:, h * 64:(h + 1) * 64],
                            start=True, stop=True))
                    p.wait("gpsimd", tEAL)
                    p.wait("gpsimd", t0)
                    if tok_sup.get(g8 - 1) is not None:
                        p.wait("gpsimd", tok_sup[g8 - 1])
                    tP = p.op("gpsimd", lambda e, g8=g8: e.tensor_tensor(
                        su_tmp[:, :].rearrange("p (h d) -> p h d", h=8),
                        Sst[:, g8 * 512:(g8 + 1) * 512].rearrange("p (h d) -> p h d", h=8),
                        dtv["eal"][:, g8 * 8:(g8 + 1) * 8].unsqueeze(2).broadcast_to([128, 8, 64]),
                        ALU.mult))
                    p.wait("vector", tP)
                    p.wait("vector", tm)
                    tS = p.op("vector", lambda e, b=bank, g8=g8: e.tensor_add(
                        Sst[:, g8 * 512:(g8 + 1) * 512], su_tmp[:, :], pb[b][:, :]))
                    tok_sup[g8] = tS
                p.wait("scalar", tok_sup[6])
                p.wait("scalar", tok_sup[7])
                p.wait("scalar", last_yo_mm)
                tok_state_b = p.op("scalar", lambda e: e.activation(Sst_b, Sst, AF.Copy))
                prev_yd = tok_sup[7]
                prev_chunk_tensor = tok_sup[7]

    def p3_ffn(self):
        p = self.p
        A = self.bufA
        C_ = self.bufC
        D_ = self.bufD
        E_ = self.bufE
        pb = self.pb
        h_q = lambda es: A[:, es * 512:(es + 1) * 512]
        wo_sl = lambda i: A[:, 4096 + i * 2048: 4096 + (i + 1) * 2048]
        ynT_q = lambda t4: C_[:, t4 * 4096:(t4 + 1) * 4096]
        wup_sl = lambda i: C_[:, i * 2048:(i + 1) * 2048]
        wg_sl = lambda i: C_[:, 4096 + i * 2048: 4096 + (i + 1) * 2048]
        wd_sl = lambda i: C_[:, 8192 + i * 2048: 8192 + (i + 1) * 2048]
        xln2T = lambda k: E_[:, k * 512:(k + 1) * 512]
        x1t = D_[:, 0:2048]
        x1r = [D_[:, 0:2048], D_[:, 2048:4096]]
        sq = D_[:, 4096:6144]
        st = self.bufS
        xl2 = self.bufB[:, 0:1024].bitcast(BF16)
        sup = self.bufB[:, 1024:1280].bitcast(BF16)
        outst = [D_[:, 6144 + i * 512: 6144 + (i + 1) * 512] for i in range(2)]

        sl_yn = [p.slot(f"p3yn{i}") for i in range(4)]
        sl_wo = [p.slot("p3wo0"), p.slot("p3wo1")]
        sl_x = p.slot("p3x")
        sl_x1w = p.slot("p3x1w")
        sl_x1r = [p.slot("p3xr0"), p.slot("p3xr1")]
        sl_wu = [p.slot("p3wu0"), p.slot("p3wu1")]
        sl_wg = [p.slot("p3wg0"), p.slot("p3wg1")]
        sl_wd = [p.slot(f"p3wd{i}") for i in range(4)]
        sl_out = [p.slot("p3o0"), p.slot("p3o1")]
        self._wd_busy = [None] * 4
        self._dn_prev_evac = None

        for q in range(NQ):
            p.fence()
            tok_yn = {}
            for t4 in range(4):
                tok_yn[t4] = sl_yn[t4].load(
                    "sync", ynT_q(t4).rearrange("p (s b) -> p s b", s=32),
                    self.ynT_s[q * 4 + t4].transpose([1, 0, 2]))
            ev_tok = [None, None]
            prev_evac = None
            for pr in range(2):
                wo_tok = [None, None]
                last_mm = None
                if prev_evac is not None:
                    p.wait("tensor", prev_evac)
                for b2 in range(2):
                    if ev_tok[b2] is not None:
                        p.wait("tensor", ev_tok[b2])
                for isl in range(32):
                    slot = isl % 2
                    eng_w = "sync" if slot == 0 else "scalar"
                    if isl >= 2:
                        p.wait(eng_w, last_mm)
                    wo_tok[slot] = sl_wo[slot].load(
                        eng_w, wo_sl(slot), self.wo_b[isl * 128:(isl + 1) * 128, :])
                    p.wait("tensor", wo_tok[slot])
                    for ttl in range(2):
                        t4 = pr * 2 + ttl
                        p.wait("tensor", tok_yn[t4])
                        for dp in range(4):
                            bank = ttl * 4 + dp
                            last_mm = p.op("tensor", lambda e, b=bank, isl=isl, t4=t4,
                                           slot=slot, dp=dp: e.matmul(
                                pb[b][:, :],
                                ynT_q(t4)[:, isl * 128:(isl + 1) * 128],
                                wo_sl(slot)[:, dp * 512:(dp + 1) * 512],
                                start=(isl == 0), stop=(isl == 31)))
                tadds = [None, None]
                for ttl in range(2):
                    t4 = pr * 2 + ttl
                    tt_glob = q * 4 + t4
                    tx = sl_x1r[ttl].load("sync", x1r[ttl],
                                          self.x_in[tt_glob * 128:(tt_glob + 1) * 128, :])
                    p.wait("vector", tx)
                    p.wait("vector", last_mm)
                    tadd = None
                    for dp in range(4):
                        bank = ttl * 4 + dp
                        tadd = p.op("vector", lambda e, b=bank, dp=dp, ttl=ttl: e.tensor_add(
                            x1r[ttl][:, dp * 512:(dp + 1) * 512],
                            x1r[ttl][:, dp * 512:(dp + 1) * 512], pb[b][:, :]))
                    tadds[ttl] = tadd
                prev_evac = tadds[1]
                for ttl in range(2):
                    t4 = pr * 2 + ttl
                    tt_glob = q * 4 + t4
                    x1v = x1r[ttl]
                    p.wait("vector", tadds[ttl])
                    t_s1 = p.op("vector", lambda e, x1v=x1v: e.reduce_sum(st[:, 0:1], x1v, axis=AX.X))
                    p.wait("scalar", tadds[ttl])
                    t_s2 = p.op("scalar", lambda e, x1v=x1v: e.activation(sq, x1v, AF.Square,
                                                                 accum_out=st[:, 1:2]))
                    p.wait("scalar", t_s1)
                    t_m = p.op("scalar", lambda e: e.activation(st[:, 2:3], st[:, 0:1],
                                                                AF.Copy, scale=1.0 / 2048.0))
                    p.wait("scalar", t_m)
                    t_m2 = p.op("scalar", lambda e: e.activation(st[:, 3:4], st[:, 2:3], AF.Square))
                    p.wait("vector", t_s2)
                    p.wait("vector", t_m2)
                    t_v = p.op("vector", lambda e: e.tensor_scalar(
                        st[:, 4:5], st[:, 1:2], 1.0 / 2048.0, None, ALU.mult))
                    p.wait("vector", t_v)
                    t_v = p.op("vector", lambda e: e.tensor_sub(st[:, 4:5], st[:, 4:5], st[:, 3:4]))
                    p.wait("scalar", t_v)
                    t_ln = p.op("scalar", lambda e: e.activation(st[:, 5:6], st[:, 4:5], AF.Ln,
                                                                 bias=self.c_eps[:, 1:2]))
                    p.wait("scalar", t_ln)
                    t_rs = p.op("scalar", lambda e: e.activation(st[:, 6:7], st[:, 5:6], AF.Exp,
                                                                 scale=-0.5))
                    p.wait("vector", t_rs)
                    t_x2 = p.op("vector", lambda e, x1v=x1v: e.tensor_scalar(
                        xl2[:, 0:2048], x1v, st[:, 2:3], st[:, 6:7], ALU.subtract, ALU.mult))
                    p.wait("sync", t_x2)
                    p.wait("sync", t_s2)
                    tw = sl_x1w.load("sync", self.x1_s[tt_glob * 128:(tt_glob + 1) * 128, :], x1v)
                    for kb in range(4):
                        bank = kb % 2
                        if ev_tok[bank] is not None:
                            p.wait("tensor", ev_tok[bank])
                        p.wait("tensor", t_x2)
                        ttr = None
                        for kk in range(4):
                            k = kb * 4 + kk
                            ttr = p.op("tensor", lambda e, b=bank, kk=kk, k=k: e.transpose(
                                self.pt[b][:, kk * 128:(kk + 1) * 128],
                                xl2[:, k * 128:(k + 1) * 128], self.c_identb[:, :]))
                        eng = "vector" if bank == 0 else "scalar"
                        p.wait(eng, ttr)
                        dst = E_[:, :].rearrange("p (k c) -> p k c", c=512)[
                            :, kb * 4:(kb + 1) * 4, t4 * 128:(t4 + 1) * 128]
                        if eng == "vector":
                            tev = p.op("vector", lambda e, d=dst, b=bank: e.tensor_copy(
                                d, self.pt[b][:, 0:512].rearrange("p (k c) -> p k c", c=128)))
                        else:
                            tev = p.op("scalar", lambda e, d=dst, b=bank: e.activation(
                                d, self.pt[b][:, 0:512].rearrange("p (k c) -> p k c", c=128),
                                AF.Copy))
                        ev_tok[bank] = tev
                    p.wait("sync", tw)
            p.fence()
            h_tok = {}
            wu_tok = [None, None]
            wg_tok = [None, None]
            for es in range(64):
                slot = es % 2
                if es >= 2:
                    p.wait("sync", h_tok[es - 2])
                    p.wait("scalar", h_tok[es - 2])
                wu_tok[slot] = sl_wu[slot].load(
                    "sync", wup_sl(slot).rearrange("p (k c) -> p k c", k=16),
                    self.q_upT[:, es * 128:(es + 1) * 128].rearrange("(k p) c -> p k c", p=128))
                wg_tok[slot] = sl_wg[slot].load(
                    "scalar", wg_sl(slot).rearrange("p (k c) -> p k c", k=16),
                    self.q_gT[:, es * 128:(es + 1) * 128].rearrange("(k p) c -> p k c", p=128))
                bku, bkg = (0, 1) if es % 2 == 0 else (2, 3)
                p.wait("tensor", wu_tok[slot])
                p.wait("tensor", wg_tok[slot])
                if es >= 2:
                    p.wait("tensor", h_tok[es - 2])
                tmu = tmg = None
                for k in range(KS):
                    tmu = p.op("tensor", lambda e, b=bku, k=k, slot=slot: e.matmul(
                        pb[b][:, :],
                        wup_sl(slot).rearrange("p (k c) -> p k c", k=16)[:, k, :],
                        xln2T(k), start=(k == 0), stop=(k == KS - 1)))
                for k in range(KS):
                    tmg = p.op("tensor", lambda e, b=bkg, k=k, slot=slot: e.matmul(
                        pb[b][:, :],
                        wg_sl(slot).rearrange("p (k c) -> p k c", k=16)[:, k, :],
                        xln2T(k), start=(k == 0), stop=(k == KS - 1)))
                p.wait("scalar", tmu)
                p.wait("scalar", self.t_scal)
                if es >= 1:
                    p.wait("scalar", h_tok[es - 1])
                tsu = p.op("scalar", lambda e, b=bku: e.activation(
                    sup[:, 0:512], pb[b][:, :], AF.Silu, scale=self.c_scal[:, 3:4]))
                p.wait("vector", tsu)
                p.wait("vector", tmg)
                th = p.op("vector", lambda e, b=bkg, es=es: e.tensor_mul(
                    h_q(es), sup[:, 0:512], pb[b][:, :]))
                h_tok[es] = th
            p.fence(("tensor", "gpsimd", "vector", "scalar", "sync"))
            for pr in range(2):
                if self._dn_prev_evac is not None:
                    p.wait("tensor", self._dn_prev_evac)
                x1_tok = [None, None]
                for ttl in range(2):
                    t4 = pr * 2 + ttl
                    tt_glob = q * 4 + t4
                    x1_tok[ttl] = sl_x1r[ttl].load(
                        "sync", x1r[ttl], self.x1_s[tt_glob * 128:(tt_glob + 1) * 128, :])
                wd_tok = [None] * 4
                last_mm = None
                for es in range(64):
                    slot = es % 4
                    eng_w = "sync" if slot % 2 == 0 else "scalar"
                    guard = self._wd_busy[slot]
                    if guard is not None:
                        p.wait(eng_w, guard)
                    wd_tok[slot] = sl_wd[slot].load(
                        eng_w, wd_sl(slot), self.q_dT[es * 128:(es + 1) * 128, :])
                    p.wait("tensor", wd_tok[slot])
                    for ttl in range(2):
                        t4 = pr * 2 + ttl
                        for dp in range(4):
                            bank = ttl * 4 + dp
                            last_mm = p.op("tensor", lambda e, b=bank, es=es, t4=t4,
                                           slot=slot, dp=dp: e.matmul(
                                pb[b][:, :],
                                h_q(es)[:, t4 * 128:(t4 + 1) * 128],
                                wd_sl(slot)[:, dp * 512:(dp + 1) * 512],
                                start=(es == 0), stop=(es == 63)))
                    self._wd_busy[slot] = last_mm
                for ttl in range(2):
                    t4 = pr * 2 + ttl
                    tt_glob = q * 4 + t4
                    for dp in range(4):
                        bank = ttl * 4 + dp
                        sslot = dp % 2
                        p.wait("vector", last_mm)
                        p.wait("vector", x1_tok[ttl])
                        if sl_out[sslot].tok() is not None:
                            p.wait("vector", sl_out[sslot].tok())
                        tev = p.op("vector", lambda e, b=bank, ttl=ttl, dp=dp, ss=sslot:
                                   e.scalar_tensor_tensor(
                            outst[ss], pb[b][:, :], self.c_scal[:, 6:7],
                            x1r[ttl][:, dp * 512:(dp + 1) * 512], ALU.mult, ALU.add))
                        p.wait("sync", tev)
                        sl_out[sslot].load(
                            "sync", self.out[tt_glob * 128:(tt_glob + 1) * 128,
                                             dp * 512:(dp + 1) * 512], outst[sslot])
                        self._dn_prev_evac = tev


def prep_base(L0_attn_in_proj_w, L0_attn_conv_w, L0_attn_conv_b, L0_attn_dt_bias,
              L0_attn_A_log, L0_attn_D, L0_attn_norm_w, L0_attn_out_proj_w,
              L0_ffn_W_up, L0_ffn_W_gate, L0_ffn_W_down):
    rep = lambda v: np.ascontiguousarray(
        np.repeat(np.asarray(v, np.float32)[None, :], 128, 0))
    convw = np.ascontiguousarray(
        np.asarray(L0_attn_conv_w, np.float32)[:, 0, :].reshape(4, NSLAB_HBC, 128)
        .transpose(2, 1, 0).reshape(128, NSLAB_HBC * 4))
    convb = np.ascontiguousarray(
        np.asarray(L0_attn_conv_b, np.float32).reshape(NSLAB_HBC, 128).T)
    base = {
        "wp": np.asarray(L0_attn_in_proj_w, np.float32),
        "convw": convw, "convb": convb,
        "dtb128": rep(L0_attn_dt_bias), "alog128": rep(L0_attn_A_log),
        "d128": rep(L0_attn_D), "normw128": rep(L0_attn_norm_w),
        "wo": np.asarray(L0_attn_out_proj_w, np.float32),
        "wu": np.asarray(L0_ffn_W_up, np.float32),
        "wg": np.asarray(L0_ffn_W_gate, np.float32),
        "wd": np.asarray(L0_ffn_W_down, np.float32),
    }
    base["eps128"] = np.ascontiguousarray(np.repeat(
        np.array([[EPS_Q, EPS_LN, EPS_RMS, 1.0]], np.float32), 128, 0))
    base.update(make_consts())
    return base


def kernel(L0_attn_input_states, L0_attn_in_proj_w, L0_attn_conv_w,
           L0_attn_conv_b, L0_attn_dt_bias, L0_attn_A_log, L0_attn_D,
           L0_attn_norm_w, L0_attn_out_proj_w, L0_ffn_W_up, L0_ffn_W_gate,
           L0_ffn_W_down):
    kb = KB()
    nc = kb.build()
    base = prep_base(L0_attn_in_proj_w, L0_attn_conv_w, L0_attn_conv_b,
                     L0_attn_dt_bias, L0_attn_A_log, L0_attn_D, L0_attn_norm_w,
                     L0_attn_out_proj_w, L0_ffn_W_up, L0_ffn_W_gate, L0_ffn_W_down)
    x = np.asarray(L0_attn_input_states, np.float32)
    in_maps = [dict(base, x=np.ascontiguousarray(x[c])) for c in range(8)]
    res = run_bass_kernel_spmd(nc, in_maps, core_ids=list(range(8)))
    return np.stack([r["out"] for r in res.results]).astype(np.float32)


# version 08
# speedup vs baseline: 1.3455x; 1.0069x over previous
import copy
import numpy as np
import ml_dtypes
import concourse.bass as bass
import concourse.mybir as mybir
from concourse.bass_utils import run_bass_kernel_spmd

F32 = mybir.dt.float32
BF16 = mybir.dt.bfloat16
AF = mybir.ActivationFunctionType
ALU = mybir.AluOpType
AX = mybir.AxisListType

T = 2048
DM = 2048
INTER = 4096
NH, HDIM, NG, SSZ = 64, 64, 8, 128
CONV_DIM = 6144
PROJ = 10304
DE = 8192
CHK = 128
NCHUNK = 16
QT = 512
NQ = 4
EPS_LN = 1e-5
EPS_RMS = 1e-6
EPS_Q = 1e-8
KS = 16
NTT = 16
NSLAB_HBC = 48
NSLAB_HS = 32


class P:

    ENGS = ("tensor", "vector", "scalar", "gpsimd", "sync")

    def __init__(self, nc, alloc_sem):
        self.nc = nc
        self.q = {k: [] for k in self.ENGS}
        self.cnt = {}
        self.sems = {}
        self.alloc_sem = alloc_sem
        self.slots = []

    def inc(self, sem, n=1):
        self.cnt[id(sem)] = self.cnt.get(id(sem), 0) + n
        return self.cnt[id(sem)]

    def val(self, sem):
        return self.cnt.get(id(sem), 0)

    def emit(self, eng, fn):
        self.q[eng].append(fn)

    def slot(self, name):
        s = DMASlot(self, self.alloc_sem(name))
        self.slots.append(s)
        return s

    def op(self, eng, fn):
        sem = self.sems["ms_" + eng]
        tgt = self.inc(sem, 1)
        self.emit(eng, lambda e, f=fn, s=sem: f(e).then_inc(s, 1))
        return (sem, tgt)

    def raw(self, eng, fn):
        self.emit(eng, fn)

    def wait(self, eng, tok):
        if tok is None:
            return
        sem, tgt = tok
        self.emit(eng, lambda e, s=sem, t=tgt: e.wait_ge(s, t))

    def fence(self, engs=None):
        snapshot = [(s, self.val(s)) for s in self.sems.values() if self.val(s) > 0]
        snapshot += [(sl.sem, sl.total) for sl in self.slots if sl.total > 0]
        for eng in engs or self.ENGS:
            for s, v in snapshot:
                self.emit(eng, lambda e, s=s, v=v: e.wait_ge(s, v))


class DMASlot:
    def __init__(self, p, sem):
        self.p = p
        self.sem = sem
        self.total = 0

    def load(self, eng, out, in_, **kw):
        p = self.p
        if self.total > 0:
            prev = self.total
            p.emit(eng, lambda e, s=self.sem, v=prev: e.wait_ge(s, v))
        self.total += 16
        tgt = self.total
        p.emit(eng, lambda e, o=out, i=in_, k=kw, s=self.sem: e.dma_start(
            out=o, in_=i, **k).then_inc(s, 16))
        return (self.sem, tgt)

    def tok(self):
        return (self.sem, self.total) if self.total else None


def dedup_ldweights(nc):
    for f in nc.m.functions:
        for bb in f.blocks:
            il = bb.instructions
            prev = None
            out_l = []
            changed = False
            for i in il:
                if i.__class__.__name__ == "InstMatmult":
                    key = str(i.ins[1])
                    if ("float32" not in key and prev == key and i.ldweights is None
                            and i.is_transpose is not True):
                        i = copy.replace(i, ldweights=False)
                        changed = True
                    prev = key
                else:
                    if i.__class__.__name__ in ("InstMatmult", "InstLdweights"):
                        prev = None
                out_l.append(i)
            if changed:
                bb.instructions = out_l


def make_consts():
    c = {}
    c["ident_b"] = np.eye(128, dtype=ml_dtypes.bfloat16)
    c["ident_f"] = np.eye(128, dtype=np.float32)
    c["cums_pos"] = np.triu(np.ones((128, 128), np.float32))
    c["rev_incl"] = np.tril(np.ones((128, 128), np.float32))
    c["rev_strict"] = np.tril(np.ones((128, 128), np.float32), -1)
    c["trilmaskT"] = np.triu(np.ones((128, 128), np.float32)).astype(ml_dtypes.bfloat16)
    c["onesrow"] = np.ones((1, 128), np.float32)
    c["onescol"] = np.ones((128, 1), np.float32)
    return c


class KB:

    def __init__(self, debug=(), p2_lvl=9, p2_aprep=5):
        self.p2_lvl = p2_lvl
        self.p2_aprep = p2_aprep
        self.debug = set(debug)
        self.nc = bass.Bass("TRN2", target_bir_lowering=False)
        self.dbg = {}
        self._ctx = []

    def _enter(self, cm):
        v = cm.__enter__()
        self._ctx.append(cm)
        return v

    def sb(self, name, shape, dt):
        return self._enter(self.nc.sbuf_tensor(name, list(shape), dt))

    def ps(self, name, shape, dt=F32):
        return self._enter(self.nc.psum_tensor(name, list(shape), dt))

    def semaphore(self, name):
        return self._enter(self.nc.semaphore(name))

    def dbg_out(self, name, shape, dt=F32):
        if name not in self.dbg:
            self.dbg[name] = self.nc.dram_tensor(name, list(shape), dt, kind="ExternalOutput").ap()
        return self.dbg[name]

    def build(self, phases=("p0", "p1", "p2", "p3")):
        nc = self.nc
        inp = lambda n, s, d=F32: nc.dram_tensor(n, list(s), d, kind="ExternalInput").ap()
        self.x_in = inp("x", (T, DM))
        self.wp = inp("wp", (DM, PROJ))
        self.convw = inp("convw", (128, NSLAB_HBC * 4))
        self.convb = inp("convb", (128, NSLAB_HBC))
        self.dtb128 = inp("dtb128", (128, NH))
        self.alog128 = inp("alog128", (128, NH))
        self.d128 = inp("d128", (128, NH))
        self.normw128 = inp("normw128", (128, INTER))
        self.wo = inp("wo", (INTER, DM))
        self.wu_in = inp("wu", (DE, DM))
        self.wg_in = inp("wg", (DE, DM))
        self.wd_in = inp("wd", (DM, DE))
        for n in ("ident_b", "trilmaskT"):
            setattr(self, n, inp(n, (128, 128), BF16))
        for n in ("ident_f", "cums_pos", "rev_incl", "rev_strict"):
            setattr(self, n, inp(n, (128, 128)))
        self.onesrow = inp("onesrow", (1, 128))
        self.onescol = inp("onescol", (128, 1))
        self.eps128 = inp("eps128", (128, 4))

        self.out = nc.dram_tensor("out", [T, DM], F32, kind="ExternalOutput").ap()

        dram = lambda n, s, d: nc.dram_tensor(n, list(s), d, kind="Internal").ap()
        self.q_upT = dram("q_upT", (DM, DE), BF16)
        self.q_gT = dram("q_gT", (DM, DE), BF16)
        self.q_dT = dram("q_dT", (DE, DM), BF16)
        self.wo_b = dram("wo_b", (INTER, DM), BF16)
        self.hbc_s = dram("hbc_s", (NSLAB_HBC, 128, T), BF16)
        self.sgate_s = dram("sgate_s", (NTT, 128, INTER), BF16)
        self.dt_s = dram("dt_s", (NTT, 128, NH), F32)
        self.ynT_s = dram("ynT_s", (NCHUNK, NSLAB_HS, 128, 128), BF16)
        self.aflat_s = dram("aflat_s", (NCHUNK, 1, NH * CHK), F32)
        self.x1_s = dram("x1_s", (T, DM), F32)

        sb = self.sb
        self.c_identb = sb("c_identb", (128, 128), BF16)
        self.c_identf = sb("c_identf", (128, 128), F32)
        self.c_cums = sb("c_cums", (128, 128), F32)
        self.c_revi = sb("c_revi", (128, 128), F32)
        self.c_revs = sb("c_revs", (128, 128), F32)
        self.c_maskT = sb("c_maskT", (128, 128), BF16)
        self.c_ones = sb("c_ones", (1, 128), F32)
        self.c_onesc = sb("c_onesc", (128, 1), F32)
        self.c_convw = sb("c_convw", (128, NSLAB_HBC * 4), F32)
        self.c_convb = sb("c_convb", (128, NSLAB_HBC), F32)
        self.c_dtb = sb("c_dtb", (128, NH), F32)
        self.c_alog = sb("c_alog", (128, NH), F32)
        self.c_expa = sb("c_expa", (128, NH), F32)
        self.c_d128 = sb("c_d128", (128, NH), F32)
        self.c_normw = sb("c_normw", (128, INTER), F32)
        self.c_scal = sb("c_scal", (128, 16), F32)
        self.c_eps = sb("c_eps", (128, 4), F32)
        self.bufA = sb("bufA", (128, 32768), BF16)
        self.bufB = sb("bufB", (128, 5120), F32)
        self.bufC = sb("bufC", (128, 16384), BF16)
        self.bufD = sb("bufD", (128, 7168), F32)
        self.bufE = sb("bufE", (128, 8192), BF16)
        self.bufS = sb("bufS", (128, 1024), F32)
        self.aflat = sb("aflat", (1, 4096), F32)
        self.pb = [self.ps(f"pb{i}", (128, 512)) for i in range(8)]
        self.pt = [self.pb[6][:, :].bitcast(BF16), self.pb[7][:, :].bitcast(BF16)]

        p = P(nc, self.semaphore)
        self.p = p
        for eng in P.ENGS:
            p.sems["ms_" + eng] = self.semaphore("sm_" + eng)

        with self.nc.Block() as blk:
            self._load_consts()
            if "p0" in phases:
                self.p0_quant()
            p.fence()
            if "p1" in phases:
                self.p1_ln1_inproj()
            p.fence()
            if "p2" in phases:
                self.p2_mamba()
            p.fence()
            if "p3" in phases:
                self.p3_ffn()
            p.fence()
            self._final_wait()

            for eng in P.ENGS:
                getattr(blk, eng)(self._runner(eng))

        for cm in reversed(self._ctx):
            cm.__exit__(None, None, None)
        dedup_ldweights(nc)
        return nc

    def _runner(self, eng):
        ops = self.p.q[eng]
        def run(e):
            for f in ops:
                f(e)
        return run

    def _final_wait(self):
        p = self.p
        for s in p.sems.values():
            v = p.val(s)
            if v > 0:
                p.raw("sync", lambda e, s=s, v=v: e.wait_ge(s, v))
        for sl in p.slots:
            if sl.total > 0:
                p.raw("sync", lambda e, s=sl.sem, v=sl.total: e.wait_ge(s, v))

    def _load_consts(self):
        p = self.p
        sl = p.slot("sl_const")
        pairs = [(self.ident_b, self.c_identb), (self.ident_f, self.c_identf),
                 (self.cums_pos, self.c_cums), (self.rev_incl, self.c_revi),
                 (self.rev_strict, self.c_revs), (self.trilmaskT, self.c_maskT),
                 (self.onesrow, self.c_ones), (self.onescol, self.c_onesc),
                 (self.convw, self.c_convw), (self.convb, self.c_convb),
                 (self.dtb128, self.c_dtb), (self.alog128, self.c_alog),
                 (self.d128, self.c_d128), (self.normw128, self.c_normw),
                 (self.eps128, self.c_eps)]
        tok = None
        for srcT, dst in pairs:
            tok = sl.load("sync", dst[:, :], srcT)
        p.wait("scalar", tok)
        self.t_expa = p.op("scalar", lambda e: e.activation(self.c_expa[:, :], self.c_alog[:, :], AF.Exp))

    def p0_quant(self):
        p = self.p
        stage = self.bufD
        red = self.bufS
        mats = [(self.wu_in, DE, DM, 0, self.q_upT), (self.wg_in, DE, DM, 1, self.q_gT),
                (self.wd_in, DM, DE, 2, self.q_dT)]
        st_sl = [p.slot(f"p0st{i}") for i in range(3)]

        toks = []
        nld = 0
        for wsrc, R, C, mi, _ in mats:
            npc = C // 2048
            for j in range(R // 128):
                for pc in range(npc):
                    idx = j * npc + pc
                    slot = nld % 3
                    ss = stage[:, slot * 2048: slot * 2048 + 2048]
                    if nld >= 3:
                        p.wait("sync", toks[-3])
                    tl = st_sl[slot].load("sync", ss, wsrc[j * 128:(j + 1) * 128,
                                                           pc * 2048:(pc + 1) * 2048])
                    p.wait("vector", tl)
                    col = red[:, mi * 300 + idx: mi * 300 + idx + 1]
                    tv = p.op("vector", lambda e, c=col, s=ss: e.tensor_reduce(
                        c, s, axis=AX.X, op=ALU.add, apply_absolute_value=True))
                    toks.append(tv)
                    nld += 1
        tf = None
        p.wait("vector", toks[-1])
        for _, R, C, mi, _ in mats:
            tf = p.op("vector", lambda e, mi=mi: e.tensor_reduce(
                red[:, 900 + mi: 900 + mi + 1], red[:, mi * 300: mi * 300 + 64],
                axis=AX.X, op=ALU.add))
        p.wait("tensor", tf)
        tm = None
        for mi in range(3):
            tm = p.op("tensor", lambda e, mi=mi: e.matmul(
                self.pb[0][0:1, mi:mi + 1], red[:, 900 + mi: 900 + mi + 1],
                self.c_onesc[:, :], start=True, stop=True))
        svec = self.bufS[0:1, 960:976]
        t0 = p.op("vector", lambda e: e.memset(svec[0:1, 0:16], 0.0))
        p.wait("scalar", t0)
        p.wait("scalar", tm)
        t1 = p.op("scalar", lambda e: e.activation(svec[0:1, 0:3], self.pb[0][0:1, 0:3],
                                                   AF.Copy, scale=1.0 / (DE * DM)))
        p.wait("scalar", t1)
        t2 = p.op("scalar", lambda e: e.activation(svec[0:1, 0:3], svec[0:1, 0:3],
                                                   AF.Identity, bias=self.c_eps[0:1, 0:1]))
        p.wait("scalar", t2)
        t3 = p.op("scalar", lambda e: e.activation(svec[0:1, 3:6], svec[0:1, 0:3],
                                                   AF.Copy, scale=0.5))
        p.wait("scalar", t3)
        t3b = p.op("scalar", lambda e: e.activation(svec[0:1, 10:13], svec[0:1, 0:3],
                                                    AF.Copy, scale=-0.5))
        p.wait("vector", t3b)
        t4 = p.op("vector", lambda e: e.tensor_mul(svec[0:1, 6:7], svec[0:1, 4:5], svec[0:1, 5:6]))
        p.wait("tensor", t4)
        p.wait("tensor", t3b)
        tm = p.op("tensor", lambda e: e.matmul(self.pb[1][:, 0:16], self.c_ones[:, :],
                                               svec[0:1, 0:16], start=True, stop=True))
        p.wait("scalar", tm)
        self.t_scal = p.op("scalar", lambda e: e.activation(self.c_scal[:, 0:16],
                                                            self.pb[1][:, 0:16], AF.Copy))

        p.fence()
        sgn = self.bufC[:, 0:2048]
        m01 = self.bufE[:, 0:2048]
        tern = self.bufA
        trn = self.bufC[:, 8192:16384]
        wr_sl = [p.slot(f"p0wr{i}") for i in range(4)]
        ev_toks = [None, None]
        wr_toks = {}
        last_tr = [None]
        prev_grp = [None]
        for wsrc, R, C, mi, qdst in mats:
            npc = C // 2048
            for g in range(R // 512):
                for pc in range(npc):
                    tt_toks = []
                    sg_mk = []
                    if prev_grp[0] is not None:
                        for tk in prev_grp[0]:
                            p.wait("sync", tk)
                    for jj in range(4):
                        j = g * 4 + jj
                        slot = jj % 3
                        ss = stage[:, slot * 2048: slot * 2048 + 2048]
                        if jj >= 3:
                            p.wait("sync", tt_toks[jj - 3])
                            p.wait("sync", sg_mk[jj - 3][0])
                            p.wait("sync", sg_mk[jj - 3][1])
                        tl = st_sl[slot].load("sync", ss, wsrc[j * 128:(j + 1) * 128,
                                                               pc * 2048:(pc + 1) * 2048])
                        p.wait("scalar", tl)
                        p.wait("scalar", self.t_scal)
                        if tt_toks:
                            p.wait("scalar", tt_toks[-1])
                        tsg = p.op("scalar", lambda e, s=ss, mi=mi: e.sign(
                            sgn[:, :], s, bias=self.c_scal[:, 10 + mi:11 + mi]))
                        p.wait("scalar", tsg)
                        tmk = p.op("scalar", lambda e, s=ss, mi=mi: e.sign(
                            m01[:, :], s, bias=self.c_scal[:, 3 + mi:4 + mi]))
                        p.wait("gpsimd", tsg)
                        p.wait("gpsimd", tmk)
                        p.wait("gpsimd", last_tr[0])
                        ttn = p.op("gpsimd", lambda e, d=tern[:, jj * 2048:(jj + 1) * 2048]:
                                   e.tensor_add(d, sgn[:, :], m01[:, :]))
                        tt_toks.append(ttn)
                        sg_mk.append((tsg, tmk))
                    prev_grp[0] = [tt_toks[-1], sg_mk[-1][0], sg_mk[-1][1]]
                    p.wait("tensor", tt_toks[-1])
                    for c in range(16):
                        bank = c % 2
                        if ev_toks[bank] is not None:
                            p.wait("tensor", ev_toks[bank])
                        ttr = None
                        for jj in range(4):
                            ttr = p.op("tensor", lambda e, o=self.pt[bank][:, jj * 128:(jj + 1) * 128],
                                       s=tern[:, jj * 2048 + c * 128: jj * 2048 + c * 128 + 128]:
                                       e.transpose(o, s, self.c_identb[:, :]))
                        last_tr[0] = ttr
                        eng = "vector" if bank == 0 else "scalar"
                        p.wait(eng, ttr)
                        if (mi, c) in wr_toks:
                            p.wait(eng, wr_toks.pop((mi, c)))
                        if eng == "vector":
                            tev = p.op("vector", lambda e, o=trn[:, c * 512:(c + 1) * 512], b=bank:
                                       e.tensor_copy(o, self.pt[b][:, 0:512]))
                        else:
                            tev = p.op("scalar", lambda e, o=trn[:, c * 512:(c + 1) * 512], b=bank:
                                       e.activation(o, self.pt[b][:, 0:512], AF.Copy))
                        ev_toks[bank] = tev
                        p.wait("scalar", tev)
                        wr_toks[(mi, c)] = wr_sl[c % 4].load(
                            "scalar", qdst[pc * 2048 + c * 128: pc * 2048 + (c + 1) * 128,
                                           g * 512:(g + 1) * 512],
                            trn[:, c * 512:(c + 1) * 512])

        p.fence()
        wo_c = p.slot("p0wo_c")
        wo_w = p.slot("p0wo_w")
        for cblk in range(4):
            dst = self.bufC[:, 0:8 * 2048].rearrange("p (e d) -> p e d", e=8)
            srcb = self.wo[cblk * 1024:(cblk + 1) * 1024, :].rearrange("(e p) d -> p e d", p=128)
            tl = wo_c.load("gpsimd", dst, srcb)
            p.wait("sync", tl)
            tw = wo_w.load("sync", self.wo_b[cblk * 1024:(cblk + 1) * 1024, :]
                           .rearrange("(e p) d -> p e d", p=128), dst)
            p.wait("gpsimd", tw)

    def p1_ln1_inproj(self):
        p = self.p
        xlnT = lambda k: self.bufA[:, k * 2048:(k + 1) * 2048]
        xst = [self.bufB[:, 0:2048], self.bufB[:, 2048:4096]]
        sq = self.bufD[:, 0:2048]
        st = self.bufS
        xln = self.bufE[:, 0:2048]
        x_sl = [p.slot("p1x0"), p.slot("p1x1")]

        xload = {}
        xln_done = {}
        def loadx(tt):
            if tt >= NTT:
                return
            if tt - 2 >= 0:
                p.wait("sync", xln_done[tt - 2])
            xload[tt] = x_sl[tt % 2].load("sync", xst[tt % 2],
                                          self.x_in[tt * 128:(tt + 1) * 128, :])
        loadx(0); loadx(1)
        ev_tok = [None, None]
        for tt in range(NTT):
            xs = xst[tt % 2]
            p.wait("vector", xload[tt])
            t_s1 = p.op("vector", lambda e, x=xs: e.reduce_sum(st[:, 0:1], x, axis=AX.X))
            p.wait("scalar", xload[tt])
            t_s2 = p.op("scalar", lambda e, x=xs: e.activation(sq, x, AF.Square,
                                                               accum_out=st[:, 1:2]))
            p.wait("scalar", t_s1)
            t_m = p.op("scalar", lambda e: e.activation(st[:, 2:3], st[:, 0:1],
                                                        AF.Copy, scale=1.0 / 2048.0))
            p.wait("scalar", t_m)
            t_m2 = p.op("scalar", lambda e: e.activation(st[:, 3:4], st[:, 2:3], AF.Square))
            p.wait("vector", t_s2)
            p.wait("vector", t_m2)
            t_var = p.op("vector", lambda e: e.tensor_scalar(
                st[:, 4:5], st[:, 1:2], 1.0 / 2048.0, None, ALU.mult))
            p.wait("vector", t_var)
            t_var = p.op("vector", lambda e: e.tensor_sub(st[:, 4:5], st[:, 4:5], st[:, 3:4]))
            p.wait("scalar", t_var)
            t_ln = p.op("scalar", lambda e: e.activation(st[:, 5:6], st[:, 4:5], AF.Ln,
                                                         bias=self.c_eps[:, 1:2]))
            p.wait("scalar", t_ln)
            t_rs = p.op("scalar", lambda e: e.activation(st[:, 6:7], st[:, 5:6], AF.Exp,
                                                         scale=-0.5))
            p.wait("vector", t_rs)
            t_xln = p.op("vector", lambda e, x=xs: e.tensor_scalar(
                xln, x, st[:, 2:3], st[:, 6:7], ALU.subtract, ALU.mult))
            for kb in range(4):
                bank = kb % 2
                if ev_tok[bank] is not None:
                    p.wait("tensor", ev_tok[bank])
                p.wait("tensor", t_xln)
                ttr = None
                for kk in range(4):
                    k = kb * 4 + kk
                    ttr = p.op("tensor", lambda e, o=self.pt[bank][:, kk * 128:(kk + 1) * 128],
                               s=xln[:, k * 128:(k + 1) * 128]:
                               e.transpose(o, s, self.c_identb[:, :]))
                eng = "vector" if bank == 0 else "scalar"
                p.wait(eng, ttr)
                dst = self.bufA[:, :].rearrange("p (k c) -> p k c", c=2048)[
                    :, kb * 4:(kb + 1) * 4, tt * 128:(tt + 1) * 128]
                if eng == "vector":
                    tev = p.op("vector", lambda e, d=dst, b=bank: e.tensor_copy(
                        d, self.pt[b][:, 0:512].rearrange("p (k c) -> p k c", c=128)))
                else:
                    tev = p.op("scalar", lambda e, d=dst, b=bank: e.activation(
                        d, self.pt[b][:, 0:512].rearrange("p (k c) -> p k c", c=128), AF.Copy))
                ev_tok[bank] = tev
            xln_done[tt] = t_xln
            loadx(tt + 2)
        p.fence()

        blk_slot = [self.bufC[:, 0:8192], self.bufC[:, 8192:16384]]
        blk_sl = [p.slot("p1w0"), p.slot("p1w1")]
        sg_stage = [self.bufE[:, 2048 + i * 512: 2048 + (i + 1) * 512] for i in range(4)]
        sg_sl = [p.slot(f"p1sg{i}") for i in range(4)]
        sg_wr = [None] * 4
        ps_ev = {}
        slot_busy = [None, None]

        def wblock(gp, cols, width=512):
            slot = blk_slot[gp % 2]
            if slot_busy[gp % 2] is not None:
                p.wait("gpsimd", slot_busy[gp % 2])
            dst = slot.rearrange("p (k c) -> p k c", k=16)[:, :, 0:width]
            srcb = self.wp[:, cols:cols + width].rearrange("(k p) c -> p k c", p=128)
            return blk_sl[gp % 2].load("gpsimd", dst, srcb), slot

        for gp in range(8):
            tok, slot = wblock(gp, gp * 512)
            for tt in range(NTT):
                bank = (gp * NTT + tt) % 4
                if ps_ev.get(bank) is not None:
                    p.wait("tensor", ps_ev[bank])
                p.wait("tensor", tok)
                tm = None
                for k in range(KS):
                    tm = p.op("tensor", lambda e, b=bank, k=k, tt=tt, slot=slot:
                              e.matmul(self.pb[b][:, :],
                                       xlnT(k)[:, tt * 128:(tt + 1) * 128],
                                       slot.rearrange("p (k c) -> p k c", k=16)[:, k, :],
                                       start=(k == 0), stop=(k == KS - 1)))
                sgs = sg_stage[bank]
                p.wait("scalar", tm)
                if sg_wr[bank] is not None:
                    p.wait("scalar", sg_wr[bank])
                tev = p.op("scalar", lambda e, d=sgs, b=bank:
                           e.activation(d, self.pb[b][:, :], AF.Silu))
                ps_ev[bank] = tev
                p.wait("scalar", tev)
                sg_wr[bank] = sg_sl[bank].load(
                    "scalar", self.sgate_s[tt][:, gp * 512:(gp + 1) * 512], sgs)
                slot_busy[gp % 2] = tm
        dtok, dslot = wblock(0, PROJ - 64, width=64)
        dt_stage = self.bufD[:, 2048:2112]
        dt_sl = p.slot("p1dt")
        last_wr = None
        for tt in range(NTT):
            p.wait("tensor", dtok)
            if ps_ev.get(4) is not None:
                p.wait("tensor", ps_ev[4])
            tm = None
            for k in range(KS):
                tm = p.op("tensor", lambda e, k=k, tt=tt:
                          e.matmul(self.pb[4][:, 0:64],
                                   xlnT(k)[:, tt * 128:(tt + 1) * 128],
                                   dslot.rearrange("p (k c) -> p k c", k=16)[:, k, 0:64],
                                   start=(k == 0), stop=(k == KS - 1)))
            p.wait("scalar", tm)
            if last_wr is not None:
                p.wait("scalar", last_wr)
            tev = p.op("scalar", lambda e: e.activation(dt_stage, self.pb[4][:, 0:64], AF.Copy))
            ps_ev[4] = tev
            p.wait("scalar", tev)
            last_wr = dt_sl.load("scalar", self.dt_s[tt], dt_stage)
            slot_busy[0] = tm

        p.fence(("tensor",))
        hb_stage = [self.bufE[:, 4096 + i * 512: 4096 + (i + 1) * 512] for i in range(4)]
        hb_sl = [p.slot(f"p1hb{i}") for i in range(4)]
        hb_wr = [None] * 4
        ps_ev = {}
        for cb in range(12):
            tok, slot = wblock(cb, INTER + cb * 512)
            for ct in range(4):
                for tp in range(4):
                    bank = (ct * 4 + tp) % 4
                    if ps_ev.get(bank) is not None:
                        p.wait("tensor", ps_ev[bank])
                    p.wait("tensor", tok)
                    tm = None
                    for k in range(KS):
                        tm = p.op("tensor", lambda e, b=bank, k=k, ct=ct, tp=tp, slot=slot:
                                  e.matmul(self.pb[b][:, :],
                                           slot.rearrange("p (k c) -> p k c", k=16)[:, k, ct * 128:(ct + 1) * 128],
                                           xlnT(k)[:, tp * 512:(tp + 1) * 512],
                                           start=(k == 0), stop=(k == KS - 1)))
                    eng = "vector" if bank % 2 == 0 else "scalar"
                    hbs = hb_stage[bank]
                    p.wait(eng, tm)
                    if hb_wr[bank] is not None:
                        p.wait(eng, hb_wr[bank])
                    if eng == "vector":
                        tev = p.op("vector", lambda e, d=hbs, b=bank: e.tensor_copy(d, self.pb[b][:, :]))
                    else:
                        tev = p.op("scalar", lambda e, d=hbs, b=bank: e.activation(d, self.pb[b][:, :], AF.Copy))
                    ps_ev[bank] = tev
                    wr_eng = "scalar" if eng == "scalar" else "sync"
                    p.wait(wr_eng, tev)
                    hb_wr[bank] = hb_sl[bank].load(
                        wr_eng, self.hbc_s[cb * 4 + ct][:, tp * 512:(tp + 1) * 512], hbs)
                    slot_busy[cb % 2] = tm

    def p2_mamba(self):
        p = self.p
        A = self.bufA
        C_ = self.bufC
        D_ = self.bufD
        E_ = self.bufE
        pb = self.pb
        hs_fm = lambda s: A[:, s * 512:(s + 1) * 512]
        hs_tm = A[:, 16384:20480]
        hs_dt = A[:, 20480:24576]
        mT = A[:, 24576:32768]
        yn_b = A[:, 24576:28672]
        conv_in = [C_[:, 0:520], C_[:, 520:1040]]
        B_fm = lambda g: C_[:, 2048 + g * 512: 2048 + (g + 1) * 512]
        Cc_fm = lambda g: C_[:, 6144 + g * 512: 6144 + (g + 1) * 512]
        sgate_c = C_[:, 10240:14336]
        ynT_st = C_[:, 14336:16384]
        B_tm = E_[:, 0:1024]
        Sst_b = E_[:, 1024:5120]
        SmT = E_[:, 5120:6144]
        Ereg = E_[:, 6144:7168]
        y = self.bufB[:, 0:4096]
        Sst = D_[:, 0:4096]
        dtv = {"raw": D_[:, 4096:4160], "u": D_[:, 4160:4224], "au": D_[:, 4224:4288],
               "e1": D_[:, 4288:4352], "l1": D_[:, 4352:4416], "mx": D_[:, 4416:4480],
               "sp": D_[:, 4480:4544], "abt": D_[:, 4544:4608], "acn": D_[:, 4608:4672],
               "eac": D_[:, 4672:4736], "dst": D_[:, 4736:4800], "eal": D_[:, 4800:4864],
               "dtD": D_[:, 4864:4928], "nacn": D_[:, 5056:5120]}
        acumT = D_[0:64, 4928:5056]
        yo_tmp = D_[:, 5120:5632]
        su_tmp = D_[:, 5632:6144]
        cva = D_[:, 6144:6656]
        cvb = D_[:, 6656:7168]
        ealr = self.bufS[0:1, 100:164]

        sl_hbc = [p.slot("p2hb0"), p.slot("p2hb1")]
        sl_dt = p.slot("p2dt")
        sl_sg = p.slot("p2sg")
        sl_af_w = p.slot("p2afw")
        sl_af_r = [p.slot(f"p2afr{i}") for i in range(4)]
        sl_ynw = [p.slot("p2yw0"), p.slot("p2yw1")]

        t0 = p.op("vector", lambda e: e.memset(Sst, 0.0))
        t0b = p.op("vector", lambda e: e.memset(Sst_b, 0.0))
        tok_state_b = t0b
        tok_state_mm = None
        conv_busy = {}
        ev_tok = [None, None]
        yn_wr = [None, None]
        prev_tYN = None
        prev_tYF = None
        prev_yd = None
        prev_chunk_tensor = None

        for q in range(NQ):
            q0 = q * QT
            for s in range(48):
                ci_sl = conv_in[s % 2]
                if conv_busy.get(("tap", s - 2)) is not None:
                    p.wait("sync", conv_busy[("tap", s - 2)])
                lo = max(q0 - 1, 0)
                hi = min(q0 + QT + 2, T)
                ncols = hi - lo
                off = 1 if q0 == 0 else 0
                tmemz = None
                if off > 0 or ncols + off < 515:
                    if conv_busy.get(("tap", s - 2)) is not None:
                        p.wait("gpsimd", conv_busy[("tap", s - 2)])
                    tmemz = p.op("gpsimd", lambda e, c=ci_sl: e.memset(c[:, 0:515], 0.0))
                    p.wait("sync", tmemz)
                tl = sl_hbc[s % 2].load("sync", ci_sl[:, off:off + ncols],
                                        self.hbc_s[s][:, lo:hi])
                p.wait("vector", tl)
                if tmemz is not None:
                    p.wait("vector", tmemz)
                if conv_busy.get(("scr", s - 1)) is not None:
                    p.wait("vector", conv_busy[("scr", s - 1)])
                if prev_chunk_tensor is not None and s < 2:
                    p.wait("vector", prev_chunk_tensor)
                    p.wait("gpsimd", prev_chunk_tensor)
                    p.wait("scalar", prev_chunk_tensor)
                tc0 = p.op("vector", lambda e, c=ci_sl, s=s: e.tensor_scalar(
                    cva, c[:, 0:512], self.c_convw[:, s * 4: s * 4 + 1], None, ALU.mult))
                p.wait("vector", tc0)
                tc1 = p.op("vector", lambda e, c=ci_sl, s=s: e.scalar_tensor_tensor(
                    cvb, c[:, 1:513], self.c_convw[:, s * 4 + 1: s * 4 + 2], cva,
                    ALU.mult, ALU.add))
                p.wait("vector", tc1)
                tc2 = p.op("vector", lambda e, c=ci_sl, s=s: e.scalar_tensor_tensor(
                    cva, c[:, 2:514], self.c_convw[:, s * 4 + 2: s * 4 + 3], cvb,
                    ALU.mult, ALU.add))
                p.wait("vector", tc2)
                tc3 = p.op("vector", lambda e, c=ci_sl, s=s: e.scalar_tensor_tensor(
                    cvb, c[:, 3:515], self.c_convw[:, s * 4 + 3: s * 4 + 4], cva,
                    ALU.mult, ALU.add))
                conv_busy[("tap", s)] = tc3
                if s < 32:
                    dst = hs_fm(s)
                elif s < 40:
                    dst = B_fm(s - 32)
                else:
                    dst = Cc_fm(s - 40)
                p.wait("scalar", tc3)
                tsl = p.op("scalar", lambda e, d=dst, s=s: e.activation(
                    d, cvb, AF.Silu, bias=self.c_convb[:, s:s + 1]))
                conv_busy[("scr", s)] = tsl
                conv_busy[("silu", s)] = tsl
            t_conv_all = conv_busy[("silu", 47)]
            p.wait("tensor", t_conv_all)
            if "p2b" in self.debug:
                dB = self.dbg_out("d_B", (NQ, 128, 4096), BF16)
                dC = self.dbg_out("d_C", (NQ, 128, 4096), BF16)
                self._dbgsl5 = getattr(self, "_dbgsl5", None) or p.slot("dbgb")
                self._dbgsl6 = getattr(self, "_dbgsl6", None) or p.slot("dbgc")
                p.wait("sync", t_conv_all)
                tb1 = self._dbgsl5.load("sync", dB[q], C_[:, 2048:6144])
                tb2 = self._dbgsl6.load("sync", dC[q], C_[:, 6144:10240])
                p.wait("vector", tb1)
                p.wait("vector", tb2)
                p.wait("scalar", tb1)
                p.wait("scalar", tb2)

            for cc in range(4):
                if self.p2_lvl < 2:
                    p.fence()
                    continue
                ci = q * 4 + cc
                csl = slice(cc * 128, (cc + 1) * 128)

                if prev_yd is not None:
                    p.wait("sync", prev_yd)
                tok_dt_l = sl_dt.load("sync", dtv["raw"], self.dt_s[ci])
                p.wait("vector", tok_dt_l)
                tU = p.op("vector", lambda e: e.tensor_add(dtv["u"], dtv["raw"], self.c_dtb[:, :]))
                p.wait("scalar", tU)
                tAU = p.op("scalar", lambda e: e.activation(dtv["au"], dtv["u"], AF.Abs))
                p.wait("scalar", tAU)
                tE1 = p.op("scalar", lambda e: e.activation(dtv["e1"], dtv["au"], AF.Exp, scale=-1.0))
                p.wait("scalar", tE1)
                tL1 = p.op("scalar", lambda e: e.activation(dtv["l1"], dtv["e1"], AF.Ln,
                                                            bias=self.c_eps[:, 3:4]))
                tMX = p.op("scalar", lambda e: e.activation(dtv["mx"], dtv["u"], AF.Relu))
                p.wait("vector", tL1)
                p.wait("vector", tMX)
                tSP = p.op("vector", lambda e: e.tensor_add(dtv["sp"], dtv["mx"], dtv["l1"]))
                p.wait("vector", tSP)
                p.wait("vector", self.t_expa)
                tABT = p.op("vector", lambda e: e.tensor_mul(dtv["abt"], dtv["sp"], self.c_expa[:, :]))
                if self.p2_aprep < 1:
                    p.fence()
                    prev_yd = prev_chunk_tensor = tABT
                    continue
                p.wait("tensor", tABT)
                tM1 = p.op("tensor", lambda e: e.matmul(pb[5][:, 0:64], self.c_cums[:, :],
                                                        dtv["abt"], start=True, stop=True))
                tM2 = p.op("tensor", lambda e: e.matmul(pb[3][:, 0:64], self.c_revs[:, :],
                                                        dtv["abt"], start=True, stop=True))
                tM3 = p.op("tensor", lambda e: e.matmul(pb[2][:, 0:64], self.c_revi[:, :],
                                                        dtv["abt"], start=True, stop=True))
                p.wait("scalar", tM1)
                tACN = p.op("scalar", lambda e: e.activation(dtv["acn"], pb[5][:, 0:64], AF.Copy))
                tNAC = p.op("scalar", lambda e: e.activation(dtv["nacn"], pb[5][:, 0:64],
                                                             AF.Copy, scale=-1.0))
                tEAC = p.op("scalar", lambda e: e.activation(dtv["eac"], pb[5][:, 0:64],
                                                             AF.Exp, scale=-1.0))
                p.wait("scalar", tM2)
                tDST = p.op("scalar", lambda e: e.activation(dtv["dst"], pb[3][:, 0:64],
                                                             AF.Exp, scale=-1.0))
                if self.p2_aprep < 2:
                    p.fence()
                    prev_yd = prev_chunk_tensor = tDST
                    continue
                p.wait("scalar", tM3)
                tEALR = p.op("scalar", lambda e: e.activation(ealr, pb[2][0:1, 0:64],
                                                              AF.Exp, scale=-1.0))
                p.wait("tensor", tEALR)
                p.wait("tensor", tDST)
                tM4 = p.op("tensor", lambda e: e.matmul(pb[3][:, 0:64], self.c_ones[:, :],
                                                        ealr, start=True, stop=True))
                p.wait("scalar", tM4)
                tEAL = p.op("scalar", lambda e: e.activation(dtv["eal"], pb[3][:, 0:64], AF.Copy))
                if self.p2_aprep < 3:
                    p.fence()
                    prev_yd = prev_chunk_tensor = tEAL
                    continue
                p.wait("tensor", tACN)
                tTRA = p.op("tensor", lambda e: e.transpose(pb[4][0:64, 0:128], dtv["acn"],
                                                            self.c_identf[:, :]))
                p.wait("scalar", tTRA)
                p.wait("scalar", sl_af_w.tok())
                tACT2 = p.op("scalar", lambda e: e.activation(acumT, pb[4][0:64, 0:128], AF.Copy))
                if self.p2_aprep < 4:
                    p.fence()
                    prev_yd = prev_chunk_tensor = tACT2
                    continue
                p.wait("scalar", tACT2)
                tAFW = sl_af_w.load("scalar", self.aflat_s[ci][0, :]
                                    .rearrange("(h t) -> h t", h=64), acumT)

                if self.p2_lvl < 3:
                    p.fence()
                    prev_yd = prev_chunk_tensor = tACT2
                    continue
                if prev_tYN is not None:
                    p.wait("tensor", prev_tYN)
                for grp in range(8):
                    bank = grp % 2
                    if ev_tok[bank] is not None:
                        p.wait("tensor", ev_tok[bank])
                    ttr = None
                    for kk in range(4):
                        ttr = p.op("tensor", lambda e, o=self.pt[bank][:, kk * 128:(kk + 1) * 128],
                                   src=hs_fm(grp * 4 + kk)[:, csl]:
                                   e.transpose(o, src, self.c_identb[:, :]))
                    eng = "vector" if bank == 0 else "scalar"
                    p.wait(eng, ttr)
                    if eng == "vector":
                        tev = p.op("vector", lambda e, g=grp, b=bank: e.tensor_copy(
                            hs_tm[:, g * 512:(g + 1) * 512], self.pt[b][:, 0:512]))
                    else:
                        tev = p.op("scalar", lambda e, g=grp, b=bank: e.activation(
                            hs_tm[:, g * 512:(g + 1) * 512], self.pt[b][:, 0:512], AF.Copy))
                    ev_tok[bank] = tev
                tok_hstm = ev_tok[:]
                for gb in range(2):
                    bank = gb
                    if ev_tok[bank] is not None:
                        p.wait("tensor", ev_tok[bank])
                    ttr = None
                    for kk in range(4):
                        ttr = p.op("tensor", lambda e, o=self.pt[bank][:, kk * 128:(kk + 1) * 128],
                                   src=B_fm(gb * 4 + kk)[:, csl]:
                                   e.transpose(o, src, self.c_identb[:, :]))
                    eng = "vector" if bank == 0 else "scalar"
                    p.wait(eng, ttr)
                    if eng == "vector":
                        tev = p.op("vector", lambda e, g=gb, b=bank: e.tensor_copy(
                            B_tm[:, g * 512:(g + 1) * 512], self.pt[b][:, 0:512]))
                    else:
                        tev = p.op("scalar", lambda e, g=gb, b=bank: e.activation(
                            B_tm[:, g * 512:(g + 1) * 512], self.pt[b][:, 0:512], AF.Copy))
                    ev_tok[bank] = tev
                tok_btm = ev_tok[:]

                if self.p2_lvl < 4:
                    p.fence()
                    prev_yd = prev_chunk_tensor = ev_tok[1]
                    continue
                tok_smt = []
                for half in range(2):
                    bank = half
                    if getattr(self, "_prev_yo", None) is not None:
                        p.wait("tensor", self._prev_yo[6 + half])
                    tm = None
                    for gg in range(4):
                        g = half * 4 + gg
                        tm = p.op("tensor", lambda e, b=bank, gg=gg, g=g, csl=csl: e.matmul(
                            pb[b][:, gg * 128:(gg + 1) * 128],
                            B_fm(g)[:, csl], Cc_fm(g)[:, csl], start=True, stop=True))
                    p.wait("vector", tm)
                    if getattr(self, "_dbg_s", None) is not None:
                        p.wait("vector", self._dbg_s)
                    tmk = p.op("vector", lambda e, b=bank, h=half: e.tensor_tensor(
                        SmT[:, h * 512:(h + 1) * 512].rearrange("p (g l) -> p g l", g=4),
                        pb[b][:, :].rearrange("p (g l) -> p g l", g=4),
                        self.c_maskT[:, :].unsqueeze(1).broadcast_to([128, 4, 128])
                            .transpose([0, 1, 2]),
                        ALU.mult))
                    tok_smt.append((tmk, tm))

                if self.p2_lvl < 5:
                    p.fence()
                    prev_yd = prev_chunk_tensor = tok_smt[1][0]
                    continue
                tok_mT = {}
                prev_ex = None
                p.wait("sync", tAFW)
                for r in range(16):
                    hbase = r * 4
                    afslot = r % 4
                    afr = self.aflat[0:1, afslot * 1024: afslot * 1024 + 512]
                    if r >= 4:
                        p.wait("sync", tok_mT[r - 4])
                    elif getattr(self, "_prev_mT", None) is not None:
                        p.wait("sync", self._prev_mT[12 + r])
                    tAF = sl_af_r[afslot].load(
                        "sync", afr, self.aflat_s[ci][0:1, hbase * 128:(hbase + 4) * 128])
                    bank = 2 + (r % 2)
                    p.wait("tensor", tAF)
                    if r >= 2:
                        p.wait("tensor", tok_mT[r - 2])
                    else:
                        p.wait("tensor", tDST)
                        p.wait("tensor", tEALR)
                        p.wait("tensor", tEAL)
                        if prev_yd is not None:
                            p.wait("tensor", prev_yd)
                    tBC = p.op("tensor", lambda e, b=bank, afr=afr: e.matmul(
                        pb[b][:, 0:512], self.c_ones[:, :], afr,
                        start=True, stop=True))
                    tEx = None
                    p.wait("scalar", tBC)
                    p.wait("scalar", tNAC)
                    if prev_ex is not None:
                        p.wait("scalar", prev_ex)
                        p.wait("scalar", tok_mT[r - 1])
                    if r < 2 and getattr(self, "_dbg_tok", None) is not None:
                        p.wait("vector", self._dbg_tok)
                    for hh in range(4):
                        h = hbase + hh
                        tEx = p.op("scalar", lambda e, b=bank, hh=hh, h=h: e.activation(
                            Ereg[:, hh * 128:(hh + 1) * 128],
                            pb[b][:, hh * 128:(hh + 1) * 128], AF.Relu, scale=1.0,
                            bias=dtv["nacn"][:, h:h + 1]))
                    p.wait("scalar", tEx)
                    tEx = p.op("scalar", lambda e: e.activation(
                        Ereg[:, 512:1024], Ereg[:, 0:512], AF.Exp, scale=-1.0))
                    prev_ex = tEx
                    p.wait("vector", tEx)
                    p.wait("vector", tok_smt[hbase // 32][0])
                    if prev_yd is not None and r < 2:
                        p.wait("vector", prev_yd)
                    g = hbase // 8
                    tMT = p.op("vector", lambda e, r=r, g=g: e.tensor_tensor(
                        mT[:, r * 512:(r + 1) * 512].rearrange("p (h l) -> p h l", h=4),
                        Ereg[:, 512:1024].rearrange("p (h l) -> p h l", h=4),
                        SmT[:, g * 128:(g + 1) * 128].unsqueeze(1).broadcast_to([128, 4, 128]),
                        ALU.mult))
                    tok_mT[r] = tMT
                if True:
                    self._prev_mT = tok_mT

                if self.p2_lvl < 6:
                    p.fence()
                    prev_yd = prev_chunk_tensor = tok_mT[15]
                    continue
                if "p2m" in self.debug:
                    dmT = self.dbg_out("d_mT", (NCHUNK, 128, 8192), BF16)
                    dsm = self.dbg_out("d_sm", (NCHUNK, 128, 1024), BF16)
                    self._dbgsl3 = getattr(self, "_dbgsl3", None) or p.slot("dbgm")
                    self._dbgsl4 = getattr(self, "_dbgsl4", None) or p.slot("dbgs")
                    p.wait("sync", tok_mT[15])
                    self._dbg_m = self._dbgsl3.load("sync", dmT[ci], mT)
                    self._dbg_s = self._dbgsl4.load("sync", dsm[ci], SmT)
                for t4 in tok_hstm:
                    p.wait("gpsimd", t4)
                p.wait("gpsimd", tSP)
                p.wait("gpsimd", tok_state_b)
                tHD = p.op("gpsimd", lambda e: e.tensor_tensor(
                    hs_dt[:, :].rearrange("p (h d) -> p h d", h=64),
                    hs_tm[:, :].rearrange("p (h d) -> p h d", h=64),
                    dtv["sp"][:, :].unsqueeze(2).broadcast_to([128, 64, 64]),
                    ALU.mult))

                tok_yd = {}
                tok_yev = {}
                for g8 in range(8):
                    bank = 4 + (g8 % 2)
                    if g8 >= 2:
                        p.wait("tensor", tok_yev[g8 - 2])
                    else:
                        p.wait("tensor", tEAL if bank == 5 else tACT2)
                    p.wait("tensor", tHD)
                    tm = None
                    for hh in range(8):
                        h = g8 * 8 + hh
                        if hh % 4 == 0:
                            p.wait("tensor", tok_mT[h // 4])
                        tm = p.op("tensor", lambda e, b=bank, hh=hh, h=h: e.matmul(
                            pb[b][:, hh * 64:(hh + 1) * 64],
                            mT[:, h * 128:(h + 1) * 128],
                            hs_dt[:, h * 64:(h + 1) * 64],
                            start=True, stop=True))
                    tok_yd[g8] = tm
                    p.wait("scalar", tm)
                    if g8 < 2 and prev_tYN is not None:
                        p.wait("scalar", prev_tYN)
                    if g8 < 2 and getattr(self, "_dbg_tok_y", None) is not None:
                        p.wait("scalar", self._dbg_tok_y)
                    tev = p.op("scalar", lambda e, b=bank, g8=g8: e.activation(
                        y[:, g8 * 512:(g8 + 1) * 512], pb[b][:, :], AF.Copy))
                    tok_yev[g8] = tev

                if self.p2_lvl < 7:
                    p.fence()
                    prev_yd = prev_chunk_tensor = tok_yd[7]
                    prev_tYN = tok_yev[7]
                    continue
                tok_yo = {}
                last_yo_mm = None
                for g8 in range(8):
                    bank = g8 % 2
                    if g8 < 2:
                        p.wait("tensor", tok_smt[0][0])
                        p.wait("tensor", tok_smt[1][0])
                        p.wait("tensor", tok_state_b)
                    if g8 >= 2:
                        p.wait("tensor", tok_yo[g8 - 2])
                    tm = None
                    for hh in range(8):
                        h = g8 * 8 + hh
                        g = h // 8
                        tm = p.op("tensor", lambda e, b=bank, hh=hh, h=h, g=g, csl=csl: e.matmul(
                            pb[b][:, hh * 64:(hh + 1) * 64],
                            Cc_fm(g)[:, csl],
                            Sst_b[:, h * 64:(h + 1) * 64],
                            start=True, stop=True))
                    last_yo_mm = tm
                    p.wait("vector", tm)
                    p.wait("vector", tEAC)
                    p.wait("vector", tok_yev[g8])
                    if g8 >= 1:
                        p.wait("vector", tok_yo[g8 - 1])
                    tt1 = p.op("vector", lambda e, b=bank, g8=g8: e.tensor_tensor(
                        yo_tmp[:, :].rearrange("p (h d) -> p h d", h=8),
                        pb[b][:, :].rearrange("p (h d) -> p h d", h=8),
                        dtv["eac"][:, g8 * 8:(g8 + 1) * 8].unsqueeze(2).broadcast_to([128, 8, 64]),
                        ALU.mult))
                    p.wait("vector", tt1)
                    tt2 = p.op("vector", lambda e, g8=g8: e.tensor_add(
                        y[:, g8 * 512:(g8 + 1) * 512], y[:, g8 * 512:(g8 + 1) * 512],
                        yo_tmp[:, :]))
                    tok_yo[g8] = tt2
                    self._prev_yo = tok_yo

                if self.p2_lvl < 8:
                    p.fence()
                    prev_yd = prev_chunk_tensor = tok_yo[7]
                    prev_tYN = tok_yev[7]
                    continue
                p.wait("gpsimd", tok_yd[7])
                p.wait("gpsimd", tHD)
                for t4 in tok_hstm:
                    p.wait("gpsimd", t4)
                tDR = p.op("gpsimd", lambda e: e.tensor_tensor(
                    hs_dt[:, :].rearrange("p (h d) -> p h d", h=64),
                    hs_tm[:, :].rearrange("p (h d) -> p h d", h=64),
                    self.c_d128[:, :].unsqueeze(2).broadcast_to([128, 64, 64]),
                    ALU.mult))
                if prev_tYF is not None:
                    p.wait("sync", prev_tYF)
                tok_sg = sl_sg.load("sync", sgate_c, self.sgate_s[ci])
                p.wait("gpsimd", tok_yo[7])
                p.wait("gpsimd", tDR)
                tY2 = p.op("gpsimd", lambda e: e.tensor_add(y, y, hs_dt))
                p.wait("gpsimd", tok_sg)
                p.wait("gpsimd", tY2)
                tYF = p.op("gpsimd", lambda e: e.tensor_mul(y, y, sgate_c))
                prev_tYF = tYF
                if "p2y" in self.debug:
                    dy = self.dbg_out("d_y", (NCHUNK, 128, INTER))
                    p.wait("sync", tYF)
                    self._dbgsl = getattr(self, "_dbgsl", None) or p.slot("dbgy")
                    self._dbg_tok_y = self._dbgsl.load("sync", dy[ci], y)
                p.wait("scalar", tYF)
                p.wait("scalar", tok_yd[7])
                if getattr(self, "_dbg_m", None) is not None:
                    p.wait("scalar", self._dbg_m)
                tSQ = p.op("scalar", lambda e: e.activation(yn_b, y, AF.Square,
                                                            accum_out=self.bufS[:, 10:11]))
                p.wait("scalar", tSQ)
                tMS = p.op("scalar", lambda e: e.activation(self.bufS[:, 11:12],
                                                            self.bufS[:, 10:11],
                                                            AF.Copy, scale=1.0 / INTER))
                p.wait("scalar", tMS)
                tLN = p.op("scalar", lambda e: e.activation(self.bufS[:, 12:13],
                                                            self.bufS[:, 11:12],
                                                            AF.Ln, bias=self.c_eps[:, 2:3]))
                p.wait("scalar", tLN)
                tRS = p.op("scalar", lambda e: e.activation(self.bufS[:, 13:14],
                                                            self.bufS[:, 12:13],
                                                            AF.Exp, scale=-0.5))
                p.wait("vector", tRS)
                p.wait("vector", tSQ)
                tYN = p.op("vector", lambda e: e.scalar_tensor_tensor(
                    yn_b, y, self.bufS[:, 13:14], self.c_normw[:, :], ALU.mult, ALU.mult))
                prev_tYN = tYN
                if "p2y" in self.debug:
                    dyn = self.dbg_out("d_yn", (NCHUNK, 128, INTER), BF16)
                    p.wait("sync", tYN)
                    self._dbgsl2 = getattr(self, "_dbgsl2", None) or p.slot("dbgyn")
                    self._dbg_tok = self._dbgsl2.load("sync", dyn[ci], yn_b)

                for grp4 in range(4):
                    bank = grp4 % 2
                    if ev_tok[bank] is not None:
                        p.wait("tensor", ev_tok[bank])
                    p.wait("tensor", tYN)
                    ttr = None
                    for kk in range(8):
                        s = grp4 * 8 + kk
                        ttr = p.op("tensor", lambda e, o=self.pt[bank][:, kk * 128:(kk + 1) * 128],
                                   src=yn_b[:, s * 128:(s + 1) * 128]:
                                   e.transpose(o, src, self.c_identb[:, :]))
                    eng = "vector" if bank == 0 else "scalar"
                    p.wait(eng, ttr)
                    if yn_wr[grp4 % 2] is not None:
                        p.wait(eng, yn_wr[grp4 % 2])
                    if eng == "vector":
                        tev = p.op("vector", lambda e, g=grp4, b=bank: e.tensor_copy(
                            ynT_st[:, (g % 2) * 1024:((g % 2) + 1) * 1024], self.pt[b][:, 0:1024]))
                    else:
                        tev = p.op("scalar", lambda e, g=grp4, b=bank: e.activation(
                            ynT_st[:, (g % 2) * 1024:((g % 2) + 1) * 1024], self.pt[b][:, 0:1024],
                            AF.Copy))
                    ev_tok[bank] = tev
                    p.wait("sync", tev)
                    yn_wr[grp4 % 2] = sl_ynw[grp4 % 2].load(
                        "sync",
                        self.ynT_s[ci][grp4 * 8:(grp4 + 1) * 8].transpose([1, 0, 2]),
                        ynT_st[:, (grp4 % 2) * 1024:((grp4 % 2) + 1) * 1024]
                        .rearrange("p (s b) -> p s b", s=8))

                if self.p2_lvl < 9:
                    p.fence()
                    prev_yd = prev_chunk_tensor = tYN
                    continue
                p.wait("vector", tY2)
                p.wait("vector", tDST)
                tdD = p.op("vector", lambda e: e.tensor_mul(dtv["dtD"], dtv["sp"], dtv["dst"]))
                p.wait("gpsimd", tdD)
                p.wait("gpsimd", tok_yd[7])
                p.wait("gpsimd", tY2)
                tHS2 = p.op("gpsimd", lambda e: e.tensor_tensor(
                    hs_dt[:, :].rearrange("p (h d) -> p h d", h=64),
                    hs_tm[:, :].rearrange("p (h d) -> p h d", h=64),
                    dtv["dtD"][:, :].unsqueeze(2).broadcast_to([128, 64, 64]),
                    ALU.mult))
                tok_sup = {}
                for g8 in range(8):
                    bank = 2 + (g8 % 2)
                    if g8 >= 2:
                        p.wait("tensor", tok_sup[g8 - 2])
                    else:
                        p.wait("tensor", tok_mT[14 + g8])
                    p.wait("tensor", tHS2)
                    for t4 in tok_btm:
                        p.wait("tensor", t4)
                    tm = None
                    for hh in range(8):
                        h = g8 * 8 + hh
                        tm = p.op("tensor", lambda e, b=bank, hh=hh, h=h, g=g8: e.matmul(
                            pb[b][:, hh * 64:(hh + 1) * 64],
                            B_tm[:, g * 128:(g + 1) * 128],
                            hs_dt[:, h * 64:(h + 1) * 64],
                            start=True, stop=True))
                    p.wait("gpsimd", tEAL)
                    p.wait("gpsimd", t0)
                    if tok_sup.get(g8 - 1) is not None:
                        p.wait("gpsimd", tok_sup[g8 - 1])
                    tP = p.op("gpsimd", lambda e, g8=g8: e.tensor_tensor(
                        su_tmp[:, :].rearrange("p (h d) -> p h d", h=8),
                        Sst[:, g8 * 512:(g8 + 1) * 512].rearrange("p (h d) -> p h d", h=8),
                        dtv["eal"][:, g8 * 8:(g8 + 1) * 8].unsqueeze(2).broadcast_to([128, 8, 64]),
                        ALU.mult))
                    p.wait("vector", tP)
                    p.wait("vector", tm)
                    tS = p.op("vector", lambda e, b=bank, g8=g8: e.tensor_add(
                        Sst[:, g8 * 512:(g8 + 1) * 512], su_tmp[:, :], pb[b][:, :]))
                    tok_sup[g8] = tS
                p.wait("scalar", tok_sup[6])
                p.wait("scalar", tok_sup[7])
                p.wait("scalar", last_yo_mm)
                tok_state_b = p.op("scalar", lambda e: e.activation(Sst_b, Sst, AF.Copy))
                prev_yd = tok_sup[7]
                prev_chunk_tensor = tok_sup[7]

    def p3_ffn(self):
        p = self.p
        A = self.bufA
        C_ = self.bufC
        D_ = self.bufD
        E_ = self.bufE
        pb = self.pb
        h_q = lambda es: A[:, es * 512:(es + 1) * 512]
        wo_sl = lambda i: A[:, 4096 + i * 2048: 4096 + (i + 1) * 2048]
        ynT_q = lambda t4: C_[:, t4 * 4096:(t4 + 1) * 4096]
        wup_sl = lambda i: C_[:, i * 2048:(i + 1) * 2048]
        wg_sl = lambda i: C_[:, 4096 + i * 2048: 4096 + (i + 1) * 2048]
        wd_sl = lambda i: C_[:, 8192 + i * 2048: 8192 + (i + 1) * 2048]
        xln2T = lambda k: E_[:, k * 512:(k + 1) * 512]
        x1t = D_[:, 0:2048]
        x1r = [D_[:, 0:2048], D_[:, 2048:4096]]
        sq = D_[:, 4096:6144]
        st = self.bufS
        xl2 = self.bufB[:, 0:1024].bitcast(BF16)
        sup = self.bufB[:, 1024:1280].bitcast(BF16)
        outst = [D_[:, 6144 + i * 512: 6144 + (i + 1) * 512] for i in range(2)]

        sl_yn = [p.slot(f"p3yn{i}") for i in range(4)]
        sl_wo = [p.slot("p3wo0"), p.slot("p3wo1")]
        sl_x = p.slot("p3x")
        sl_x1w = p.slot("p3x1w")
        sl_x1r = [p.slot("p3xr0"), p.slot("p3xr1")]
        sl_wu = [p.slot("p3wu0"), p.slot("p3wu1")]
        sl_wg = [p.slot("p3wg0"), p.slot("p3wg1")]
        sl_wd = [p.slot(f"p3wd{i}") for i in range(4)]
        sl_out = [p.slot("p3o0"), p.slot("p3o1")]
        self._wd_busy = [None] * 4
        self._dn_prev_evac = None

        for q in range(NQ):
            p.fence()
            tok_yn = {}
            for t4 in range(4):
                tok_yn[t4] = sl_yn[t4].load(
                    "sync", ynT_q(t4).rearrange("p (s b) -> p s b", s=32),
                    self.ynT_s[q * 4 + t4].transpose([1, 0, 2]))
            ev_tok = [None, None]
            prev_evac = None
            for pr in range(2):
                wo_tok = [None, None]
                last_mm = None
                if prev_evac is not None:
                    p.wait("tensor", prev_evac)
                for b2 in range(2):
                    if ev_tok[b2] is not None:
                        p.wait("tensor", ev_tok[b2])
                for isl in range(32):
                    slot = isl % 2
                    eng_w = "sync" if slot == 0 else "scalar"
                    if isl >= 2:
                        p.wait(eng_w, last_mm)
                    wo_tok[slot] = sl_wo[slot].load(
                        eng_w, wo_sl(slot), self.wo_b[isl * 128:(isl + 1) * 128, :])
                    p.wait("tensor", wo_tok[slot])
                    for ttl in range(2):
                        t4 = pr * 2 + ttl
                        p.wait("tensor", tok_yn[t4])
                        for dp in range(4):
                            bank = ttl * 4 + dp
                            last_mm = p.op("tensor", lambda e, b=bank, isl=isl, t4=t4,
                                           slot=slot, dp=dp: e.matmul(
                                pb[b][:, :],
                                ynT_q(t4)[:, isl * 128:(isl + 1) * 128],
                                wo_sl(slot)[:, dp * 512:(dp + 1) * 512],
                                start=(isl == 0), stop=(isl == 31)))
                tadds = [None, None]
                for ttl in range(2):
                    t4 = pr * 2 + ttl
                    tt_glob = q * 4 + t4
                    tx = sl_x1r[ttl].load("sync", x1r[ttl],
                                          self.x_in[tt_glob * 128:(tt_glob + 1) * 128, :])
                    p.wait("vector", tx)
                    p.wait("vector", last_mm)
                    tadd = None
                    for dp in range(4):
                        bank = ttl * 4 + dp
                        tadd = p.op("vector", lambda e, b=bank, dp=dp, ttl=ttl: e.tensor_add(
                            x1r[ttl][:, dp * 512:(dp + 1) * 512],
                            x1r[ttl][:, dp * 512:(dp + 1) * 512], pb[b][:, :]))
                    tadds[ttl] = tadd
                prev_evac = tadds[1]
                for ttl in range(2):
                    t4 = pr * 2 + ttl
                    tt_glob = q * 4 + t4
                    x1v = x1r[ttl]
                    p.wait("vector", tadds[ttl])
                    t_s1 = p.op("vector", lambda e, x1v=x1v: e.reduce_sum(st[:, 0:1], x1v, axis=AX.X))
                    p.wait("scalar", tadds[ttl])
                    t_s2 = p.op("scalar", lambda e, x1v=x1v: e.activation(sq, x1v, AF.Square,
                                                                 accum_out=st[:, 1:2]))
                    p.wait("scalar", t_s1)
                    t_m = p.op("scalar", lambda e: e.activation(st[:, 2:3], st[:, 0:1],
                                                                AF.Copy, scale=1.0 / 2048.0))
                    p.wait("scalar", t_m)
                    t_m2 = p.op("scalar", lambda e: e.activation(st[:, 3:4], st[:, 2:3], AF.Square))
                    p.wait("vector", t_s2)
                    p.wait("vector", t_m2)
                    t_v = p.op("vector", lambda e: e.tensor_scalar(
                        st[:, 4:5], st[:, 1:2], 1.0 / 2048.0, None, ALU.mult))
                    p.wait("vector", t_v)
                    t_v = p.op("vector", lambda e: e.tensor_sub(st[:, 4:5], st[:, 4:5], st[:, 3:4]))
                    p.wait("scalar", t_v)
                    t_ln = p.op("scalar", lambda e: e.activation(st[:, 5:6], st[:, 4:5], AF.Ln,
                                                                 bias=self.c_eps[:, 1:2]))
                    p.wait("scalar", t_ln)
                    t_rs = p.op("scalar", lambda e: e.activation(st[:, 6:7], st[:, 5:6], AF.Exp,
                                                                 scale=-0.5))
                    p.wait("vector", t_rs)
                    t_x2 = p.op("vector", lambda e, x1v=x1v: e.tensor_scalar(
                        xl2[:, 0:2048], x1v, st[:, 2:3], st[:, 6:7], ALU.subtract, ALU.mult))
                    p.wait("sync", t_x2)
                    p.wait("sync", t_s2)
                    tw = sl_x1w.load("sync", self.x1_s[tt_glob * 128:(tt_glob + 1) * 128, :], x1v)
                    for kb in range(4):
                        bank = kb % 2
                        if ev_tok[bank] is not None:
                            p.wait("tensor", ev_tok[bank])
                        p.wait("tensor", t_x2)
                        ttr = None
                        for kk in range(4):
                            k = kb * 4 + kk
                            ttr = p.op("tensor", lambda e, b=bank, kk=kk, k=k: e.transpose(
                                self.pt[b][:, kk * 128:(kk + 1) * 128],
                                xl2[:, k * 128:(k + 1) * 128], self.c_identb[:, :]))
                        eng = "vector" if bank == 0 else "scalar"
                        p.wait(eng, ttr)
                        dst = E_[:, :].rearrange("p (k c) -> p k c", c=512)[
                            :, kb * 4:(kb + 1) * 4, t4 * 128:(t4 + 1) * 128]
                        if eng == "vector":
                            tev = p.op("vector", lambda e, d=dst, b=bank: e.tensor_copy(
                                d, self.pt[b][:, 0:512].rearrange("p (k c) -> p k c", c=128)))
                        else:
                            tev = p.op("scalar", lambda e, d=dst, b=bank: e.activation(
                                d, self.pt[b][:, 0:512].rearrange("p (k c) -> p k c", c=128),
                                AF.Copy))
                        ev_tok[bank] = tev
                    p.wait("sync", tw)
            p.fence()
            h_tok = {}
            wu_tok = [None, None]
            wg_tok = [None, None]
            for es in range(64):
                slot = es % 2
                if es >= 2:
                    p.wait("sync", h_tok[es - 2])
                    p.wait("scalar", h_tok[es - 2])
                wu_tok[slot] = sl_wu[slot].load(
                    "sync", wup_sl(slot).rearrange("p (k c) -> p k c", k=16),
                    self.q_upT[:, es * 128:(es + 1) * 128].rearrange("(k p) c -> p k c", p=128))
                wg_tok[slot] = sl_wg[slot].load(
                    "scalar", wg_sl(slot).rearrange("p (k c) -> p k c", k=16),
                    self.q_gT[:, es * 128:(es + 1) * 128].rearrange("(k p) c -> p k c", p=128))
                bku, bkg = (0, 1) if es % 2 == 0 else (2, 3)
                p.wait("tensor", wu_tok[slot])
                p.wait("tensor", wg_tok[slot])
                if es >= 2:
                    p.wait("tensor", h_tok[es - 2])
                tmu = tmg = None
                for k in range(KS):
                    tmu = p.op("tensor", lambda e, b=bku, k=k, slot=slot: e.matmul(
                        pb[b][:, :],
                        wup_sl(slot).rearrange("p (k c) -> p k c", k=16)[:, k, :],
                        xln2T(k), start=(k == 0), stop=(k == KS - 1)))
                for k in range(KS):
                    tmg = p.op("tensor", lambda e, b=bkg, k=k, slot=slot: e.matmul(
                        pb[b][:, :],
                        wg_sl(slot).rearrange("p (k c) -> p k c", k=16)[:, k, :],
                        xln2T(k), start=(k == 0), stop=(k == KS - 1)))
                p.wait("scalar", tmu)
                p.wait("scalar", self.t_scal)
                if es >= 1:
                    p.wait("scalar", h_tok[es - 1])
                tsu = p.op("scalar", lambda e, b=bku: e.activation(
                    sup[:, 0:512], pb[b][:, :], AF.Silu, scale=self.c_scal[:, 3:4]))
                p.wait("vector", tsu)
                p.wait("vector", tmg)
                th = p.op("vector", lambda e, b=bkg, es=es: e.tensor_mul(
                    h_q(es), sup[:, 0:512], pb[b][:, :]))
                h_tok[es] = th
            p.fence(("tensor", "gpsimd", "vector", "scalar", "sync"))
            for pr in range(2):
                if self._dn_prev_evac is not None:
                    p.wait("tensor", self._dn_prev_evac)
                x1_tok = [None, None]
                for ttl in range(2):
                    t4 = pr * 2 + ttl
                    tt_glob = q * 4 + t4
                    x1_tok[ttl] = sl_x1r[ttl].load(
                        "sync", x1r[ttl], self.x1_s[tt_glob * 128:(tt_glob + 1) * 128, :])
                wd_tok = [None] * 4
                last_mm = None
                for es in range(64):
                    slot = es % 4
                    eng_w = "sync" if slot % 2 == 0 else "scalar"
                    guard = self._wd_busy[slot]
                    if guard is not None:
                        p.wait(eng_w, guard)
                    wd_tok[slot] = sl_wd[slot].load(
                        eng_w, wd_sl(slot), self.q_dT[es * 128:(es + 1) * 128, :])
                    p.wait("tensor", wd_tok[slot])
                    for ttl in range(2):
                        t4 = pr * 2 + ttl
                        for dp in range(4):
                            bank = ttl * 4 + dp
                            last_mm = p.op("tensor", lambda e, b=bank, es=es, t4=t4,
                                           slot=slot, dp=dp: e.matmul(
                                pb[b][:, :],
                                h_q(es)[:, t4 * 128:(t4 + 1) * 128],
                                wd_sl(slot)[:, dp * 512:(dp + 1) * 512],
                                start=(es == 0), stop=(es == 63)))
                    self._wd_busy[slot] = last_mm
                for ttl in range(2):
                    t4 = pr * 2 + ttl
                    tt_glob = q * 4 + t4
                    for dp in range(4):
                        bank = ttl * 4 + dp
                        sslot = dp % 2
                        p.wait("vector", last_mm)
                        p.wait("vector", x1_tok[ttl])
                        if sl_out[sslot].tok() is not None:
                            p.wait("vector", sl_out[sslot].tok())
                        tev = p.op("vector", lambda e, b=bank, ttl=ttl, dp=dp, ss=sslot:
                                   e.scalar_tensor_tensor(
                            outst[ss], pb[b][:, :], self.c_scal[:, 6:7],
                            x1r[ttl][:, dp * 512:(dp + 1) * 512], ALU.mult, ALU.add))
                        p.wait("sync", tev)
                        sl_out[sslot].load(
                            "sync", self.out[tt_glob * 128:(tt_glob + 1) * 128,
                                             dp * 512:(dp + 1) * 512], outst[sslot])
                        self._dn_prev_evac = tev


def prep_base(L0_attn_in_proj_w, L0_attn_conv_w, L0_attn_conv_b, L0_attn_dt_bias,
              L0_attn_A_log, L0_attn_D, L0_attn_norm_w, L0_attn_out_proj_w,
              L0_ffn_W_up, L0_ffn_W_gate, L0_ffn_W_down):
    rep = lambda v: np.ascontiguousarray(
        np.repeat(np.asarray(v, np.float32)[None, :], 128, 0))
    convw = np.ascontiguousarray(
        np.asarray(L0_attn_conv_w, np.float32)[:, 0, :].reshape(4, NSLAB_HBC, 128)
        .transpose(2, 1, 0).reshape(128, NSLAB_HBC * 4))
    convb = np.ascontiguousarray(
        np.asarray(L0_attn_conv_b, np.float32).reshape(NSLAB_HBC, 128).T)
    base = {
        "wp": np.asarray(L0_attn_in_proj_w, np.float32),
        "convw": convw, "convb": convb,
        "dtb128": rep(L0_attn_dt_bias), "alog128": rep(L0_attn_A_log),
        "d128": rep(L0_attn_D), "normw128": rep(L0_attn_norm_w),
        "wo": np.asarray(L0_attn_out_proj_w, np.float32),
        "wu": np.asarray(L0_ffn_W_up, np.float32),
        "wg": np.asarray(L0_ffn_W_gate, np.float32),
        "wd": np.asarray(L0_ffn_W_down, np.float32),
    }
    base["eps128"] = np.ascontiguousarray(np.repeat(
        np.array([[EPS_Q, EPS_LN, EPS_RMS, 1.0]], np.float32), 128, 0))
    base.update(make_consts())
    return base


def kernel(L0_attn_input_states, L0_attn_in_proj_w, L0_attn_conv_w,
           L0_attn_conv_b, L0_attn_dt_bias, L0_attn_A_log, L0_attn_D,
           L0_attn_norm_w, L0_attn_out_proj_w, L0_ffn_W_up, L0_ffn_W_gate,
           L0_ffn_W_down):
    kb = KB()
    nc = kb.build()
    base = prep_base(L0_attn_in_proj_w, L0_attn_conv_w, L0_attn_conv_b,
                     L0_attn_dt_bias, L0_attn_A_log, L0_attn_D, L0_attn_norm_w,
                     L0_attn_out_proj_w, L0_ffn_W_up, L0_ffn_W_gate, L0_ffn_W_down)
    x = np.asarray(L0_attn_input_states, np.float32)
    in_maps = [dict(base, x=np.ascontiguousarray(x[c])) for c in range(8)]
    res = run_bass_kernel_spmd(nc, in_maps, core_ids=list(range(8)))
    return np.stack([r["out"] for r in res.results]).astype(np.float32)
